# version 18
# baseline (speedup 1.0000x reference)
import numpy as np
import ml_dtypes
import concourse.bass as bass
import concourse.mybir as mybir
from concourse.bass_utils import run_bass_kernel_spmd

F32 = mybir.dt.float32
BF16 = mybir.dt.bfloat16
FP16 = mybir.dt.float16
U16 = mybir.dt.uint16
U8 = mybir.dt.uint8
AF = mybir.ActivationFunctionType
ALU = mybir.AluOpType

D = 2048
T = 2048
CC = 16
TW = 512
NT = T // TW
LN_EPS = 1e-5
BF = ml_dtypes.bfloat16


def pack_wT(W):
    dout, din = W.shape
    n_oc, n_cc = dout // 128, din // 128
    a = np.asarray(W, np.float32).reshape(n_oc, 128, n_cc, 128)
    return np.ascontiguousarray(a.transpose(0, 3, 2, 1)).reshape(n_oc, 128, n_cc * 128)


def pack_vecs(vecs):
    cols = [np.asarray(v, np.float32).reshape(CC, 128).T for v in vecs]
    return np.ascontiguousarray(np.stack(cols, axis=1))


def pack_x_chunks(xb):
    return np.ascontiguousarray(np.asarray(xb, np.float32).T.reshape(CC, 128, T))


def unpack_x_chunks(ch):
    return np.ascontiguousarray(ch.reshape(D, T).T)



class Sem:
    def __init__(self, nc, name):
        self.h = nc.alloc_semaphore(name)
        self.n = 0

    def inc(self, ins, k=1):
        ins.then_inc(self.h, k)
        self.n += k
        return ins

    def wait(self, eng, n=None):
        eng.wait_ge(self.h, self.n if n is None else n)


class Ctx:
    def __init__(self):
        self.nc = bass.Bass("TRN2", target_bir_lowering=False, debug=False)
        self.dram = {}
        self.sems = {}

    def sem(self, name):
        if name not in self.sems:
            self.sems[name] = Sem(self.nc, name)
        return self.sems[name]

    def din(self, name, shape, dtype):
        self.dram[name] = self.nc.dram_tensor(name, list(shape), dtype, kind="ExternalInput").ap()
        return self.dram[name]

    def dout(self, name, shape, dtype):
        self.dram[name] = self.nc.dram_tensor(name, list(shape), dtype, kind="ExternalOutput").ap()
        return self.dram[name]

    def dscratch(self, name, shape, dtype):
        self.dram[name] = self.nc.dram_tensor(name, list(shape), dtype, kind="Internal").ap()
        return self.dram[name]


class Bufs:
    pass


def alloc_shared(ctx):
    nc = ctx.nc
    b = Bufs()
    b.slotA = nc.alloc_sbuf_tensor("slotA", [128, CC, T + 4], BF16)
    b.slotB = nc.alloc_sbuf_tensor("slotB", [128, CC, T + 4], BF16)
    b.wring = nc.alloc_sbuf_tensor("wring", [128, 3, CC * 128], BF16)
    b.sring = nc.alloc_sbuf_tensor("sring", [128, 4, T], BF16)
    b.tmp1 = nc.alloc_sbuf_tensor("tmp1", [128, 3, T], BF16)
    b.tmp2 = nc.alloc_sbuf_tensor("tmp2", [128, 2, T], BF16)
    b.aB = nc.alloc_sbuf_tensor("aB", [128, T], F32)
    b.bB = nc.alloc_sbuf_tensor("bB", [128, T], F32)
    b.ones_bf = nc.alloc_sbuf_tensor("ones_bf", [128, 1], BF16)
    b.zb = nc.alloc_sbuf_tensor("zb128", [128, 1], F32)
    b.psA = nc.alloc_psum_tensor("psA", [128, NT, TW], F32)
    b.psB = nc.alloc_psum_tensor("psB", [128, NT, TW], F32)
    return b


def psflat(ps):
    return ps.ap().rearrange("p a b -> p (a b)")


def zact(s, b, out, in_, af, **kw):
    kw.setdefault("bias", b.zb.ap()[0:out.shape[0], 0:1])
    return s.activation(out, in_, af, **kw)


def zcopy(s, b, out, in_):
    return s.activation(out, in_, AF.Identity, bias=b.zb.ap()[0:out.shape[0], 0:1])



def emit_l0_attn(ctx, blk, b, vecs_sb, x_in_name, x_out_name):
    V = {"ln1_g": 0, "ln1_b": 1, "ln2_g": 2, "ln2_b": 3, "u18": 4,
         "mu_r": 5, "mu_k": 6, "mu_v": 7, "cmu_r": 8, "cmu_k": 9}

    x_in = ctx.dram[x_in_name]
    x_out = ctx.dram[x_out_name]
    sp_n0 = ctx.dscratch("sp_n0", [CC, 128, T], BF16)
    sp_z = ctx.dscratch("sp_z", [CC, 128, T], BF16)
    sp_wkv = ctx.dscratch("sp_wkv", [CC, 128, T], BF16)
    sp_rwkv = ctx.dscratch("sp_rwkv", [CC, 128, T], BF16)
    sp_xmid = ctx.dscratch("sp_xmid", [CC, 128, T], BF16)
    sp_k2 = ctx.dscratch("sp_k2", [CC, 128, T], BF16)
    sp_row = ctx.dscratch("sp_row_l0a", [8, T], F32)

    s_mm = ctx.sem("s_mm")
    s_v = ctx.sem("s_v")
    s_s = ctx.sem("s_s")
    s_p = ctx.sem("s_p")
    s_evV = ctx.sem("s_evV")
    s_evS = ctx.sem("s_evS")
    s_srg = [ctx.sem(f"s_srg{j}") for j in range(4)]
    s_srs = [ctx.sem(f"s_srs{j}") for j in range(4)]
    s_wr = [ctx.sem(f"s_wr{j}") for j in range(3)]
    s_t1 = [ctx.sem(f"s_t1{j}") for j in range(3)]
    s_bc = ctx.sem("s_bc")
    s_slot = ctx.sem("s_slot")
    s_rsp = ctx.sem("s_rsp")
    s_cv = ctx.sem("s_cv")
    ctx.dma_sems = [s_bc, s_slot, s_rsp] + s_srg + s_srs + s_wr + s_t1

    def stat_pass(src_ap, queue, gate=None):
        mm0 = s_mm.n
        sq0 = s_s.n
        ld_vals = []

        s_q = s_srg if queue == "g" else s_srs

        v_need, s_need = s_v.n, s_s.n

        def loads(q):
            if gate is not None:
                for sem, val in gate:
                    q.wait_ge(sem.h, val)
            q.wait_ge(s_v.h, v_need)
            q.wait_ge(s_s.h, s_need)
            q.wait_ge(s_mm.h, mm0)
            for ci in range(CC):
                if ci >= 4:
                    q.wait_ge(s_mm.h, mm0 + ci - 3)
                ins = q.dma_start(out=b.sring.ap()[:, ci % 4], in_=src_ap[ci])
                s_q[ci % 4].inc(ins, 16)
                ld_vals.append((s_q[ci % 4], s_q[ci % 4].n))
        (blk.gpsimd if queue == "g" else blk.sync)(loads)

        @blk.scalar
        def _(s):
            for ci in range(CC):
                s.wait_ge(ld_vals[ci][0].h, ld_vals[ci][1])
                if ci >= 2:
                    s.wait_ge(s_mm.h, mm0 + ci - 1)
                zact(s, b, b.tmp2.ap()[:, ci % 2], b.sring.ap()[:, ci % 4], AF.Square).then_inc(s_s.h, 1)
                s_s.n += 1

        evV0, evS0 = s_evV.n, s_evS.n

        @blk.tensor
        def _(t):
            t.wait_ge(s_evV.h, evV0)
            t.wait_ge(s_evS.h, evS0)
            for ci in range(CC):
                t.wait_ge(s_s.h, sq0 + ci + 1)
                for tt in range(NT):
                    t.matmul(b.psA.ap()[0:1, tt], b.ones_bf[:],
                             b.sring.ap()[:, ci % 4, tt * TW:(tt + 1) * TW],
                             start=(ci == 0), stop=(ci == CC - 1))
                    ins = t.matmul(b.psB.ap()[0:1, tt], b.ones_bf[:],
                                   b.tmp2.ap()[:, ci % 2, tt * TW:(tt + 1) * TW],
                                   start=(ci == 0), stop=(ci == CC - 1))
                ins.then_inc(s_mm.h, 1)
        s_mm.n += CC

    def rows_from_stats():
        r0 = b.aB.ap()[0:1, :]
        r1 = b.bB.ap()[0:1, :]

        @blk.vector
        def _(v):
            s_mm.wait(v)
            s_bc.wait(v)
            for tt in range(NT):
                ins = v.tensor_scalar(r1[:, tt * TW:(tt + 1) * TW], b.psA.ap()[0:1, tt], 1.0 / D, None, ALU.mult)
            s_v.inc(ins, 1)

        @blk.scalar
        def _(s):
            s_v.wait(s)
            ins = zact(s, b, r0, r1, AF.Square)
            s_s.inc(ins, 1)

        @blk.vector
        def _(v):
            s_s.wait(v)
            for tt in range(NT):
                ins = v.tensor_scalar(r1[:, tt * TW:(tt + 1) * TW], b.psB.ap()[0:1, tt], 1.0 / D, None, ALU.mult)
            s_cv.inc(ins, 1)
            s_cv.wait(v)
            ins = v.tensor_tensor(r1, r1, r0, ALU.subtract)
            s_cv.inc(ins, 1)
            s_cv.wait(v)
            ins = v.tensor_scalar(r1, r1, 1.0, LN_EPS, ALU.mult, ALU.add)
            s_cv.inc(ins, 1)
            s_cv.wait(v)
            ins = v.reciprocal(r0, r1)
            s_v.inc(ins, 1)

        @blk.scalar
        def _(s):
            s_v.wait(s)
            ins = zact(s, b, r0, r0, AF.Sqrt)
            s_s.inc(ins, 1)

        @blk.vector
        def _(v):
            s_s.wait(v)
            for tt in range(NT):
                ins = v.tensor_scalar(r1[:, tt * TW:(tt + 1) * TW], b.psA.ap()[0:1, tt], -1.0 / D, None, ALU.mult)
            s_cv.inc(ins, 1)
            s_cv.wait(v)
            ins = v.tensor_tensor(r1, r1, r0, ALU.mult)
            s_evV.inc(ins, 1)

    def bcast_rows(slot_a, slot_b):
        row_done = s_evV.n

        @blk.scalar
        def _(s):
            s.wait_ge(s_evV.h, row_done)
            s.dma_start(out=sp_row[slot_a:slot_a + 1], in_=b.aB.ap()[0:1, :]).then_inc(s_rsp.h, 16)
            s.dma_start(out=sp_row[slot_b:slot_b + 1], in_=b.bB.ap()[0:1, :]).then_inc(s_rsp.h, 16)
        s_rsp.n += 32

        mm_need_bc = s_mm.n

        @blk.sync
        def _(sy):
            s_rsp.wait(sy)
            sy.wait_ge(s_mm.h, mm_need_bc)
            sy.dma_start(out=b.aB[:], in_=sp_row[slot_a].unsqueeze(0).broadcast_to([128, T])).then_inc(s_bc.h, 16)
            sy.dma_start(out=b.bB[:], in_=sp_row[slot_b].unsqueeze(0).broadcast_to([128, T])).then_inc(s_bc.h, 16)
        s_bc.n += 32

    def ln_apply(src_dram, dst_slot, g_idx, b_idx, t1_gate):
        bc_need = s_bc.n
        mm_need = s_mm.n
        bld0 = s_v.n
        p0 = s_p.n
        ld_vals = []

        ss_need = s_s.n

        @blk.sync
        def _(sy):
            for sem, val in t1_gate:
                sy.wait_ge(sem.h, val)
            sy.wait_ge(s_v.h, bld0)
            sy.wait_ge(s_s.h, ss_need)
            sy.wait_ge(s_mm.h, mm_need)
            for ci in range(CC):
                if ci >= 4:
                    sy.wait_ge(s_v.h, bld0 + 2 * (ci - 4) + 1)
                ins = sy.dma_start(out=b.sring.ap()[:, ci % 4], in_=src_dram[ci])
                s_srs[ci % 4].inc(ins, 16)
                ld_vals.append(s_srs[ci % 4].n)

        ss0 = s_s.n

        @blk.vector
        def _(v):
            v.wait_ge(s_bc.h, bc_need)
            v.wait_ge(s_mm.h, mm_need)
            v.wait_ge(s_s.h, ss0)
            for ci in range(CC):
                v.wait_ge(s_srs[ci % 4].h, ld_vals[ci])
                v.wait_ge(s_v.h, s_v.n)
                ins = v.tensor_tensor(b.tmp2.ap()[:, 0], b.sring.ap()[:, ci % 4], b.aB[:], ALU.mult)
                s_cv.inc(ins, 1)
                s_cv.wait(v)
                ins1 = v.tensor_tensor(b.tmp2.ap()[:, 0], b.tmp2.ap()[:, 0], b.bB[:], ALU.add)
                s_v.inc(ins1, 1)
                v.wait_ge(s_v.h, s_v.n)
                ins = v.tensor_scalar(dst_slot.ap()[:, ci, 1:1 + T], b.tmp2.ap()[:, 0],
                                      vecs_sb.ap()[:, g_idx, ci:ci + 1], vecs_sb.ap()[:, b_idx, ci:ci + 1],
                                      ALU.mult, ALU.add)
                s_v.inc(ins, 1)

        @blk.gpsimd
        def _(g):
            g.wait_ge(s_mm.h, mm_need)
            for ci in range(CC):
                ins = g.memset(dst_slot.ap()[:, ci, 0:1], 0.0)
            s_p.inc(ins, 1)
        return s_v.n, s_p.n

    def build_mix(mu_idx, src_slot, dst_slot, extra_waits=()):
        m0 = s_mm.n

        s0 = s_s.n
        p0 = s_p.n

        @blk.vector
        def _(v):
            v.wait_ge(s_mm.h, m0)
            v.wait_ge(s_s.h, s0)
            v.wait_ge(s_p.h, p0)
            for sem, val in extra_waits:
                v.wait_ge(sem.h, val)
            for ci in range(CC):
                h = src_slot.ap()[:, ci, 1:1 + T]
                hp = src_slot.ap()[:, ci, 0:T]
                d = b.tmp2.ap()[:, 0]
                v.wait_ge(s_v.h, s_v.n)
                ins = v.tensor_tensor(d, h, hp, ALU.subtract)
                s_cv.inc(ins, 1)
                s_cv.wait(v)
                ins = v.scalar_tensor_tensor(dst_slot.ap()[:, ci, 1:1 + T], d,
                                             vecs_sb.ap()[:, mu_idx, ci:ci + 1], hp, ALU.mult, ALU.add)
                s_v.inc(ins, 1)
        return s_v.n

    def proj_pass(w_ap, rhs_slot, ev_sem, evac, gate=None, n_oc=CC):
        ev0 = ev_sem.n
        mm0 = s_mm.n
        lw_vals = []

        @blk.gpsimd
        def _(g):
            g.wait_ge(s_mm.h, mm0)
            for oc in range(n_oc):
                if oc >= 3:
                    g.wait_ge(ev_sem.h, ev0 + oc - 2)
                ins = g.dma_start(out=b.wring.ap()[:, oc % 3], in_=w_ap[oc])
                s_wr[oc % 3].inc(ins, 16)
                lw_vals.append(s_wr[oc % 3].n)

        evV_need, evS_need = s_evV.n, s_evS.n

        @blk.tensor
        def _(t):
            if gate is not None:
                for sem, val in gate:
                    t.wait_ge(sem.h, val)
            t.wait_ge(s_evV.h, evV_need)
            t.wait_ge(s_evS.h, evS_need)
            for oc in range(n_oc):
                ps = b.psA if oc % 2 == 0 else b.psB
                t.wait_ge(s_wr[oc % 3].h, lw_vals[oc])
                if oc >= 2:
                    t.wait_ge(ev_sem.h, ev0 + oc - 1)
                wv = b.wring.ap()[:, oc % 3].rearrange("p (cc o) -> p cc o", o=128)
                for cc in range(CC):
                    for tt in range(NT):
                        ins = t.matmul(ps.ap()[:, tt], wv[:, cc],
                                       rhs_slot.ap()[:, cc, 1 + tt * TW:1 + (tt + 1) * TW],
                                       start=(cc == 0), stop=(cc == CC - 1))
                ins.then_inc(s_mm.h, 1)
        s_mm.n += n_oc

        for oc in range(n_oc):
            evac(oc, b.psA if oc % 2 == 0 else b.psB, mm0 + oc + 1)

    def spill(oc, dst_dram, ready_sem, ready_val):
        @blk.scalar
        def _(s):
            s.wait_ge(ready_sem.h, ready_val)
            ins = s.dma_start(out=dst_dram[oc], in_=b.tmp1.ap()[:, oc % 3])
            s_t1[oc % 3].inc(ins, 16)

    def t1_free_wait(eng, oc):
        if oc >= 3:
            eng.wait_ge(s_t1[oc % 3].h, s_t1[oc % 3].n + 16)

    def t1_vals():
        return [s.n for s in s_t1]

    stat_pass(x_in, "g")
    rows_from_stats()
    bcast_rows(0, 1)

    bcB = s_bc.n
    mmB0 = s_mm.n
    sqB0 = s_s.n
    bldB0 = s_v.n
    ldB_vals = []
    t1B = t1_vals()

    vB_need, sB_need = s_v.n, s_s.n

    @blk.gpsimd
    def _(g):
        g.wait_ge(s_bc.h, bcB)
        g.wait_ge(s_v.h, vB_need)
        g.wait_ge(s_s.h, sB_need)
        g.wait_ge(s_mm.h, mmB0)
        for ci in range(CC):
            if ci >= 4:
                g.wait_ge(s_v.h, bldB0 + ci - 3)
            ins = g.dma_start(out=b.sring.ap()[:, ci % 4], in_=x_in[ci])
            s_srg[ci % 4].inc(ins, 16)
            ldB_vals.append(s_srg[ci % 4].n)

    @blk.vector
    def _(v):
        for ci in range(CC):
            v.wait_ge(s_srg[ci % 4].h, ldB_vals[ci])
            if ci >= 3:
                v.wait_ge(s_mm.h, mmB0 + ci - 2)
                v.wait_ge(s_t1[ci % 3].h, t1B[ci % 3] + 16)
                t1B[ci % 3] += 16
                v.wait_ge(s_s.h, sqB0 + ci - 2)
            ins = v.tensor_tensor(b.tmp1.ap()[:, ci % 3], b.sring.ap()[:, ci % 4], b.aB[:], ALU.mult)
            s_cv.inc(ins, 1)
            s_cv.wait(v)
            ins = v.tensor_tensor(b.tmp1.ap()[:, ci % 3], b.tmp1.ap()[:, ci % 3], b.bB[:], ALU.add)
            s_v.inc(ins, 1)

    @blk.scalar
    def _(s):
        for ci in range(CC):
            s.wait_ge(s_v.h, bldB0 + ci + 1)
            if ci >= 2:
                s.wait_ge(s_mm.h, mmB0 + ci - 1)
            zact(s, b, b.tmp2.ap()[:, ci % 2], b.tmp1.ap()[:, ci % 3], AF.Square).then_inc(s_s.h, 1)
            s_s.n += 1
            ins = s.dma_start(out=sp_n0[ci], in_=b.tmp1.ap()[:, ci % 3])
            s_t1[ci % 3].inc(ins, 16)
    spB_done = t1_vals()

    evV0, evS0 = s_evV.n, s_evS.n

    @blk.tensor
    def _(t):
        t.wait_ge(s_evV.h, evV0)
        t.wait_ge(s_evS.h, evS0)
        for ci in range(CC):
            t.wait_ge(s_s.h, sqB0 + ci + 1)
            for tt in range(NT):
                t.matmul(b.psA.ap()[0:1, tt], b.ones_bf[:],
                         b.tmp1.ap()[:, ci % 3, tt * TW:(tt + 1) * TW],
                         start=(ci == 0), stop=(ci == CC - 1))
                ins = t.matmul(b.psB.ap()[0:1, tt], b.ones_bf[:],
                               b.tmp2.ap()[:, ci % 2, tt * TW:(tt + 1) * TW],
                               start=(ci == 0), stop=(ci == CC - 1))
            ins.then_inc(s_mm.h, 1)
    s_mm.n += CC

    rows_from_stats()
    bcast_rows(2, 3)

    ln_apply(sp_n0, b.slotA, V["ln1_g"], V["ln1_b"],
             [(s, v) for s, v in zip(s_t1, spB_done)])

    bldk = build_mix(V["mu_k"], b.slotA, b.slotB)

    def evac_k(oc, ps, mmv):
        t1v = s_t1[oc % 3].n

        @blk.scalar
        def _(s):
            s.wait_ge(s_mm.h, mmv)
            s.wait_ge(s_t1[oc % 3].h, t1v)
            ins = zact(s, b, b.tmp1.ap()[:, oc % 3], psflat(ps), AF.Sigmoid,
                               bias=vecs_sb.ap()[:, V["u18"], oc:oc + 1])
            s_evS.inc(ins, 1)
        spill(oc, sp_z, s_evS, s_evS.n)

    proj_pass(ctx.dram["l0a_wk"], b.slotB, s_evS, evac_k, gate=[(s_v, bldk)])
    spz_done = t1_vals()

    bldv = build_mix(V["mu_v"], b.slotA, b.slotB)
    rcv_vals = []
    evVv0 = s_evV.n

    vE, sE, mE = s_v.n, s_s.n, s_mm.n

    @blk.sync
    def _(sy):
        for sem, val in zip(s_t1, spz_done):
            sy.wait_ge(sem.h, val)
        sy.wait_ge(s_v.h, vE)
        sy.wait_ge(s_s.h, sE)
        sy.wait_ge(s_mm.h, mE)
        for oc in range(CC):
            if oc >= 4:
                sy.wait_ge(s_evV.h, evVv0 + oc - 3)
            ins = sy.dma_start(out=b.sring.ap()[:, oc % 4], in_=sp_z[oc])
            s_srs[oc % 4].inc(ins, 16)
            rcv_vals.append(s_srs[oc % 4].n)

    def evac_v(oc, ps, mmv):
        t1v = s_t1[oc % 3].n

        @blk.vector
        def _(v):
            v.wait_ge(s_mm.h, mmv)
            v.wait_ge(s_srs[oc % 4].h, rcv_vals[oc])
            v.wait_ge(s_t1[oc % 3].h, t1v)
            ins = v.tensor_tensor(b.tmp1.ap()[:, oc % 3], b.sring.ap()[:, oc % 4], psflat(ps), ALU.mult)
            s_evV.inc(ins, 1)
        spill(oc, sp_wkv, s_evV, s_evV.n)

    proj_pass(ctx.dram["l0a_wv"], b.slotB, s_evV, evac_v, gate=[(s_v, bldv)])
    spwkv_done = t1_vals()

    bldr = build_mix(V["mu_r"], b.slotA, b.slotB)
    rcr_vals = []
    vr0 = s_v.n
    evSr0 = s_evS.n

    vE, sE, mE = s_v.n, s_s.n, s_mm.n

    @blk.sync
    def _(sy):
        for sem, val in zip(s_t1, spwkv_done):
            sy.wait_ge(sem.h, val)
        sy.wait_ge(s_v.h, vE)
        sy.wait_ge(s_s.h, sE)
        sy.wait_ge(s_mm.h, mE)
        for oc in range(CC):
            if oc >= 4:
                sy.wait_ge(s_v.h, vr0 + oc - 3)
            ins = sy.dma_start(out=b.sring.ap()[:, oc % 4], in_=sp_wkv[oc])
            s_srs[oc % 4].inc(ins, 16)
            rcr_vals.append(s_srs[oc % 4].n)

    def evac_r(oc, ps, mmv):
        t1v = s_t1[oc % 3].n

        @blk.scalar
        def _(s):
            s.wait_ge(s_mm.h, mmv)
            if oc >= 2:
                s.wait_ge(s_v.h, vr0 + oc - 1)
            ins = zact(s, b, b.tmp2.ap()[:, oc % 2], psflat(ps), AF.Sigmoid)
            s_evS.inc(ins, 1)

        @blk.vector
        def _(v):
            v.wait_ge(s_evS.h, evSr0 + oc + 1)
            v.wait_ge(s_srs[oc % 4].h, rcr_vals[oc])
            v.wait_ge(s_t1[oc % 3].h, t1v)
            ins = v.tensor_tensor(b.tmp1.ap()[:, oc % 3], b.tmp2.ap()[:, oc % 2],
                                  b.sring.ap()[:, oc % 4], ALU.mult)
            s_v.inc(ins, 1)
        spill(oc, sp_rwkv, s_v, s_v.n)

    proj_pass(ctx.dram["l0a_wr"], b.slotB, s_evS, evac_r, gate=[(s_v, bldr)])
    sprw_done = t1_vals()
    mm_wr_done = s_mm.n
    bld_wr = s_v.n

    @blk.sync
    def _(sy):
        sy.wait_ge(s_v.h, bld_wr)
        for sem, val in zip(s_t1, sprw_done):
            sy.wait_ge(sem.h, val)
        for cc in range(CC):
            sy.dma_start(out=b.slotB.ap()[:, cc, 1:1 + T], in_=sp_rwkv[cc]).then_inc(s_slot.h, 16)
    s_slot.n += 16 * CC
    rwkv_ready = s_slot.n

    rco_vals = []
    evVo0 = s_evV.n

    vE, sE, mE = s_v.n, s_s.n, s_mm.n

    @blk.sync
    def _(sy):
        sy.wait_ge(s_v.h, vE)
        sy.wait_ge(s_s.h, sE)
        sy.wait_ge(s_mm.h, mE)
        for oc in range(CC):
            if oc >= 4:
                sy.wait_ge(s_evV.h, evVo0 + oc - 3)
            ins = sy.dma_start(out=b.sring.ap()[:, oc % 4], in_=sp_n0[oc])
            s_srs[oc % 4].inc(ins, 16)
            rco_vals.append(s_srs[oc % 4].n)

    def evac_o(oc, ps, mmv):
        t1v = s_t1[oc % 3].n

        @blk.vector
        def _(v):
            v.wait_ge(s_mm.h, mmv)
            v.wait_ge(s_srs[oc % 4].h, rco_vals[oc])
            v.wait_ge(s_t1[oc % 3].h, t1v)
            ins = v.tensor_tensor(b.tmp1.ap()[:, oc % 3], b.sring.ap()[:, oc % 4], psflat(ps), ALU.add)
            s_evV.inc(ins, 1)
        spill(oc, sp_xmid, s_evV, s_evV.n)

    proj_pass(ctx.dram["l0a_wo"], b.slotB, s_evV, evac_o, gate=[(s_slot, rwkv_ready)])
    spxm_done = t1_vals()

    stat_pass(sp_xmid, "g", gate=[(s, v) for s, v in zip(s_t1, spxm_done)])
    rows_from_stats()
    bcast_rows(4, 5)

    ln_apply(sp_xmid, b.slotA, V["ln2_g"], V["ln2_b"],
             [(s, v) for s, v in zip(s_t1, spxm_done)])

    bldk2 = build_mix(V["cmu_k"], b.slotA, b.slotB)
    sg0 = s_s.n
    evVg0 = s_evV.n

    def evac_k2(oc, ps, mmv):
        t1v = s_t1[oc % 3].n

        @blk.vector
        def _(v):
            v.wait_ge(s_mm.h, mmv)
            if oc >= 2:
                v.wait_ge(s_s.h, sg0 + oc - 1)
            ins = v.tensor_scalar(b.tmp2.ap()[:, oc % 2], psflat(ps), 0.0, None, ALU.max)
            s_evV.inc(ins, 1)

        @blk.scalar
        def _(s):
            s.wait_ge(s_evV.h, evVg0 + oc + 1)
            s.wait_ge(s_t1[oc % 3].h, t1v)
            ins = zact(s, b, b.tmp1.ap()[:, oc % 3], b.tmp2.ap()[:, oc % 2], AF.Square)
            s_s.inc(ins, 1)
        spill(oc, sp_k2, s_s, s_s.n)

    proj_pass(ctx.dram["l0a_cwk"], b.slotB, s_evV, evac_k2, gate=[(s_v, bldk2)])
    spk2_done = t1_vals()

    xr2_ready = build_mix(V["cmu_r"], b.slotA, b.slotB)

    @blk.sync
    def _(sy):
        sy.wait_ge(s_v.h, xr2_ready)
        for sem, val in zip(s_t1, spk2_done):
            sy.wait_ge(sem.h, val)
        for cc in range(CC):
            sy.dma_start(out=b.slotA.ap()[:, cc, 1:1 + T], in_=sp_k2[cc]).then_inc(s_slot.h, 16)
    s_slot.n += 16 * CC
    k2_ready = s_slot.n

    mmI0 = s_mm.n
    evSI0 = s_evS.n
    evVI0 = s_evV.n
    vI0 = s_v.n
    sI0 = s_s.n
    lwI_vals = []
    rcI_vals = []
    ldI_vals = []

    @blk.gpsimd
    def _(g):
        g.wait_ge(s_mm.h, mmI0)
        g.wait_ge(s_v.h, vI0)
        g.wait_ge(s_s.h, sI0)
        for oc in range(CC):
            if oc >= 1:
                g.wait_ge(s_mm.h, mmI0 + oc)
            ins = g.dma_start(out=b.wring.ap()[:, 0], in_=ctx.dram["l0a_cwv"][oc])
            s_wr[0].inc(ins, 16)
            ins = g.dma_start(out=b.wring.ap()[:, 1], in_=ctx.dram["l0a_cwr"][oc])
            s_wr[1].inc(ins, 16)
            lwI_vals.append((s_wr[0].n, s_wr[1].n))
            if oc >= 2:
                g.wait_ge(s_v.h, vI0 + oc - 1)
            ins = g.dma_start(out=b.sring.ap()[:, 2 + oc % 2], in_=x_in[oc])
            s_srg[2 + oc % 2].inc(ins, 16)
            ldI_vals.append(s_srg[2 + oc % 2].n)

    @blk.sync
    def _(sy):
        sy.wait_ge(s_mm.h, mmI0)
        sy.wait_ge(s_v.h, vI0)
        sy.wait_ge(s_s.h, sI0)
        for oc in range(CC):
            if oc >= 2:
                sy.wait_ge(s_v.h, vI0 + oc - 1)
            ins = sy.dma_start(out=b.sring.ap()[:, oc % 2], in_=sp_xmid[oc])
            s_srs[oc % 2].inc(ins, 16)
            rcI_vals.append(s_srs[oc % 2].n)

    @blk.tensor
    def _(t):
        t.wait_ge(s_slot.h, k2_ready)
        t.wait_ge(s_v.h, xr2_ready)
        t.wait_ge(s_evV.h, evVI0)
        t.wait_ge(s_evS.h, evSI0)
        for oc in range(CC):
            t.wait_ge(s_wr[0].h, lwI_vals[oc][0])
            t.wait_ge(s_wr[1].h, lwI_vals[oc][1])
            if oc >= 1:
                t.wait_ge(s_evV.h, evVI0 + oc)
                t.wait_ge(s_evS.h, evSI0 + oc)
            for cc in range(CC):
                for tt in range(NT):
                    t.matmul(b.psA.ap()[:, tt],
                             b.wring.ap()[:, 0].rearrange("p (cc o) -> p cc o", o=128)[:, cc],
                             b.slotA.ap()[:, cc, 1 + tt * TW:1 + (tt + 1) * TW],
                             start=(cc == 0), stop=(cc == CC - 1))
            for cc in range(CC):
                for tt in range(NT):
                    ins = t.matmul(b.psB.ap()[:, tt],
                                   b.wring.ap()[:, 1].rearrange("p (cc o) -> p cc o", o=128)[:, cc],
                                   b.slotB.ap()[:, cc, 1 + tt * TW:1 + (tt + 1) * TW],
                                   start=(cc == 0), stop=(cc == CC - 1))
            ins.then_inc(s_mm.h, 1)
    s_mm.n += CC

    for oc in range(CC):
        t1v = s_t1[oc % 3].n

        @blk.scalar
        def _(s, oc=oc):
            s.wait_ge(s_mm.h, mmI0 + oc + 1)
            if oc >= 2:
                s.wait_ge(s_evV.h, evVI0 + oc - 1)
            ins = zact(s, b, b.tmp2.ap()[:, oc % 2], psflat(b.psB), AF.Sigmoid)
            s_evS.inc(ins, 1)

        @blk.vector
        def _(v, oc=oc):
            v.wait_ge(s_evS.h, evSI0 + oc + 1)
            v.wait_ge(s_srs[oc % 2].h, rcI_vals[oc])
            v.wait_ge(s_srg[2 + oc % 2].h, ldI_vals[oc])
            v.wait_ge(s_t1[oc % 3].h, t1v)
            ins = v.tensor_tensor(b.tmp1.ap()[:, oc % 3], b.tmp2.ap()[:, oc % 2], psflat(b.psA), ALU.mult)
            s_evV.inc(ins, 1)
            s_evV.wait(v)
            ins = v.tensor_tensor(b.tmp1.ap()[:, oc % 3], b.tmp1.ap()[:, oc % 3],
                                  b.sring.ap()[:, oc % 2], ALU.add)
            s_cv.inc(ins, 1)
            s_cv.wait(v)
            ins = v.tensor_tensor(b.tmp1.ap()[:, oc % 3], b.tmp1.ap()[:, oc % 3],
                                  b.sring.ap()[:, 2 + oc % 2], ALU.add)
            s_v.inc(ins, 1)
        spill(oc, x_out, s_v, s_v.n)



def emit_l0_ffn(ctx, blk, b, x_in_name, x_out_name):
    x_in = ctx.dram[x_in_name]
    x_out = ctx.dram[x_out_name]
    sp_ymoe = ctx.dscratch("sp_ymoe", [CC, 128, T], BF16)
    sp_row2 = ctx.dscratch("sp_row_l0f", [4, T], F32)
    we_dram = ctx.dscratch("we_dram", [32, T], F32)

    s_mm = ctx.sem("s_mm")
    s_v = ctx.sem("s_v")
    s_s = ctx.sem("s_s")
    s_evV = ctx.sem("s_evV")
    s_evS = ctx.sem("s_evS")
    s_srg = [ctx.sem(f"s_srg{j}") for j in range(4)]
    s_srs = [ctx.sem(f"s_srs{j}") for j in range(4)]
    s_wr = [ctx.sem(f"s_wr{j}") for j in range(3)]
    s_t1 = [ctx.sem(f"s_t1{j}") for j in range(3)]
    s_bc = ctx.sem("s_bc")
    s_rsp = ctx.sem("s_rsp")
    s_cv = ctx.sem("s_cv")
    s_er = [ctx.sem(f"s_er{j}") for j in range(2)]
    if not hasattr(ctx, "dma_sems"):
        ctx.dma_sems = [s_bc, s_rsp] + s_srg + s_srs + s_wr + s_t1
    for x in s_er:
        if x not in ctx.dma_sems:
            ctx.dma_sems.append(x)

    fl32 = b.slotB.ap().rearrange("p a c -> p (a c)").bitcast(F32)

    def ch_bf(c):
        return b.slotB.ap()[:, c, 0:T]

    rg_sb, ru_sb, srg_sb, sru_sb = ch_bf(0), ch_bf(1), ch_bf(2), ch_bf(3)
    srg2_sb, sru2_sb, sh1_sb, sh2_sb = ch_bf(4), ch_bf(5), ch_bf(6), ch_bf(7)
    lgT_sb = fl32[:, 8 * 1026: 8 * 1026 + 2048]
    rtok_sb = fl32[:, 10 * 1026: 10 * 1026 + 512]
    weE_sb = fl32[:, 11 * 1026: 11 * 1026 + 2048]
    m8_sb = fl32[:, 13 * 1026: 13 * 1026 + 8]
    nm_sb = fl32[:, 13 * 1026 + 8: 13 * 1026 + 9]
    es_sb = fl32[:, 13 * 1026 + 16: 13 * 1026 + 48]
    esum_sb = fl32[:, 13 * 1026 + 48: 13 * 1026 + 49]
    et4_sb = fl32[:, 13 * 1026 + 56: 13 * 1026 + 60]
    den_sb = fl32[:, 13 * 1026 + 64: 13 * 1026 + 65]
    msk_sb = fl32[:, 13 * 1026 + 96: 13 * 1026 + 128]

    ag_sb = b.sring.ap()[:, 0, 0:1024]
    au_sb = b.sring.ap()[:, 0, 1024:2048]
    rd2_sb = b.sring.ap()[:, 2, 0:1024]
    web_sb = b.sring.ap()[:, 3].bitcast(F32)[:, 0:1024]

    def wsb(i):
        s4 = i % 4
        return b.slotB.ap()[:, 2 + s4 // 2, (s4 % 2) * 1024:(s4 % 2) * 1024 + 1024]

    ering = [b.aB.ap().bitcast(BF16)[:, 0:3200], b.bB.ap().bitcast(BF16)[:, 0:3200]]
    EOFF = {"ug": 0, "cg": 128, "uu": 1536, "cu": 1664, "ud": 3072}
    ident = ctx.ident_sb

    def stat_pass(src_ap):
        mm0 = s_mm.n
        sq0 = s_s.n
        ld_vals = []
        v_need, s_need = s_v.n, s_s.n

        @blk.gpsimd
        def _(g):
            g.wait_ge(s_v.h, v_need)
            g.wait_ge(s_s.h, s_need)
            g.wait_ge(s_mm.h, mm0)
            for ci in range(CC):
                if ci >= 4:
                    g.wait_ge(s_mm.h, mm0 + ci - 3)
                ins = g.dma_start(out=b.sring.ap()[:, ci % 4], in_=src_ap[ci])
                s_srg[ci % 4].inc(ins, 16)
                ld_vals.append(s_srg[ci % 4].n)

        @blk.scalar
        def _(s):
            for ci in range(CC):
                s.wait_ge(s_srg[ci % 4].h, ld_vals[ci])
                if ci >= 2:
                    s.wait_ge(s_mm.h, mm0 + ci - 1)
                zact(s, b, b.tmp2.ap()[:, ci % 2], b.sring.ap()[:, ci % 4], AF.Square).then_inc(s_s.h, 1)
                s_s.n += 1

        evV0, evS0 = s_evV.n, s_evS.n

        @blk.tensor
        def _(t):
            t.wait_ge(s_evV.h, evV0)
            t.wait_ge(s_evS.h, evS0)
            for ci in range(CC):
                t.wait_ge(s_s.h, sq0 + ci + 1)
                for tt in range(NT):
                    t.matmul(b.psA.ap()[0:1, tt], b.ones_bf[:],
                             b.sring.ap()[:, ci % 4, tt * TW:(tt + 1) * TW],
                             start=(ci == 0), stop=(ci == CC - 1))
                    ins = t.matmul(b.psB.ap()[0:1, tt], b.ones_bf[:],
                                   b.tmp2.ap()[:, ci % 2, tt * TW:(tt + 1) * TW],
                                   start=(ci == 0), stop=(ci == CC - 1))
                ins.then_inc(s_mm.h, 1)
        s_mm.n += CC

    def rows_from_stats():
        r0 = b.aB.ap()[0:1, :]
        r1 = b.bB.ap()[0:1, :]

        @blk.vector
        def _(v):
            s_mm.wait(v)
            s_bc.wait(v)
            for tt in range(NT):
                ins = v.tensor_scalar(r1[:, tt * TW:(tt + 1) * TW], b.psA.ap()[0:1, tt], 1.0 / D, None, ALU.mult)
            s_v.inc(ins, 1)

        @blk.scalar
        def _(s):
            s_v.wait(s)
            ins = zact(s, b, r0, r1, AF.Square)
            s_s.inc(ins, 1)

        @blk.vector
        def _(v):
            s_s.wait(v)
            for tt in range(NT):
                ins = v.tensor_scalar(r1[:, tt * TW:(tt + 1) * TW], b.psB.ap()[0:1, tt], 1.0 / D, None, ALU.mult)
            s_cv.inc(ins, 1)
            s_cv.wait(v)
            ins = v.tensor_tensor(r1, r1, r0, ALU.subtract)
            s_cv.inc(ins, 1)
            s_cv.wait(v)
            ins = v.tensor_scalar(r1, r1, 1.0, LN_EPS, ALU.mult, ALU.add)
            s_cv.inc(ins, 1)
            s_cv.wait(v)
            ins = v.reciprocal(r0, r1)
            s_v.inc(ins, 1)

        @blk.scalar
        def _(s):
            s_v.wait(s)
            ins = zact(s, b, r0, r0, AF.Sqrt)
            s_s.inc(ins, 1)

        @blk.vector
        def _(v):
            s_s.wait(v)
            for tt in range(NT):
                ins = v.tensor_scalar(r1[:, tt * TW:(tt + 1) * TW], b.psA.ap()[0:1, tt], -1.0 / D, None, ALU.mult)
            s_cv.inc(ins, 1)
            s_cv.wait(v)
            ins = v.tensor_tensor(r1, r1, r0, ALU.mult)
            s_evV.inc(ins, 1)

    def bcast_rows(slot_a, slot_b):
        row_done = s_evV.n

        @blk.scalar
        def _(s):
            s.wait_ge(s_evV.h, row_done)
            s.dma_start(out=sp_row2[slot_a:slot_a + 1], in_=b.aB.ap()[0:1, :]).then_inc(s_rsp.h, 16)
            s.dma_start(out=sp_row2[slot_b:slot_b + 1], in_=b.bB.ap()[0:1, :]).then_inc(s_rsp.h, 16)
        s_rsp.n += 32

        mm_need_bc = s_mm.n

        @blk.sync
        def _(sy):
            s_rsp.wait(sy)
            sy.wait_ge(s_mm.h, mm_need_bc)
            sy.dma_start(out=b.aB[:], in_=sp_row2[slot_a].unsqueeze(0).broadcast_to([128, T])).then_inc(s_bc.h, 16)
            sy.dma_start(out=b.bB[:], in_=sp_row2[slot_b].unsqueeze(0).broadcast_to([128, T])).then_inc(s_bc.h, 16)
        s_bc.n += 32

    def ln_apply_plain(src_dram, dst_slot):
        bc_need = s_bc.n
        mm_need = s_mm.n
        bld0 = s_v.n
        ss_need = s_s.n
        ld_vals = []

        @blk.sync
        def _(sy):
            sy.wait_ge(s_v.h, bld0)
            sy.wait_ge(s_s.h, ss_need)
            sy.wait_ge(s_mm.h, mm_need)
            for ci in range(CC):
                if ci >= 4:
                    sy.wait_ge(s_v.h, bld0 + ci - 3)
                ins = sy.dma_start(out=b.sring.ap()[:, ci % 4], in_=src_dram[ci])
                s_srs[ci % 4].inc(ins, 16)
                ld_vals.append(s_srs[ci % 4].n)

        @blk.vector
        def _(v):
            v.wait_ge(s_bc.h, bc_need)
            v.wait_ge(s_mm.h, mm_need)
            v.wait_ge(s_s.h, ss_need)
            for ci in range(CC):
                v.wait_ge(s_srs[ci % 4].h, ld_vals[ci])
                v.wait_ge(s_v.h, s_v.n)
                ins = v.tensor_tensor(b.tmp2.ap()[:, 0], b.sring.ap()[:, ci % 4], b.aB[:], ALU.mult)
                s_cv.inc(ins, 1)
                s_cv.wait(v)
                ins = v.tensor_tensor(dst_slot.ap()[:, ci, 1:1 + T], b.tmp2.ap()[:, 0], b.bB[:], ALU.add)
                s_v.inc(ins, 1)

    def proj2(w_ap, n_oc, n_cc, rhs_fn, evac, gate=None, o_width=128):
        ev_sem = s_evS
        ev0 = ev_sem.n
        mm0 = s_mm.n
        evV_need, evS_need = s_evV.n, s_evS.n
        lw_vals = []

        @blk.gpsimd
        def _(g):
            g.wait_ge(s_mm.h, mm0)
            for oc in range(n_oc):
                if oc >= 3:
                    g.wait_ge(ev_sem.h, ev0 + oc - 2)
                ins = g.dma_start(out=b.wring.ap()[:, oc % 3, 0:n_cc * o_width], in_=w_ap[oc])
                s_wr[oc % 3].inc(ins, 16)
                lw_vals.append(s_wr[oc % 3].n)

        @blk.tensor
        def _(t):
            if gate is not None:
                for sem, val in gate:
                    t.wait_ge(sem.h, val)
            t.wait_ge(s_evV.h, evV_need)
            t.wait_ge(s_evS.h, evS_need)
            for oc in range(n_oc):
                ps = b.psA if oc % 2 == 0 else b.psB
                t.wait_ge(s_wr[oc % 3].h, lw_vals[oc])
                if oc >= 2:
                    t.wait_ge(ev_sem.h, ev0 + oc - 1)
                wv = b.wring.ap()[:, oc % 3, 0:n_cc * o_width].rearrange("p (cc o) -> p cc o", o=o_width)
                for cc in range(n_cc):
                    for tt in range(NT):
                        ins = t.matmul(ps.ap()[0:o_width, tt], wv[:, cc], rhs_fn(cc, tt),
                                       start=(cc == 0), stop=(cc == n_cc - 1))
                ins.then_inc(s_mm.h, 1)
        s_mm.n += n_oc

        for oc in range(n_oc):
            evac(oc, b.psA if oc % 2 == 0 else b.psB, mm0 + oc + 1)

    def n1_rhs(cc, tt):
        return b.slotA.ap()[:, cc, 1 + tt * TW:1 + (tt + 1) * TW]

    def mk_copy_evac(dst_ap, nrows=128):
        def evac(oc, ps, mmv):
            @blk.scalar
            def _(s):
                s.wait_ge(s_mm.h, mmv)
                ins = zcopy(s, b, dst_ap, psflat(ps)[0:nrows, :])
                s_evS.inc(ins, 1)
        return evac

    t1_entry = [s.n for s in s_t1]

    @blk.gpsimd
    def _(g):
        for sem, val in zip(s_t1, t1_entry):
            g.wait_ge(sem.h, val)

    @blk.sync
    def _(sy):
        for sem, val in zip(s_t1, t1_entry):
            sy.wait_ge(sem.h, val)

    stat_pass(x_in)
    rows_from_stats()
    bcast_rows(0, 1)
    ln_apply_plain(x_in, b.slotA)
    n1_done = s_v.n

    proj2(ctx.dram["l0f_rg"], 1, CC, n1_rhs, mk_copy_evac(rg_sb), gate=[(s_v, n1_done)])
    proj2(ctx.dram["l0f_ru"], 1, CC, n1_rhs, mk_copy_evac(ru_sb))
    proj2(ctx.dram["l0f_srg"], 1, CC, n1_rhs, mk_copy_evac(srg_sb))
    proj2(ctx.dram["l0f_sru"], 1, CC, n1_rhs, mk_copy_evac(sru_sb))
    proj2(ctx.dram["l0f_gate"], 1, CC, n1_rhs, mk_copy_evac(lgT_sb[0:32, :], nrows=32), o_width=32)
    proj2(ctx.dram["l0f_sug"], 1, 1, lambda cc, tt: srg_sb[:, tt * TW:(tt + 1) * TW],
          mk_copy_evac(srg2_sb), gate=[(s_evS, s_evS.n)])
    proj2(ctx.dram["l0f_suu"], 1, 1, lambda cc, tt: sru_sb[:, tt * TW:(tt + 1) * TW],
          mk_copy_evac(sru2_sb), gate=[(s_evS, s_evS.n)])

    lg_done = s_evS.n
    mmR0 = s_mm.n
    evV_R = s_evV.n
    vbaseR = s_v.n

    @blk.tensor
    def _(t):
        t.wait_ge(s_evS.h, lg_done)
        t.wait_ge(s_evV.h, evV_R)
        for i in range(CC):
            if i >= 1:
                t.wait_ge(s_v.h, vbaseR + i)
            ins = t.transpose(b.psB.ap()[:, 0, 0:32], lgT_sb[0:32, i * 128:(i + 1) * 128],
                              ident[0:32, 0:32])
            ins.then_inc(s_mm.h, 1)
    s_mm.n += CC

    @blk.vector
    def _(v):
        for i in range(CC):
            v.wait_ge(s_mm.h, mmR0 + i + 1)
            ins = v.tensor_copy(rtok_sb[:, i * 32:(i + 1) * 32], b.psB.ap()[:, 0, 0:32])
            s_v.inc(ins, 1)
    rtok_done = s_v.n

    sR0 = s_s.n
    for i in range(CC):
        @blk.vector
        def _(v, i=i):
            v.wait_ge(s_v.h, rtok_done)
            if i >= 1:
                v.wait_ge(s_s.h, sR0 + 2 * i)
            rt = rtok_sb[:, i * 32:(i + 1) * 32]
            ins = v.max(m8_sb[:, 0:8], rt)
            s_cv.inc(ins, 1)
            s_cv.wait(v)
            ins = v.tensor_scalar(nm_sb, m8_sb[:, 0:1], -1.0, None, ALU.mult)
            s_v.inc(ins, 1)

        @blk.scalar
        def _(s, i=i):
            s_v.wait(s)
            rt = rtok_sb[:, i * 32:(i + 1) * 32]
            ins = zact(s, b, es_sb[:, 0:32], rt, AF.Exp, bias=nm_sb, accum_out=esum_sb)
            s_s.inc(ins, 1)
            ins = zact(s, b, et4_sb[:, 0:4], m8_sb[:, 0:4], AF.Exp, bias=nm_sb)
            s_s.inc(ins, 1)

        @blk.vector
        def _(v, i=i):
            s_s.wait(v)
            rt = rtok_sb[:, i * 32:(i + 1) * 32]
            ins = v.reduce_sum(den_sb, et4_sb[:, 0:4], axis=mybir.AxisListType.X)
            s_cv.inc(ins, 1)
            s_cv.wait(v)
            ins = v.scalar_tensor_tensor(den_sb, esum_sb, 1e-20, den_sb, ALU.mult, ALU.add)
            s_cv.inc(ins, 1)
            s_cv.wait(v)
            ins = v.reciprocal(den_sb, den_sb)
            s_cv.inc(ins, 1)
            s_cv.wait(v)
            ins = v.tensor_scalar(msk_sb[:, 0:32], rt, m8_sb[:, 3:4], None, ALU.is_ge)
            s_cv.inc(ins, 1)
            s_cv.wait(v)
            ins = v.tensor_scalar(es_sb[:, 0:32], es_sb[:, 0:32], den_sb, None, ALU.mult)
            s_cv.inc(ins, 1)
            s_cv.wait(v)
            ins = v.tensor_tensor(rtok_sb[:, i * 32:(i + 1) * 32], es_sb[:, 0:32], msk_sb[:, 0:32], ALU.mult)
            s_v.inc(ins, 1)

    we_done = s_v.n
    mmW0 = s_mm.n
    vbaseW = s_v.n

    @blk.tensor
    def _(t):
        t.wait_ge(s_v.h, we_done)
        for i in range(CC):
            if i >= 1:
                t.wait_ge(s_v.h, vbaseW + i)
            ins = t.transpose(b.psB.ap()[0:32, 0, 0:128], rtok_sb[:, i * 32:(i + 1) * 32],
                              ident[:, 0:128])
            ins.then_inc(s_mm.h, 1)
    s_mm.n += CC

    @blk.vector
    def _(v):
        for i in range(CC):
            v.wait_ge(s_mm.h, mmW0 + i + 1)
            ins = v.tensor_copy(weE_sb[0:32, i * 128:(i + 1) * 128], b.psB.ap()[0:32, 0, 0:128])
            s_v.inc(ins, 1)
    weE_done = s_v.n

    @blk.scalar
    def _(s):
        s.wait_ge(s_v.h, weE_done)
        s.dma_start(out=we_dram, in_=weE_sb[0:32, :]).then_inc(s_rsp.h, 16)
    s_rsp.n += 16
    we_spilled = s_rsp.n

    NSI = 22
    sCg, sCu, sRdT = ctx.dram["l0f_scg"], ctx.dram["l0f_scu"], ctx.dram["l0f_srd"]
    mmS0 = s_mm.n
    evV_S0 = s_evV.n
    evS_S0 = s_evS.n
    vS0 = s_v.n
    lwS_vals = []

    @blk.gpsimd
    def _(g):
        g.wait_ge(s_mm.h, mmS0)
        for si in range(NSI):
            if si >= 1:
                g.wait_ge(s_mm.h, mmS0 + 2 * NT * si)
            ins = g.dma_start(out=b.wring.ap()[:, 0, 0:128], in_=sCg[si])
            s_wr[0].inc(ins, 16)
            ins = g.dma_start(out=b.wring.ap()[:, 1, 0:128], in_=sCu[si])
            s_wr[1].inc(ins, 16)
            ins = g.dma_start(out=b.wring.ap()[:, 2, 0:128], in_=sRdT[0, :, si * 128:(si + 1) * 128])
            s_wr[2].inc(ins, 16)
            lwS_vals.append((s_wr[0].n, s_wr[1].n, s_wr[2].n))

    @blk.tensor
    def _(t):
        t.wait_ge(s_evV.h, evV_S0)
        t.wait_ge(s_evS.h, evS_S0)
        for si in range(NSI):
            t.wait_ge(s_wr[0].h, lwS_vals[si][0])
            t.wait_ge(s_wr[1].h, lwS_vals[si][1])
            t.wait_ge(s_wr[2].h, lwS_vals[si][2])
            for tq in range(NT):
                k = 4 * si + tq
                if k >= 2:
                    t.wait_ge(s_evS.h, evS_S0 + k - 1)
                    t.wait_ge(s_evV.h, evV_S0 + k - 1)
                t.matmul(b.psA.ap()[:, tq % 2], b.wring.ap()[:, 0, 0:128],
                         srg2_sb[:, tq * TW:(tq + 1) * TW], start=True, stop=True)
                ins = t.matmul(b.psA.ap()[:, 2 + tq % 2], b.wring.ap()[:, 1, 0:128],
                               sru2_sb[:, tq * TW:(tq + 1) * TW], start=True, stop=True)
                ins.then_inc(s_mm.h, 1)
                t.wait_ge(s_evV.h, evV_S0 + k + 1)
                ins = t.matmul(b.psB.ap()[:, tq], b.wring.ap()[:, 2, 0:128],
                               b.tmp1.ap()[:, k % 3, 0:TW],
                               start=(si == 0), stop=(si == NSI - 1))
                ins.then_inc(s_mm.h, 1)
    s_mm.n += NSI * NT * 2

    for si in range(NSI):
        for tq in range(NT):
            k = 4 * si + tq

            @blk.scalar
            def _(s, si=si, tq=tq, k=k):
                s.wait_ge(s_mm.h, mmS0 + 2 * k + 1)
                if k >= 2:
                    s.wait_ge(s_evV.h, evV_S0 + k - 1)
                ins = zact(s, b, b.tmp2.ap()[:, k % 2, 0:TW], b.psA.ap()[:, tq % 2], AF.Silu)
                s_evS.inc(ins, 1)

            @blk.vector
            def _(v, si=si, tq=tq, k=k):
                v.wait_ge(s_evS.h, evS_S0 + k + 1)
                if k >= 3:
                    v.wait_ge(s_mm.h, mmS0 + 2 * (k - 3) + 2)
                ins = v.tensor_tensor(b.tmp1.ap()[:, k % 3, 0:TW], b.tmp2.ap()[:, k % 2, 0:TW],
                                      b.psA.ap()[:, 2 + tq % 2], ALU.mult)
                s_evV.inc(ins, 1)

    shmm = s_mm.n

    @blk.scalar
    def _(s):
        s.wait_ge(s_mm.h, shmm)
        for tq in range(NT):
            ins = zcopy(s, b, sh1_sb[:, tq * TW:(tq + 1) * TW], b.psB.ap()[:, tq])
            s_evS.inc(ins, 1)
    sh1_done = s_evS.n

    proj2(ctx.dram["l0f_sud"], 1, 1, lambda cc, tt: sh1_sb[:, tt * TW:(tt + 1) * TW],
          mk_copy_evac(sh2_sb), gate=[(s_evS, sh1_done)])
    sh2_done = s_evS.n

    rdt = ctx.dram["l0f_rdt"]
    epack = ctx.dram["l0f_epack"]
    cd2 = ctx.dram["l0f_cdpack"]
    mm_pre = s_mm.n
    v_pre, s_pre = s_v.n, s_s.n

    rdt_sb = ctx.nc.alloc_sbuf_tensor("rdt_sb", [128, 1408], BF16)

    @blk.gpsimd
    def _(g):
        g.wait_ge(s_mm.h, mm_pre)
        ins = g.dma_start(out=rdt_sb[:], in_=rdt[0])
        s_wr[2].inc(ins, 16)
    rdt_ready = s_wr[2].n

    ad_slot = lambda e, q: b.slotA.ap()[:, e // 2, (e % 2) * 1024 + q * TW:(e % 2) * 1024 + (q + 1) * TW]

    _torder = [("agau", -1)]
    for _i in range(11):
        _torder.append(("gu", _i))
        if _i >= 2:
            _torder.append(("rd", _i - 2))
    _torder += [("rd", 9), ("rd", 10), ("ad", -1)]
    TM = {op: ix + 1 for ix, op in enumerate(_torder)}
    _vorder = [("agc", -1), ("auc", -1), ("rd2", -1), ("adc", -1)]
    VM = {op: ix + 1 for ix, op in enumerate(_vorder)}
    NV = len(_vorder)
    NT_E = len(_torder)
    tmp2fl = b.tmp2.ap().rearrange("p a c -> p (a c)")
    s_p = ctx.sem("s_p")
    s_wm = ctx.sem("s_wm")
    psAflat = psflat(b.psA)
    psBflat = psflat(b.psB)

    def gsl(i):
        return tmp2fl[:, (i % 4) * 1024:(i % 4) * 1024 + 1024]

    mm_after_expert = []
    for half in range(2):
        t0 = half * 1024
        for e in range(32):
            eid = half * 32 + e
            evV_entry = s_evV.n
            bc_pre = s_bc.n
            rsp_pre = s_rsp.n

            @blk.gpsimd
            def _(g, e=e, eid=eid, t0=t0, evV_entry=evV_entry, bc_pre=bc_pre, rsp_pre=rsp_pre):
                if eid >= 2:
                    g.wait_ge(s_mm.h, mm_after_expert[-2])
                elif eid <= 1:
                    g.wait_ge(s_v.h, v_pre)
                    g.wait_ge(s_s.h, s_pre)
                    g.wait_ge(s_mm.h, mm_pre)
                    g.wait_ge(s_bc.h, bc_pre)
                    g.wait_ge(s_rsp.h, rsp_pre)
                ins = g.dma_start(out=ering[eid % 2], in_=epack[e])
                s_er[eid % 2].inc(ins, 16)
                g.wait_ge(s_evV.h, evV_entry)
                g.wait_ge(s_rsp.h, we_spilled)
                ins = g.dma_start(out=web_sb,
                                  in_=we_dram[e, t0:t0 + 1024].unsqueeze(0).broadcast_to([128, 1024]))
                s_srg[3].inc(ins, 16)
            er_val = s_er[eid % 2].n
            web_val = s_srg[3].n

            mme0 = s_mm.n
            evSe0 = s_evS.n
            evVe0 = s_evV.n
            pe0 = s_p.n
            wme0 = s_wm.n

            @blk.tensor
            def _(t, e=e, eid=eid, t0=t0, mme0=mme0, evSe0=evSe0, evVe0=evVe0, pe0=pe0,
                  wme0=wme0, er_val=er_val):
                t.wait_ge(s_er[eid % 2].h, er_val)
                t.wait_ge(s_wr[2].h, rdt_ready)
                if eid == 0:
                    t.wait_ge(s_mm.h, mm_pre)
                    t.wait_ge(s_evV.h, evVe0)
                    t.wait_ge(s_p.h, pe0)
                er = ering[eid % 2]
                t.wait_ge(s_evS.h, evSe0)
                for q in range(2):
                    t.matmul(b.psA.ap()[:, q], er[:, EOFF["ug"]:EOFF["ug"] + 128],
                             rg_sb[:, t0 + q * TW:t0 + (q + 1) * TW], start=True, stop=True)
                for q in range(2):
                    ins = t.matmul(b.psA.ap()[:, 2 + q], er[:, EOFF["uu"]:EOFF["uu"] + 128],
                                   ru_sb[:, t0 + q * TW:t0 + (q + 1) * TW], start=True, stop=True)
                ins.then_inc(s_mm.h, 1)
                for i in range(11):
                    if i == 0:
                        t.wait_ge(s_evV.h, evVe0 + VM[("agc", -1)])
                    elif i == 1:
                        t.wait_ge(s_evV.h, evVe0 + VM[("auc", -1)])
                    else:
                        t.wait_ge(s_evS.h, evSe0 + i - 1)
                    for q in range(2):
                        t.matmul(b.psA.ap()[:, 2 * (i % 2) + q],
                                 er[:, EOFF["cg"] + i * 128:EOFF["cg"] + (i + 1) * 128],
                                 ag_sb[:, q * TW:(q + 1) * TW], start=True, stop=True)
                    if i == 0:
                        t.wait_ge(s_evV.h, evVe0 + VM[("auc", -1)])
                    else:
                        t.wait_ge(s_p.h, pe0 + i)
                    for q in range(2):
                        ins = t.matmul(b.psB.ap()[:, 2 + q],
                                       er[:, EOFF["cu"] + i * 128:EOFF["cu"] + (i + 1) * 128],
                                       au_sb[:, q * TW:(q + 1) * TW], start=True, stop=True)
                    ins.then_inc(s_mm.h, 1)
                    if i >= 2:
                        j = i - 2
                        t.wait_ge(s_wm.h, wme0 + j + 1)
                        for q in range(2):
                            ins = t.matmul(b.psB.ap()[:, q], rdt_sb.ap()[:, j * 128:(j + 1) * 128],
                                           wsb(j)[:, q * TW:(q + 1) * TW],
                                           start=(j == 0), stop=False)
                        ins.then_inc(s_mm.h, 1)
                for j in (9, 10):
                    t.wait_ge(s_wm.h, wme0 + j + 1)
                    for q in range(2):
                        ins = t.matmul(b.psB.ap()[:, q], rdt_sb.ap()[:, j * 128:(j + 1) * 128],
                                       wsb(j)[:, q * TW:(q + 1) * TW],
                                       start=False, stop=(j == 10))
                    ins.then_inc(s_mm.h, 1)
                t.wait_ge(s_evV.h, evVe0 + VM[("rd2", -1)])
                t.wait_ge(s_p.h, pe0 + 11)
                for q in range(2):
                    ins = t.matmul(b.psB.ap()[:, 2 + q], er[:, EOFF["ud"]:EOFF["ud"] + 128],
                                   rd2_sb[:, q * TW:(q + 1) * TW], start=True, stop=True)
                ins.then_inc(s_mm.h, 1)
            s_mm.n += NT_E
            mm_after_expert.append(s_mm.n)

            for i in range(11):
                @blk.scalar
                def _(s, i=i, eid=eid, mme0=mme0, wme0=wme0):
                    s.wait_ge(s_mm.h, mme0 + TM[("gu", i)])
                    if i >= 4:
                        s.wait_ge(s_wm.h, wme0 + i - 3)
                    elif eid > 0:
                        s.wait_ge(s_wm.h, wme0 - 11 + 8 + i)
                    ins = zact(s, b, gsl(i), psAflat[:, (i % 2) * 1024:(i % 2) * 1024 + 1024], AF.Silu)
                    s_evS.inc(ins, 1)

            @blk.vector
            def _(v, mme0=mme0):
                v.wait_ge(s_mm.h, mme0 + TM[("agau", -1)])
                ins = v.tensor_copy(ag_sb, psAflat[:, 0:1024])
                s_evV.inc(ins, 1)
                ins = v.tensor_copy(au_sb, psAflat[:, 1024:2048])
                s_evV.inc(ins, 1)

            def emit_uc(i, mme0=mme0):
                @blk.vector
                def _(v, i=i, mme0=mme0):
                    v.wait_ge(s_mm.h, mme0 + TM[("gu", i)])
                    ins = v.tensor_copy(wsb(i), psBflat[:, 1024:2048])
                    s_p.inc(ins, 1)

            def emit_wmul(i, evSe0=evSe0, pe0=pe0):
                @blk.vector
                def _(v, i=i, evSe0=evSe0, pe0=pe0):
                    v.wait_ge(s_evS.h, evSe0 + i + 1)
                    v.wait_ge(s_p.h, pe0 + i + 1)
                    ins = v.tensor_tensor(wsb(i), wsb(i), gsl(i), ALU.mult)
                    s_wm.inc(ins, 1)

            emit_uc(0)
            for i in range(1, 11):
                emit_uc(i)
                emit_wmul(i - 1)
            emit_wmul(10)

            @blk.vector
            def _(v, mme0=mme0, web_val=web_val):
                v.wait_ge(s_mm.h, mme0 + TM[("rd", 10)])
                v.wait_ge(s_srg[3].h, web_val)
                ins = v.tensor_tensor(rd2_sb, psBflat[:, 0:1024], web_sb, ALU.mult)
                s_evV.inc(ins, 1)

            @blk.vector
            def _(v, e=e, mme0=mme0):
                v.wait_ge(s_mm.h, mme0 + TM[("ad", -1)])
                ins = v.tensor_copy(b.slotA.ap()[:, e // 2, (e % 2) * 1024:(e % 2) * 1024 + 1024],
                                    psBflat[:, 1024:2048])
                s_evV.inc(ins, 1)

        ad_ready = s_evS.n
        mmy_entry = s_mm.n
        grp_mark = []
        grp_load = []
        for oc in range(CC):
            for gq in range(2):
                gidx = oc * 2 + gq
                slot = gidx % 3
                gate_val = grp_mark[gidx - 3] if gidx >= 3 else mmy_entry

                @blk.gpsimd
                def _(g, oc=oc, gq=gq, slot=slot, gate_val=gate_val):
                    g.wait_ge(s_mm.h, gate_val)
                    ins = g.dma_start(out=b.wring.ap()[:, slot], in_=cd2[oc, gq])
                    s_wr[slot].inc(ins, 16)
                grp_load.append((slot, s_wr[slot].n))
                grp_mark.append(mmy_entry + gidx + 1)

            evy_entry = s_evV.n

            @blk.tensor
            def _(t, oc=oc, evy_entry=evy_entry, gl=list(grp_load[-2:])):
                t.wait_ge(s_evS.h, ad_ready)
                t.wait_ge(s_evV.h, evy_entry)
                ps = b.psA if oc % 2 == 0 else b.psB
                for gq in range(2):
                    slot, lv = gl[gq]
                    t.wait_ge(s_wr[slot].h, lv)
                    for el in range(16):
                        e = gq * 16 + el
                        for q in range(2):
                            ins = t.matmul(ps.ap()[:, q], b.wring.ap()[:, slot, el * 128:(el + 1) * 128],
                                           b.slotA.ap()[:, e // 2, (e % 2) * 1024 + q * TW:(e % 2) * 1024 + (q + 1) * TW],
                                           start=(e == 0), stop=(e == 31))
                    ins.then_inc(s_mm.h, 1)
            s_mm.n += 2

            t1v = s_t1[oc % 3].n
            mmy_done = s_mm.n

            @blk.vector
            def _(v, oc=oc, t1v=t1v, mmy_done=mmy_done):
                v.wait_ge(s_mm.h, mmy_done)
                v.wait_ge(s_t1[oc % 3].h, t1v)
                ps = b.psA if oc % 2 == 0 else b.psB
                ins = v.tensor_copy(b.tmp1.ap()[:, oc % 3, 0:1024],
                                    ps.ap().rearrange("p a c -> p (a c)")[:, 0:1024])
                s_evV.inc(ins, 1)

            evy_done = s_evV.n

            @blk.scalar
            def _(s, oc=oc, half=half, evy_done=evy_done):
                s.wait_ge(s_evV.h, evy_done)
                ins = s.dma_start(out=sp_ymoe[oc][:, half * 1024:(half + 1) * 1024],
                                  in_=b.tmp1.ap()[:, oc % 3, 0:1024])
                s_t1[oc % 3].inc(ins, 16)

    ymoe_done = [s.n for s in s_t1]

    sCd = ctx.dram["l0f_scd"]
    mmF0 = s_mm.n
    evVF0, evSF0 = s_evV.n, s_evS.n
    vF0 = s_v.n
    lwF_vals = []
    rcF_vals = []
    ldF_vals = []

    vpreF, spreF = s_v.n, s_s.n

    @blk.gpsimd
    def _(g):
        g.wait_ge(s_mm.h, mmF0)
        g.wait_ge(s_v.h, vpreF)
        g.wait_ge(s_s.h, spreF)
        for oc in range(CC):
            if oc >= 3:
                g.wait_ge(s_evV.h, evVF0 + oc - 2)
            ins = g.dma_start(out=b.wring.ap()[:, oc % 3, 0:128], in_=sCd[oc])
            s_wr[oc % 3].inc(ins, 16)
            lwF_vals.append(s_wr[oc % 3].n)
            if oc >= 2:
                g.wait_ge(s_v.h, vF0 + oc - 1)
            ins = g.dma_start(out=b.sring.ap()[:, 2 + oc % 2], in_=x_in[oc])
            s_srg[2 + oc % 2].inc(ins, 16)
            ldF_vals.append(s_srg[2 + oc % 2].n)

    @blk.sync
    def _(sy):
        for sem, val in zip(s_t1, ymoe_done):
            sy.wait_ge(sem.h, val)
        sy.wait_ge(s_v.h, vpreF)
        sy.wait_ge(s_s.h, spreF)
        sy.wait_ge(s_mm.h, mmF0)
        for oc in range(CC):
            if oc >= 2:
                sy.wait_ge(s_evV.h, evVF0 + oc - 1)
            ins = sy.dma_start(out=b.sring.ap()[:, oc % 2], in_=sp_ymoe[oc])
            s_srs[oc % 2].inc(ins, 16)
            rcF_vals.append(s_srs[oc % 2].n)

    @blk.tensor
    def _(t):
        t.wait_ge(s_evV.h, evVF0)
        t.wait_ge(s_evS.h, evSF0)
        for oc in range(CC):
            ps = b.psA if oc % 2 == 0 else b.psB
            t.wait_ge(s_wr[oc % 3].h, lwF_vals[oc])
            if oc >= 2:
                t.wait_ge(s_evV.h, evVF0 + oc - 1)
            for tt in range(NT):
                ins = t.matmul(ps.ap()[:, tt], b.wring.ap()[:, oc % 3, 0:128],
                               sh2_sb[:, tt * TW:(tt + 1) * TW], start=True, stop=True)
            ins.then_inc(s_mm.h, 1)
    s_mm.n += CC

    for oc in range(CC):
        t1v = s_t1[oc % 3].n

        @blk.vector
        def _(v, oc=oc, t1v=t1v):
            v.wait_ge(s_mm.h, mmF0 + oc + 1)
            v.wait_ge(s_srs[oc % 2].h, rcF_vals[oc])
            v.wait_ge(s_srg[2 + oc % 2].h, ldF_vals[oc])
            v.wait_ge(s_t1[oc % 3].h, t1v)
            ps = b.psA if oc % 2 == 0 else b.psB
            ins = v.tensor_tensor(b.tmp1.ap()[:, oc % 3], b.sring.ap()[:, oc % 2], psflat(ps), ALU.add)
            s_evV.inc(ins, 1)
            s_evV.wait(v)
            ins = v.tensor_tensor(b.tmp1.ap()[:, oc % 3], b.tmp1.ap()[:, oc % 3],
                                  b.sring.ap()[:, 2 + oc % 2], ALU.add)
            s_v.inc(ins, 1)

        @blk.scalar
        def _(s, oc=oc):
            s.wait_ge(s_v.h, s_v.n)
            ins = s.dma_start(out=x_out[oc], in_=b.tmp1.ap()[:, oc % 3])
            s_t1[oc % 3].inc(ins, 16)


def host_pack_l0f(inp):
    gw = np.asarray(inp["L0_ffn_gate_w"], np.float32)
    gpack = np.ascontiguousarray(gw.reshape(32, 16, 128).transpose(2, 1, 0)).reshape(1, 128, 512)
    Ug = np.asarray(inp["L0_ffn_Ug"], np.float32)
    Cg = np.asarray(inp["L0_ffn_Cg"], np.float32)
    Uu = np.asarray(inp["L0_ffn_Uu"], np.float32)
    Cu = np.asarray(inp["L0_ffn_Cu"], np.float32)
    Ud = np.asarray(inp["L0_ffn_Ud"], np.float32)
    Cd = np.asarray(inp["L0_ffn_Cd"], np.float32)
    epack = np.concatenate([Ug.transpose(0, 2, 1), Cg.transpose(0, 2, 1),
                            Uu.transpose(0, 2, 1), Cu.transpose(0, 2, 1),
                            Ud.transpose(0, 2, 1)], axis=2)
    epack = np.ascontiguousarray(epack)
    cdpack = Cd.transpose(0, 2, 1).reshape(32, 128, 16, 128).transpose(0, 2, 1, 3)
    cdpack = np.ascontiguousarray(
        cdpack.transpose(1, 0, 2, 3).reshape(16, 2, 16, 128, 128).transpose(0, 1, 3, 2, 4)
        .reshape(16, 2, 128, 2048))
    return {
        "l0f_rg": pack_wT(inp["L0_ffn_Rg"]),
        "l0f_ru": pack_wT(inp["L0_ffn_Ru"]),
        "l0f_srg": pack_wT(inp["L0_ffn_s_Rg"]),
        "l0f_sru": pack_wT(inp["L0_ffn_s_Ru"]),
        "l0f_gate": gpack,
        "l0f_sug": pack_wT(inp["L0_ffn_s_Ug"]),
        "l0f_suu": pack_wT(inp["L0_ffn_s_Uu"]),
        "l0f_sud": pack_wT(inp["L0_ffn_s_Ud"]),
        "l0f_scg": pack_wT(inp["L0_ffn_s_Cg"]),
        "l0f_scu": pack_wT(inp["L0_ffn_s_Cu"]),
        "l0f_srd": pack_wT(inp["L0_ffn_s_Rd"]),
        "l0f_scd": pack_wT(inp["L0_ffn_s_Cd"]),
        "l0f_rdt": pack_wT(inp["L0_ffn_Rd"]),
        "l0f_epack": epack,
        "l0f_cdpack": cdpack,
    }


def declare_l0f_inputs(ctx):
    ctx.din("l0f_rg", [1, 128, D], F32)
    ctx.din("l0f_ru", [1, 128, D], F32)
    ctx.din("l0f_srg", [1, 128, D], F32)
    ctx.din("l0f_sru", [1, 128, D], F32)
    ctx.din("l0f_gate", [1, 128, 512], F32)
    ctx.din("l0f_sug", [1, 128, 128], F32)
    ctx.din("l0f_suu", [1, 128, 128], F32)
    ctx.din("l0f_sud", [1, 128, 128], F32)
    ctx.din("l0f_scg", [22, 128, 128], F32)
    ctx.din("l0f_scu", [22, 128, 128], F32)
    ctx.din("l0f_srd", [1, 128, 2816], F32)
    ctx.din("l0f_scd", [16, 128, 128], F32)
    ctx.din("l0f_rdt", [1, 128, 1408], F32)
    ctx.din("l0f_epack", [32, 128, 3200], F32)
    ctx.din("l0f_cdpack", [16, 2, 128, 2048], F32)


def build_l0f_program():
    ctx = Ctx()
    nc = ctx.nc
    ctx.din("x_in", [CC, 128, T], BF16)
    declare_l0f_inputs(ctx)
    ctx.dout("x_out", [CC, 128, T], BF16)

    b = alloc_shared(ctx)
    ctx.ident_sb = nc.alloc_sbuf_tensor("ident_sb", [128, 128], F32).ap()
    ident_dram = nc.inline_tensor(np.eye(128, dtype=np.float32), "ident_const")
    s_init = ctx.sem("init")

    with nc.Block() as blk:
        @blk.sync
        def _(sy):
            sy.dma_start(out=ctx.ident_sb, in_=ident_dram.ap()).then_inc(s_init.h, 16)
        s_init.n += 16

        @blk.vector
        def _(v):
            ins = v.memset(b.ones_bf[:], 1.0)
            s_init.inc(ins, 1)
            ins = v.memset(b.zb[:], 0.0)
            s_init.inc(ins, 1)

        for eng in [blk.tensor, blk.gpsimd, blk.vector, blk.scalar, blk.sync]:
            def emit(e):
                s_init.wait(e)
            eng(emit)

        emit_l0_ffn(ctx, blk, b, "x_in", "x_out")

        @blk.sync
        def _(sy):
            for sem in ctx.dma_sems:
                sy.wait_ge(sem.h, sem.n)
    return ctx



def emit_l1_attn(ctx, blk, b, x_in_name, x_out_name):
    x_in = ctx.dram[x_in_name]
    x_out = ctx.dram[x_out_name]
    sp_cv = ctx.dscratch("sp_cv", [16, 64, T], BF16)
    sp_row3 = ctx.dscratch("sp_row_l1a", [8, T], F32)
    sp_rows8 = ctx.dscratch("sp_rows8", [24, T], F32)
    qs_dram = ctx.dscratch("qs_dram", [16, T], F32)
    l_dram = ctx.dscratch("l_dram", [16, T], F32)

    s_mm = ctx.sem("s_mm")
    s_v = ctx.sem("s_v")
    s_s = ctx.sem("s_s")
    s_evV = ctx.sem("s_evV")
    s_evS = ctx.sem("s_evS")
    s_srg = [ctx.sem(f"s_srg{j}") for j in range(4)]
    s_srs = [ctx.sem(f"s_srs{j}") for j in range(4)]
    s_wr = [ctx.sem(f"s_wr{j}") for j in range(3)]
    s_t1 = [ctx.sem(f"s_t1{j}") for j in range(3)]
    s_bc = ctx.sem("s_bc")
    s_rsp = ctx.sem("s_rsp")
    s_cv = ctx.sem("s_cv")
    s_qb = [ctx.sem("s_qb0"), ctx.sem("s_qb1")]
    if not hasattr(ctx, "dma_sems"):
        ctx.dma_sems = [s_bc, s_rsp] + s_srg + s_srs + s_wr + s_t1
    for x in s_qb:
        if x not in ctx.dma_sems:
            ctx.dma_sems.append(x)

    def ch_bf(c):
        return b.slotB.ap()[:, c, 0:T]

    kaug_sb = ch_bf(0)
    v_sb = ch_bf(1)
    def qlslot(hh):
        return b.slotB.ap()[:, 2 + hh, 0:T]
    qh_sb = ch_bf(10)
    def pt_tile(sc):
        return b.slotB.ap()[:, 11 + sc // 4, (sc % 4) * TW:(sc % 4 + 1) * TW]

    rowsF = ctx.rowsF
    r_qq = rowsF.ap()[0:1, :]
    r_qs = rowsF.ap()[32:33, :]
    r_sc = rowsF.ap()[64:65, :]
    scal = ctx.scal_sb
    kcol_sb = ctx.kcol_sb
    qsb_f32 = b.tmp2.ap()[:, 1].bitcast(F32)

    def czv(p):
        return b.sring.ap()[:, 0, p * TW:(p + 1) * TW]

    tmp1f32 = b.tmp1.ap().rearrange("p a c -> p (a c)").bitcast(F32)
    w_max = tmp1f32[32:33, 0:T]
    w_min = b.aB.ap()[32:33, :]
    scr2 = tmp1f32[64:65, 0:T]
    k32 = b.bB.ap()[32:33, :]
    m64 = b.aB.ap()[64:65, :]
    s0 = b.aB.ap()[0:1, :]
    band1 = ctx.band1_sb
    band2 = ctx.band2_sb

    def stat_pass(src_ap):
        mm0 = s_mm.n
        sq0 = s_s.n
        ld_vals = []
        v_need, s_need = s_v.n, s_s.n

        @blk.gpsimd
        def _(g):
            g.wait_ge(s_v.h, v_need)
            g.wait_ge(s_s.h, s_need)
            g.wait_ge(s_mm.h, mm0)
            for ci in range(CC):
                if ci >= 4:
                    g.wait_ge(s_mm.h, mm0 + ci - 3)
                ins = g.dma_start(out=b.sring.ap()[:, ci % 4], in_=src_ap[ci])
                s_srg[ci % 4].inc(ins, 16)
                ld_vals.append(s_srg[ci % 4].n)

        @blk.scalar
        def _(s):
            for ci in range(CC):
                s.wait_ge(s_srg[ci % 4].h, ld_vals[ci])
                if ci >= 2:
                    s.wait_ge(s_mm.h, mm0 + ci - 1)
                zact(s, b, b.tmp2.ap()[:, ci % 2], b.sring.ap()[:, ci % 4], AF.Square).then_inc(s_s.h, 1)
                s_s.n += 1

        evV0, evS0 = s_evV.n, s_evS.n

        @blk.tensor
        def _(t):
            t.wait_ge(s_evV.h, evV0)
            t.wait_ge(s_evS.h, evS0)
            for ci in range(CC):
                t.wait_ge(s_s.h, sq0 + ci + 1)
                for tt in range(NT):
                    t.matmul(b.psA.ap()[0:1, tt], b.ones_bf[:],
                             b.sring.ap()[:, ci % 4, tt * TW:(tt + 1) * TW],
                             start=(ci == 0), stop=(ci == CC - 1))
                    ins = t.matmul(b.psB.ap()[0:1, tt], b.ones_bf[:],
                                   b.tmp2.ap()[:, ci % 2, tt * TW:(tt + 1) * TW],
                                   start=(ci == 0), stop=(ci == CC - 1))
                ins.then_inc(s_mm.h, 1)
        s_mm.n += CC

    def rows_from_stats():
        r0 = b.aB.ap()[0:1, :]
        r1 = b.bB.ap()[0:1, :]

        @blk.vector
        def _(v):
            s_mm.wait(v)
            s_bc.wait(v)
            for tt in range(NT):
                ins = v.tensor_scalar(r1[:, tt * TW:(tt + 1) * TW], b.psA.ap()[0:1, tt], 1.0 / D, None, ALU.mult)
            s_v.inc(ins, 1)

        @blk.scalar
        def _(s):
            s_v.wait(s)
            ins = zact(s, b, r0, r1, AF.Square)
            s_s.inc(ins, 1)

        @blk.vector
        def _(v):
            s_s.wait(v)
            for tt in range(NT):
                ins = v.tensor_scalar(r1[:, tt * TW:(tt + 1) * TW], b.psB.ap()[0:1, tt], 1.0 / D, None, ALU.mult)
            s_cv.inc(ins, 1)
            s_cv.wait(v)
            ins = v.tensor_tensor(r1, r1, r0, ALU.subtract)
            s_cv.inc(ins, 1)
            s_cv.wait(v)
            ins = v.tensor_scalar(r1, r1, 1.0, LN_EPS, ALU.mult, ALU.add)
            s_cv.inc(ins, 1)
            s_cv.wait(v)
            ins = v.reciprocal(r0, r1)
            s_v.inc(ins, 1)

        @blk.scalar
        def _(s):
            s_v.wait(s)
            ins = zact(s, b, r0, r0, AF.Sqrt)
            s_s.inc(ins, 1)

        @blk.vector
        def _(v):
            s_s.wait(v)
            for tt in range(NT):
                ins = v.tensor_scalar(r1[:, tt * TW:(tt + 1) * TW], b.psA.ap()[0:1, tt], -1.0 / D, None, ALU.mult)
            s_cv.inc(ins, 1)
            s_cv.wait(v)
            ins = v.tensor_tensor(r1, r1, r0, ALU.mult)
            s_evV.inc(ins, 1)

    def bcast_rows(slot_a, slot_b):
        row_done = s_evV.n

        @blk.scalar
        def _(s):
            s.wait_ge(s_evV.h, row_done)
            s.dma_start(out=sp_row3[slot_a:slot_a + 1], in_=b.aB.ap()[0:1, :]).then_inc(s_rsp.h, 16)
            s.dma_start(out=sp_row3[slot_b:slot_b + 1], in_=b.bB.ap()[0:1, :]).then_inc(s_rsp.h, 16)
        s_rsp.n += 32

        mm_need_bc = s_mm.n

        @blk.sync
        def _(sy):
            s_rsp.wait(sy)
            sy.wait_ge(s_mm.h, mm_need_bc)
            sy.dma_start(out=b.aB[:], in_=sp_row3[slot_a].unsqueeze(0).broadcast_to([128, T])).then_inc(s_bc.h, 16)
            sy.dma_start(out=b.bB[:], in_=sp_row3[slot_b].unsqueeze(0).broadcast_to([128, T])).then_inc(s_bc.h, 16)
        s_bc.n += 32

    def ln_apply_plain(src_dram, dst_slot):
        bc_need = s_bc.n
        mm_need = s_mm.n
        bld0 = s_v.n
        ss_need = s_s.n
        ld_vals = []

        @blk.sync
        def _(sy):
            sy.wait_ge(s_v.h, bld0)
            sy.wait_ge(s_s.h, ss_need)
            sy.wait_ge(s_mm.h, mm_need)
            for ci in range(CC):
                if ci >= 4:
                    sy.wait_ge(s_v.h, bld0 + ci - 3)
                ins = sy.dma_start(out=b.sring.ap()[:, ci % 4], in_=src_dram[ci])
                s_srs[ci % 4].inc(ins, 16)
                ld_vals.append(s_srs[ci % 4].n)

        @blk.vector
        def _(v):
            v.wait_ge(s_bc.h, bc_need)
            v.wait_ge(s_mm.h, mm_need)
            v.wait_ge(s_s.h, ss_need)
            for ci in range(CC):
                v.wait_ge(s_srs[ci % 4].h, ld_vals[ci])
                v.wait_ge(s_v.h, s_v.n)
                ins = v.tensor_tensor(b.tmp2.ap()[:, 0], b.sring.ap()[:, ci % 4], b.aB[:], ALU.mult)
                s_cv.inc(ins, 1)
                s_cv.wait(v)
                ins = v.tensor_tensor(dst_slot.ap()[:, ci, 1:1 + T], b.tmp2.ap()[:, 0], b.bB[:], ALU.add)
                s_v.inc(ins, 1)

    def n2_rhs(cc, tt):
        return b.slotA.ap()[:, cc, 1 + tt * TW:1 + (tt + 1) * TW]

    t1_entry = [s.n for s in s_t1]

    @blk.gpsimd
    def _(g):
        for sem, val in zip(s_t1, t1_entry):
            g.wait_ge(sem.h, val)

    @blk.sync
    def _(sy):
        for sem, val in zip(s_t1, t1_entry):
            sy.wait_ge(sem.h, val)

    stat_pass(x_in)
    rows_from_stats()
    bcast_rows(0, 1)
    ln_apply_plain(x_in, b.slotA)
    n2_done = s_v.n

    s_cz = ctx.sem("s_cz")
    if s_cz not in ctx.dma_sems:
        ctx.dma_sems.append(s_cz)
    ss_cz, mm_cz = s_s.n, s_mm.n

    @blk.sync
    def _(sy):
        sy.wait_ge(s_v.h, n2_done)
        sy.wait_ge(s_s.h, ss_cz)
        sy.wait_ge(s_mm.h, mm_cz)
        ins = sy.dma_start(out=b.sring.ap()[:, 0].rearrange("p (a c) -> p a c", c=TW),
                           in_=ctx.cz_dram.ap())
        s_cz.inc(ins, 16)
    cz_ready = s_cz.n

    def kv_pass(w_name, dst, gate=None):
        lw_vals = []
        mm0 = s_mm.n
        evV_need, evS_need = s_evV.n, s_evS.n
        w_ap = ctx.dram[w_name]

        @blk.gpsimd
        def _(g):
            g.wait_ge(s_mm.h, mm0)
            ins = g.dma_start(out=b.wring.ap()[:, 0, 0:1024], in_=w_ap[0])
            s_wr[0].inc(ins, 16)
        lwv = s_wr[0].n

        @blk.tensor
        def _(t):
            if gate is not None:
                for sem, val in gate:
                    t.wait_ge(sem.h, val)
            t.wait_ge(s_wr[0].h, lwv)
            t.wait_ge(s_evV.h, evV_need)
            t.wait_ge(s_evS.h, evS_need)
            wv = b.wring.ap()[:, 0, 0:1024].rearrange("p (cc o) -> p cc o", o=64)
            for cc in range(CC):
                for tt in range(NT):
                    ins = t.matmul(b.psA.ap()[0:64, tt], wv[:, cc], n2_rhs(cc, tt),
                                   start=(cc == 0), stop=(cc == CC - 1))
            ins.then_inc(s_mm.h, 1)
        s_mm.n += 1

        @blk.scalar
        def _(s):
            s.wait_ge(s_mm.h, s_mm.n)
            ins = zcopy(s, b, dst[0:64, :], psflat(b.psA)[0:64, :])
            s_evS.inc(ins, 1)

    kv_pass("l1a_wk", kaug_sb, gate=[(s_v, n2_done)])
    kv_pass("l1a_wv", v_sb)
    kv_done = s_evS.n

    @blk.gpsimd
    def _(g):
        g.wait_ge(s_evS.h, kv_done)
        ins = g.memset(kaug_sb[64:65, :], 1.0)
        ctx.sem("s_p").inc(ins, 1)
    s_p = ctx.sem("s_p")
    ones_row_done = s_p.n

    @blk.scalar
    def _(s):
        s.wait_ge(s_evS.h, kv_done)
        ins = zact(s, b, b.tmp2.ap()[0:64, 0], kaug_sb[0:64, :], AF.Square)
        s_s.inc(ins, 1)
    k2_done = s_s.n
    mmK0 = s_mm.n
    evS_K = s_evS.n

    @blk.tensor
    def _(t):
        t.wait_ge(s_s.h, k2_done)
        t.wait_ge(s_evS.h, evS_K)
        for tt in range(NT):
            t.matmul(b.psA.ap()[0:1, tt], b.ones_bf[0:64, :], kaug_sb[0:64, tt * TW:(tt + 1) * TW],
                     start=True, stop=True)
            ins = t.matmul(b.psB.ap()[0:1, tt], b.ones_bf[0:64, :],
                           b.tmp2.ap()[0:64, 0, tt * TW:(tt + 1) * TW], start=True, stop=True)
        ins.then_inc(s_mm.h, 1)
        for sc in range(CC):
            ins = t.matmul(b.psB.ap()[:, 2, sc:sc + 1], kaug_sb[0:64, sc * 128:(sc + 1) * 128],
                           b.ones_bf[0:64, :], start=True, stop=True)
        ins.then_inc(s_mm.h, 1)
    s_mm.n += 2

    r_ks = b.bB.ap()[0:1, :]

    @blk.vector
    def _(v):
        v.wait_ge(s_mm.h, s_mm.n)
        v.wait_ge(s_bc.h, s_bc.n)
        for tt in range(NT):
            ins = v.tensor_copy(r_sc[:, tt * TW:(tt + 1) * TW], b.psA.ap()[0:1, tt])
        s_cv.inc(ins, 1)
        s_cv.wait(v)
        ins = v.tensor_copy(r_ks, r_sc)
        s_cv.inc(ins, 1)
        s_cv.wait(v)
        ins = v.tensor_copy(k32, r_sc)
        s_cv.inc(ins, 1)
        s_cv.wait(v)
        ins = v.tensor_copy(kcol_sb.ap()[:, 0:CC], b.psB.ap()[:, 2, 0:CC])
        s_cv.inc(ins, 1)
        s_cv.wait(v)
        ins = v.reduce_max(scal.ap()[0:1, 0:1], r_sc, axis=mybir.AxisListType.X, apply_absolute_value=True)
        s_cv.inc(ins, 1)
        s_cv.wait(v)
        for tt in range(NT):
            ins = v.tensor_copy(r_qq[:, tt * TW:(tt + 1) * TW], b.psB.ap()[0:1, tt])
        s_cv.inc(ins, 1)
        s_cv.wait(v)
        ins = v.reduce_max(scal.ap()[0:1, 1:2], r_qq, axis=mybir.AxisListType.X)
        s_v.inc(ins, 1)

    @blk.scalar
    def _(s):
        s_v.wait(s)
        ins = zact(s, b, scal.ap()[0:1, 1:2], scal.ap()[0:1, 1:2], AF.Sqrt)
        s_evS.inc(ins, 1)
    scal_done = s_evS.n


    @blk.vector
    def _(v):
        v.wait_ge(s_evS.h, scal_done)
        ins = v.tensor_copy(w_max, r_ks)
        s_cv.inc(ins, 1)
        s_cv.wait(v)
        ins = v.tensor_copy(w_min, r_ks)
        s_cv.inc(ins, 1)
        s_cv.wait(v)
        win = 1
        for step in [1, 2, 4, 8, 16, 32, 64]:
            d = min(step, 65 - win)
            if d <= 0:
                break
            for (buf, op) in ((w_max, ALU.max), (w_min, ALU.min)):
                ins = v.tensor_tensor(scr2[:, d:T], buf[:, d:T], buf[:, 0:T - d], op)
                s_cv.inc(ins, 1)
                s_cv.wait(v)
                ins = v.tensor_copy(buf[:, d:T], scr2[:, d:T])
                s_cv.inc(ins, 1)
                s_cv.wait(v)
            win += d
        ins = v.nop()
        s_v.inc(ins, 1)
    wdone = s_v.n


    stk_qq = rowsF.ap()[0:8, :]
    stk_qs = rowsF.ap()[32:40, :]
    stk_dg = b.aB.ap()[0:8, :]
    wmax8 = b.aB.ap()[32:40, :]
    wmin8 = b.bB.ap()[32:40, :]
    ks8 = tmp1f32[32:40, 0:T]
    sc1 = b.bB.ap()[0:8, :]
    sc2 = tmp1f32[0:8, 0:T]
    stq_row = rowsF.ap()[96:97, :]
    sts_row = b.aB.ap()[96:97, :]
    std_row = b.bB.ap()[96:97, :]
    r_l = b.bB.ap()[64:65, :]
    scal8 = ctx.scal8_sb
    vup2 = ctx.vup2_sb
    vtok_sb = ch_bf(15)
    wq = ctx.dram["l1a_wq"]
    q2l = ctx.dram["l1a_q2l"]
    s_p = ctx.sem("s_p")

    @blk.scalar
    def _(s, wdone=wdone):
        s.wait_ge(s_v.h, wdone)
        s.dma_start(out=sp_row3[2:3, :], in_=w_max).then_inc(s_rsp.h, 16)
        s.dma_start(out=sp_row3[3:4, :], in_=w_min).then_inc(s_rsp.h, 16)
        s.dma_start(out=sp_row3[4:5, :], in_=r_ks).then_inc(s_rsp.h, 16)
        s.dma_start(out=sp_row3[5:6, 0:16], in_=scal.ap()[0:1, :]).then_inc(s_rsp.h, 16)
    s_rsp.n += 64
    wrows_sp = s_rsp.n

    @blk.sync
    def _(sy, wrows_sp=wrows_sp, n2_done=n2_done):
        sy.wait_ge(s_rsp.h, wrows_sp)
        sy.wait_ge(s_v.h, n2_done)
        sy.dma_start(out=wmax8, in_=sp_row3[2].unsqueeze(0).broadcast_to([8, T])).then_inc(s_bc.h, 16)
        sy.dma_start(out=wmin8, in_=sp_row3[3].unsqueeze(0).broadcast_to([8, T])).then_inc(s_bc.h, 16)
        sy.dma_start(out=ks8, in_=sp_row3[4].unsqueeze(0).broadcast_to([8, T])).then_inc(s_bc.h, 16)
        sy.dma_start(out=scal8.ap()[:, 0:4],
                     in_=sp_row3[5, 0:4].unsqueeze(0).broadcast_to([128, 4])).then_inc(s_bc.h, 16)
    s_bc.n += 64
    w8_ready = s_bc.n


    mmV0 = s_mm.n
    evV_V0 = s_evV.n

    @blk.tensor
    def _(t, mmV0=mmV0, evV_V0=evV_V0):
        t.wait_ge(s_evS.h, scal_done)
        t.wait_ge(s_evV.h, evV_V0)
        for sc in range(CC):
            if sc >= 2:
                t.wait_ge(s_evV.h, evV_V0 + sc - 1)
            ins = t.matmul(b.psA.ap()[:, sc % 2, 0:64], v_sb[0:64, sc * 128:(sc + 1) * 128],
                           ctx.ident64_sb.ap()[0:64, 0:64], start=True, stop=True)
            ins.then_inc(s_mm.h, 1)
    s_mm.n += CC

    @blk.vector
    def _(v, mmV0=mmV0):
        for sc in range(CC):
            v.wait_ge(s_mm.h, mmV0 + sc + 1)
            ins = v.tensor_copy(vtok_sb[:, sc * 64:(sc + 1) * 64], b.psA.ap()[:, sc % 2, 0:64])
            s_evV.inc(ins, 1)
    vtok_done = s_evV.n

    def qs512(slot):
        return b.sring.ap()[:, 2 + slot].bitcast(F32)[:, 0:TW]

    bsc_scr = b.sring.ap()[:, 1].bitcast(F32)[:, 0:128]
    band_done = {0: s_v.n, 1: s_v.n}
    pt_free = [s_mm.n] * 4
    cv_bank_free = [s_evV.n] * 4
    exp_last = (s_evS.n, s_evS.n)
    l_sp_last = s_rsp.n
    qs_sp_marks = []

    for g in range(2):
        entry_mm = s_mm.n
        entry_evS = s_evS.n
        entry_evV = s_evV.n
        entry_v = s_v.n
        qh_marks = []
        ql_marks = []
        rows_marks = []
        wq_pass_mark = []
        q2l_mark = []
        row_mm_mark = []
        for hh in range(8):
            h = g * 8 + hh

            @blk.gpsimd
            def _(gq, h=h, hh=hh, entry_mm=entry_mm):
                if hh >= 3:
                    gq.wait_ge(s_mm.h, wq_pass_mark[hh - 3])
                else:
                    gq.wait_ge(s_mm.h, entry_mm)
                ins = gq.dma_start(out=b.wring.ap()[:, hh % 3], in_=wq[h])
                s_wr[hh % 3].inc(ins, 16)
                if hh >= 2:
                    gq.wait_ge(s_mm.h, q2l_mark[hh - 2])
                ins = gq.dma_start(out=ctx.q2l_sb.ap()[:, (hh % 2) * 64:(hh % 2) * 64 + 64], in_=q2l[h])
                s_wr[hh % 3].inc(ins, 16)
            lw_val = s_wr[hh % 3].n

            evV_need = qh_marks[hh - 1] if hh >= 1 else (vtok_done if g == 0 else entry_evV)

            @blk.tensor
            def _(t, hh=hh, g=g, lw_val=lw_val, evV_need=evV_need,
                  entry_evS=entry_evS):
                t.wait_ge(s_wr[hh % 3].h, lw_val)
                t.wait_ge(s_evV.h, evV_need)
                if hh == 0:
                    t.wait_ge(s_v.h, n2_done)
                wv = b.wring.ap()[:, hh % 3].rearrange("p (cc o) -> p cc o", o=128)
                for cc in range(CC):
                    for tt in range(NT):
                        ins = t.matmul(b.psA.ap()[:, tt], wv[:, cc], n2_rhs(cc, tt),
                                       start=(cc == 0), stop=(cc == CC - 1))
                ins.then_inc(s_mm.h, 1)
            s_mm.n += 1
            wq_pass_mark.append(s_mm.n)

            @blk.vector
            def _(v, mmv=s_mm.n):
                v.wait_ge(s_mm.h, mmv)
                ins = v.tensor_copy(qh_sb, psflat(b.psA))
                s_evV.inc(ins, 1)
            qh_marks.append(s_evV.n)

            rows_prev = rows_marks[hh - 1] if hh >= 1 else None

            @blk.tensor
            def _(t, hh=hh, g=g, qh_val=qh_marks[-1], lw_val=lw_val, rows_prev=rows_prev,
                  entry_evS=entry_evS, entry_v=entry_v):
                t.wait_ge(s_evV.h, qh_val)
                t.wait_ge(s_wr[hh % 3].h, lw_val)
                if rows_prev is not None:
                    t.wait_ge(s_v.h, rows_prev)
                elif g == 0:
                    t.wait_ge(s_evS.h, scal_done)
                else:
                    t.wait_ge(s_evS.h, entry_evS)
                    t.wait_ge(s_v.h, entry_v)
                for tt in range(NT):
                    ins = t.matmul(b.psB.ap()[0:64, tt],
                                   ctx.q2l_sb.ap()[:, (hh % 2) * 64:(hh % 2) * 64 + 64],
                                   qh_sb[:, tt * TW:(tt + 1) * TW], start=True, stop=True)
                ins.then_inc(s_mm.h, 1)
            s_mm.n += 1
            q2l_mark.append(s_mm.n)

            @blk.vector
            def _(v, hh=hh, mmv=s_mm.n):
                v.wait_ge(s_mm.h, mmv)
                ins = v.tensor_scalar(qlslot(hh)[0:64, :], psflat(b.psB)[0:64, :], 0.125, None, ALU.mult)
                s_evV.inc(ins, 1)
            ql_marks.append(s_evV.n)

            mm_prev_rows = row_mm_mark[hh - 1] if hh >= 1 else None

            @blk.scalar
            def _(s, hh=hh, ql_val=ql_marks[-1], mm_prev_rows=mm_prev_rows):
                s.wait_ge(s_evV.h, ql_val)
                if mm_prev_rows is not None:
                    s.wait_ge(s_mm.h, mm_prev_rows)
                ins = zact(s, b, b.tmp2.ap()[0:64, 0], qlslot(hh)[0:64, :], AF.Square)
                s_s.inc(ins, 1)
            sq_mark = s_s.n

            @blk.tensor
            def _(t, hh=hh, sqv=sq_mark, ql_val=ql_marks[-1]):
                t.wait_ge(s_s.h, sqv)
                t.wait_ge(s_evV.h, ql_val)
                for tt in range(NT):
                    t.matmul(b.psB.ap()[0:1, tt], b.ones_bf[0:64, :],
                             b.tmp2.ap()[0:64, 0, tt * TW:(tt + 1) * TW], start=True, stop=True)
                    ins = t.matmul(b.psB.ap()[32:33, tt], b.ones_bf[0:64, :],
                                   qlslot(hh)[0:64, tt * TW:(tt + 1) * TW], start=True, stop=True)
                ins.then_inc(s_mm.h, 1)
            s_mm.n += 1
            qqqs_mark = s_mm.n

            @blk.vector
            def _(v, hh=hh, qqqs_mark=qqqs_mark):
                v.wait_ge(s_mm.h, qqqs_mark)
                ins = v.tensor_tensor(b.tmp2.ap()[0:64, 0], qlslot(hh)[0:64, :],
                                      kaug_sb[0:64, :], ALU.mult)
                s_v.inc(ins, 1)
            qk_mark = s_v.n

            @blk.tensor
            def _(t, qkv=qk_mark):
                t.wait_ge(s_v.h, qkv)
                for tt in range(NT):
                    ins = t.matmul(b.psB.ap()[64:65, tt], b.ones_bf[0:64, :],
                                   b.tmp2.ap()[0:64, 0, tt * TW:(tt + 1) * TW], start=True, stop=True)
                ins.then_inc(s_mm.h, 1)
            s_mm.n += 1
            row_mm_mark.append(s_mm.n)

            stage_gate = s_rsp.n

            @blk.vector
            def _(v, mmv=s_mm.n, stage_gate=stage_gate):
                v.wait_ge(s_mm.h, mmv)
                v.wait_ge(s_rsp.h, stage_gate)
                ins = v.tensor_copy(stq_row, psflat(b.psB)[0:1, :])
                s_cv.inc(ins, 1)
                s_cv.wait(v)
                ins = v.tensor_copy(sts_row, psflat(b.psB)[32:33, :])
                s_cv.inc(ins, 1)
                s_cv.wait(v)
                ins = v.tensor_copy(std_row, psflat(b.psB)[64:65, :])
                s_v.inc(ins, 1)
            rows_marks.append(s_v.n)

            @blk.sync
            def _(sy, hh=hh, vmark=rows_marks[-1], srg0_gate=s_srg[0].n):
                sy.wait_ge(s_v.h, vmark)
                sy.wait_ge(s_srg[0].h, srg0_gate)
                sy.dma_start(out=sp_rows8[hh:hh + 1, :], in_=stq_row).then_inc(s_rsp.h, 16)
                sy.dma_start(out=sp_rows8[8 + hh:9 + hh, :], in_=sts_row).then_inc(s_rsp.h, 16)
                sy.dma_start(out=sp_rows8[16 + hh:17 + hh, :], in_=std_row).then_inc(s_rsp.h, 16)
            s_rsp.n += 48

        all_spills = s_rsp.n
        vchain_gate = s_v.n

        @blk.sync
        def _(sy, all_spills=all_spills, vchain_gate=vchain_gate, g=g,
              qs_sp_marks=list(qs_sp_marks)):
            sy.wait_ge(s_rsp.h, all_spills)
            sy.wait_ge(s_v.h, vchain_gate)
            if g == 1:
                sy.wait_ge(s_rsp.h, qs_sp_marks[-1])
            sy.dma_start(out=stk_qq, in_=sp_rows8[0:8, :]).then_inc(s_rsp.h, 16)
            sy.dma_start(out=stk_qs, in_=sp_rows8[8:16, :]).then_inc(s_rsp.h, 16)
            sy.dma_start(out=stk_dg, in_=sp_rows8[16:24, :]).then_inc(s_rsp.h, 16)
        s_rsp.n += 48
        stacks_ready = s_rsp.n

        @blk.scalar
        def _(s, stacks_ready=stacks_ready, w8=w8_ready):
            s.wait_ge(s_rsp.h, stacks_ready)
            s.wait_ge(s_bc.h, w8)
            ins = zact(s, b, stk_qq, stk_qq, AF.Sqrt)
            s_s.inc(ins, 1)
        sqrt_mark = s_s.n

        @blk.vector
        def _(v, sqrt_mark=sqrt_mark, lsp=l_sp_last):
            v.wait_ge(s_s.h, sqrt_mark)
            v.wait_ge(s_rsp.h, lsp)
            ins = v.tensor_scalar(stk_qq, stk_qq, scal8.ap()[0:8, 1:2], None, ALU.mult)
            s_cv.inc(ins, 1)
            s_cv.wait(v)
            ins = v.tensor_tensor(sc1, stk_qs, wmax8, ALU.mult)
            s_cv.inc(ins, 1)
            s_cv.wait(v)
            ins = v.tensor_tensor(sc2, stk_qs, wmin8, ALU.mult)
            s_cv.inc(ins, 1)
            s_cv.wait(v)
            ins = v.tensor_tensor(sc1, sc1, sc2, ALU.max)
            s_cv.inc(ins, 1)
            s_cv.wait(v)
            ins = v.tensor_scalar(sc1, sc1, 0.5, None, ALU.mult)
            s_cv.inc(ins, 1)
            s_cv.wait(v)
            ins = v.scalar_tensor_tensor(sc2, stk_qs, 0.5, ks8, ALU.mult, ALU.mult)
            s_cv.inc(ins, 1)
            s_cv.wait(v)
            ins = v.tensor_tensor(stk_dg, stk_dg, sc2, ALU.add)
            s_cv.inc(ins, 1)
            s_cv.wait(v)
            ins = v.tensor_tensor(sc2, sc1, stk_qq, ALU.subtract)
            s_cv.inc(ins, 1)
            s_cv.wait(v)
            ins = v.tensor_scalar(sc1, sc1, 1.0, 0.0, ALU.mult, ALU.max)
            s_cv.inc(ins, 1)
            s_cv.wait(v)
            ins = v.tensor_tensor(sc1, sc1, stk_qq, ALU.add)
            s_cv.inc(ins, 1)
            s_cv.wait(v)
            ins = v.tensor_tensor(stk_dg, stk_dg, sc2, ALU.max)
            s_cv.inc(ins, 1)
            s_cv.wait(v)
            ins = v.tensor_scalar(stk_dg, stk_dg, 1.0, 80.0, ALU.mult, ALU.add)
            s_cv.inc(ins, 1)
            s_cv.wait(v)
            ins = v.tensor_tensor(stk_qq, sc1, stk_dg, ALU.min)
            s_v.inc(ins, 1)
        mhat_rows = s_v.n

        mm_now = s_mm.n

        @blk.vector
        def _(v, mhat_rows=mhat_rows):
            v.wait_ge(s_v.h, mhat_rows)
            ins = v.tensor_scalar(sc1, stk_qq, -1.0, None, ALU.mult)
            s_v.inc(ins, 1)
        neg_mark = s_v.n

        @blk.scalar
        def _(s, g=g, neg_mark=neg_mark):
            s.wait_ge(s_v.h, neg_mark)
            s.dma_start(out=sp_rows8[0:8, :], in_=sc1).then_inc(s_rsp.h, 16)
            s.dma_start(out=qs_dram[8 * g:8 * g + 8, :], in_=stk_qs).then_inc(s_rsp.h, 16)
        s_rsp.n += 32
        qs_sp_marks.append(s_rsp.n)
        qs_spilled = s_rsp.n
        negsp = s_rsp.n

        @blk.gpsimd
        def _(gq, negsp=negsp, mm_now=mm_now):
            gq.wait_ge(s_rsp.h, negsp)
            gq.wait_ge(s_mm.h, mm_now)
            for hh in range(8):
                ins = gq.dma_start(out=qlslot(hh)[64:65, :], in_=sp_rows8[hh:hh + 1, :])
                s_srg[0].inc(ins, 16)
        mhat_v = s_srg[0].n

        for hh in range(8):
            h = g * 8 + hh
            for ap_ in range(4):
                nbat = ap_ + 1
                nsc = 4 * nbat
                cv_entry = cv_bank_free[ap_]
                qslot = ap_ % 2
                bgate = band_done[qslot]

                @blk.sync
                def _(sy, h=h, ap_=ap_, qslot=qslot, qs_spilled=qs_spilled, bgate=bgate):
                    sy.wait_ge(s_rsp.h, qs_spilled)
                    sy.wait_ge(s_v.h, bgate)
                    ins = sy.dma_start(out=qs512(qslot),
                                       in_=qs_dram[h, ap_ * TW:(ap_ + 1) * TW]
                                       .unsqueeze(0).broadcast_to([128, TW]))
                    s_qb[qslot].inc(ins, 16)
                qv = s_qb[qslot].n

                for bi in range(nbat):
                    exp_need = exp_last
                    ptf = pt_free[bi]
                    tile_mm = []
                    units = []
                    for j in range(4):
                        sc = 4 * bi + j

                        @blk.tensor
                        def _(t, sc=sc, hh=hh, ap_=ap_, j=j, exp_need=exp_need, mhat_v=mhat_v):
                            if j == 0:
                                t.wait_ge(s_srg[0].h, mhat_v)
                                t.wait_ge(s_p.h, ones_row_done)
                                t.wait_ge(s_evS.h, exp_need[0])
                            elif j == 2:
                                t.wait_ge(s_evS.h, exp_need[1])
                            ins = t.matmul(b.psB.ap()[:, sc % 4],
                                           kaug_sb[0:65, sc * 128:(sc + 1) * 128],
                                           qlslot(hh)[0:65, ap_ * TW:(ap_ + 1) * TW],
                                           start=True, stop=True)
                            ins.then_inc(s_mm.h, 1)
                        s_mm.n += 1
                        tile_mm.append(s_mm.n)

                        if sc >= 4 * ap_ - 1:
                            if 4 * ap_ <= sc <= 4 * ap_ + 3:
                                units.append((j, sc, band2, sc))
                            if 4 * ap_ <= sc + 1 <= 4 * ap_ + 3:
                                units.append((j, sc + 1, band1, sc))

                    v_bands = []
                    for half_u in range(2):
                        hu = [u for u in units if u[0] // 2 == half_u]
                        if hu:
                            @blk.vector
                            def _(v, units=list(hu), tile_mm=list(tile_mm), qv=qv,
                                  ap_=ap_, qslot=qslot):
                                v.wait_ge(s_qb[qslot].h, qv)
                                v.wait_ge(s_v.h, s_v.n)
                                for (j, a_q, mask_t, kidx) in units:
                                    v.wait_ge(s_mm.h, tile_mm[j])
                                    colw = (a_q - 4 * ap_) * 128
                                    ins = v.tensor_tensor(bsc_scr, qs512(qslot)[:, colw:colw + 128],
                                                          mask_t.ap(), ALU.mult)
                                    s_cv.inc(ins, 1)
                                    s_cv.wait(v)
                                    ins = v.scalar_tensor_tensor(
                                        b.psB.ap()[:, j, colw:colw + 128], bsc_scr,
                                        kcol_sb.ap()[:, kidx:kidx + 1],
                                        b.psB.ap()[:, j, colw:colw + 128], ALU.mult, ALU.add)
                                    s_cv.inc(ins, 1)
                                    s_cv.wait(v)
                                ins = v.nop()
                                s_v.inc(ins, 1)
                            band_done[qslot] = s_v.n
                        v_bands.append(s_v.n)

                    @blk.scalar
                    def _(s, bi=bi, v_bands=list(v_bands), m01=tile_mm[1], m23=tile_mm[3], ptf=ptf):
                        s.wait_ge(s_v.h, v_bands[0])
                        s.wait_ge(s_mm.h, ptf)
                        s.wait_ge(s_mm.h, m01)
                        ins = zact(s, b, b.slotB.ap()[:, 11 + bi, 0:1024],
                                   psflat(b.psB)[:, 0:1024], AF.Exp)
                        s_evS.inc(ins, 1)
                        s.wait_ge(s_v.h, v_bands[1])
                        s.wait_ge(s_mm.h, m23)
                        ins = zact(s, b, b.slotB.ap()[:, 11 + bi, 1024:2048],
                                   psflat(b.psB)[:, 1024:2048], AF.Exp)
                        s_evS.inc(ins, 1)
                    exp_last = (s_evS.n - 1, s_evS.n)
                    exp_batch = s_evS.n

                    cz_mark = None
                    if bi == ap_:
                        @blk.vector
                        def _(v, ap_=ap_, exp_batch=exp_batch):
                            v.wait_ge(s_evS.h, exp_batch)
                            v.wait_ge(s_cz.h, cz_ready)
                            for p in range(4):
                                v.wait_ge(s_v.h, s_v.n)
                                ins = v.tensor_tensor(pt_tile(4 * ap_ + p), pt_tile(4 * ap_ + p),
                                                      czv(p), ALU.mult)
                                s_v.inc(ins, 1)
                        cz_mark = s_v.n

                    @blk.tensor
                    def _(t, bi=bi, ap_=ap_, nsc=nsc, exp_batch=exp_batch, cz_mark=cz_mark,
                          cv_entry=cv_entry):
                        t.wait_ge(s_evS.h, exp_batch)
                        if cz_mark is not None:
                            t.wait_ge(s_v.h, cz_mark)
                        if bi == 0:
                            t.wait_ge(s_evV.h, cv_entry)
                        for j in range(4):
                            sc = 4 * bi + j
                            t.matmul(b.psA.ap()[0:64, ap_], vtok_sb[:, sc * 64:(sc + 1) * 64],
                                     pt_tile(sc), start=(sc == 0), stop=(sc == nsc - 1))
                        for j in range(4):
                            sc = 4 * bi + j
                            ins = t.matmul(b.psA.ap()[64:65, ap_], b.ones_bf[:],
                                           pt_tile(sc), start=(sc == 0), stop=(sc == nsc - 1))
                        ins.then_inc(s_mm.h, 1)
                    s_mm.n += 1
                    pt_free[bi] = s_mm.n
                cv_done = s_mm.n

                t12 = s_t1[2].n
                lsp = l_sp_last

                @blk.vector
                def _(v, ap_=ap_, cv_done=cv_done, t12=t12, lsp=lsp):
                    v.wait_ge(s_mm.h, cv_done)
                    v.wait_ge(s_t1[2].h, t12)
                    if ap_ == 0:
                        v.wait_ge(s_rsp.h, lsp)
                    ins = v.tensor_copy(b.tmp1.ap()[0:64, 2, (ap_ % 4) * TW:(ap_ % 4 + 1) * TW],
                                        b.psA.ap()[0:64, ap_])
                    s_cv.inc(ins, 1)
                    s_cv.wait(v)
                    ins = v.tensor_copy(r_l[:, ap_ * TW:(ap_ + 1) * TW], b.psA.ap()[64:65, ap_])
                    s_evV.inc(ins, 1)
                cv_bank_free[ap_] = s_evV.n

                @blk.scalar
                def _(s, h=h, ap_=ap_, evv=s_evV.n):
                    s.wait_ge(s_evV.h, evv)
                    ins = s.dma_start(out=sp_cv[h][:, ap_ * TW:(ap_ + 1) * TW],
                                      in_=b.tmp1.ap()[0:64, 2, (ap_ % 4) * TW:(ap_ % 4 + 1) * TW])
                    s_t1[2].inc(ins, 16)

            @blk.scalar
            def _(s, h=h, evv=s_evV.n):
                s.wait_ge(s_evV.h, evv)
                s.dma_start(out=l_dram[h:h + 1, :], in_=r_l).then_inc(s_rsp.h, 16)
            s_rsp.n += 16
            l_sp_last = s_rsp.n

    attn_mm = s_mm.n
    cv_spilled = s_t1[2].n
    l_all = s_rsp.n
    evS_fin = s_evS.n
    vW0 = s_v.n
    rc_vals = []
    lb_vals = []

    @blk.sync
    def _(sy, attn_mm=attn_mm, cv_spilled=cv_spilled, l_all=l_all, vW0=vW0):
        sy.wait_ge(s_t1[2].h, cv_spilled)
        sy.wait_ge(s_rsp.h, l_all)
        sy.wait_ge(s_mm.h, attn_mm)
        sy.wait_ge(s_v.h, vW0)
        for hp in range(8):
            ins = sy.dma_start(out=b.slotB.ap()[0:64, hp, 0:T], in_=sp_cv[2 * hp])
            s_srs[hp % 4].inc(ins, 16)
            ins = sy.dma_start(out=b.slotB.ap()[64:128, hp, 0:T], in_=sp_cv[2 * hp + 1])
            s_srs[hp % 4].inc(ins, 16)
            rc_vals.append(s_srs[hp % 4].n)
            if hp >= 1:
                sy.wait_ge(s_v.h, vW0 + 2 * (hp - 1) + 1)
            ins = sy.dma_start(out=b.aB.ap()[0:64, :],
                               in_=l_dram[2 * hp].unsqueeze(0).broadcast_to([64, T]))
            s_bc.inc(ins, 16)
            ins = sy.dma_start(out=b.aB.ap()[64:128, :],
                               in_=l_dram[2 * hp + 1].unsqueeze(0).broadcast_to([64, T]))
            s_bc.inc(ins, 16)
            lb_vals.append(s_bc.n)

    @blk.vector
    def _(v, attn_mm=attn_mm):
        v.wait_ge(s_mm.h, attn_mm)
        for hp in range(8):
            v.wait_ge(s_srs[hp % 4].h, rc_vals[hp])
            v.wait_ge(s_bc.h, lb_vals[hp])
            v.wait_ge(s_v.h, s_v.n)
            ins = v.reciprocal(b.bB[:], b.aB[:])
            s_v.inc(ins, 1)
            v.wait_ge(s_v.h, s_v.n)
            ins = v.tensor_tensor(b.slotB.ap()[:, hp, 0:T], b.slotB.ap()[:, hp, 0:T],
                                  b.bB[:], ALU.mult)
            s_v.inc(ins, 1)
    cvn_done = s_v.n

    mmU0 = s_mm.n
    evV_U0 = s_evV.n
    vup_vals = []

    @blk.gpsimd
    def _(g, mmU0=mmU0):
        for hd in range(16):
            if hd >= 2:
                g.wait_ge(s_mm.h, mmU0 + hd - 1)
            else:
                g.wait_ge(s_mm.h, mmU0)
            ins = g.dma_start(out=vup2.ap()[(hd % 2) * 64:(hd % 2) * 64 + 64, :],
                              in_=ctx.dram["l1a_vup"][hd])
            s_wr[2].inc(ins, 16)
            vup_vals.append(s_wr[2].n)

    @blk.tensor
    def _(t, vW0=vW0, evV_U0=evV_U0, evS_fin=evS_fin):
        t.wait_ge(s_evV.h, evV_U0)
        t.wait_ge(s_evS.h, evS_fin)
        for hd in range(16):
            hp, par = hd // 2, hd % 2
            t.wait_ge(s_wr[2].h, vup_vals[hd])
            t.wait_ge(s_v.h, vW0 + 2 * hp + 2)
            if hd >= 2:
                t.wait_ge(s_evV.h, evV_U0 + hd - 1)
            ps = b.psA if hd % 2 == 0 else b.psB
            for tt in range(NT):
                ins = t.matmul(ps.ap()[:, tt], vup2.ap()[par * 64:(par + 1) * 64, :],
                               b.slotB.ap()[par * 64:(par + 1) * 64, hp, tt * TW:(tt + 1) * TW],
                               start=True, stop=True)
            ins.then_inc(s_mm.h, 1)
    s_mm.n += 16

    @blk.vector
    def _(v, mmU0=mmU0):
        for hd in range(16):
            v.wait_ge(s_mm.h, mmU0 + hd + 1)
            ps = b.psA if hd % 2 == 0 else b.psB
            ins = v.tensor_copy(b.slotA.ap()[:, hd, 0:T], psflat(ps))
            s_evV.inc(ins, 1)
    ctx_done0 = s_evV.n - 16

    wo = ctx.dram["l1a_wo"]
    mmWo0 = s_mm.n
    lwWo_vals = []

    @blk.gpsimd
    def _(g, mmWo0=mmWo0):
        g.wait_ge(s_mm.h, mmWo0)
        for oc in range(CC):
            if oc >= 3:
                g.wait_ge(s_mm.h, mmWo0 + oc - 2)
            ins = g.dma_start(out=b.wring.ap()[:, oc % 3], in_=wo[oc])
            s_wr[oc % 3].inc(ins, 16)
            lwWo_vals.append(s_wr[oc % 3].n)

    evV_Wo = s_evV.n

    @blk.tensor
    def _(t, evV_Wo=evV_Wo):
        t.wait_ge(s_evV.h, evV_Wo)
        for oc in range(CC):
            ps = b.psA if oc % 2 == 0 else b.psB
            t.wait_ge(s_wr[oc % 3].h, lwWo_vals[oc])
            if oc >= 2:
                t.wait_ge(s_evV.h, evV_Wo + oc - 1)
            wv = b.wring.ap()[:, oc % 3].rearrange("p (cc o) -> p cc o", o=128)
            for cc in range(CC):
                for tt in range(NT):
                    ins = t.matmul(ps.ap()[:, tt], wv[:, cc],
                                   b.slotA.ap()[:, cc, tt * TW:(tt + 1) * TW],
                                   start=(cc == 0), stop=(cc == CC - 1))
            ins.then_inc(s_mm.h, 1)
    s_mm.n += CC

    evV_fin0 = s_evV.n
    vfin0 = s_v.n
    xin_vals = []

    @blk.sync
    def _(sy, vfin0=vfin0, evV_fin0=evV_fin0):
        sy.wait_ge(s_v.h, vfin0)
        for oc in range(CC):
            if oc >= 2:
                sy.wait_ge(s_evV.h, evV_fin0 + oc - 1)
            ins = sy.dma_start(out=b.tmp2.ap()[:, oc % 2], in_=x_in[oc])
            s_qb[oc % 2].inc(ins, 16)
            xin_vals.append(s_qb[oc % 2].n)

    for oc in range(CC):
        t1v = s_t1[oc % 3].n

        @blk.vector
        def _(v, oc=oc, t1v=t1v, mmWo0=mmWo0):
            v.wait_ge(s_mm.h, mmWo0 + oc + 1)
            v.wait_ge(s_qb[oc % 2].h, xin_vals[oc])
            v.wait_ge(s_t1[oc % 3].h, t1v)
            ps = b.psA if oc % 2 == 0 else b.psB
            ins = v.tensor_tensor(b.tmp1.ap()[:, oc % 3], b.tmp2.ap()[:, oc % 2], psflat(ps), ALU.add)
            s_evV.inc(ins, 1)

        @blk.scalar
        def _(s, oc=oc):
            s.wait_ge(s_evV.h, s_evV.n)
            ins = s.dma_start(out=x_out[oc], in_=b.tmp1.ap()[:, oc % 3])
            s_t1[oc % 3].inc(ins, 16)


def host_pack_l1a(inp):
    wk = np.asarray(inp["L1_attn_Wk"], np.float32)
    wv = np.asarray(inp["L1_attn_Wv"], np.float32)
    kpack = np.ascontiguousarray(wk.reshape(64, 16, 128).transpose(2, 1, 0)).reshape(1, 128, 1024)
    vpack = np.ascontiguousarray(wv.reshape(64, 16, 128).transpose(2, 1, 0)).reshape(1, 128, 1024)
    return {
        "l1a_wq": pack_wT(inp["L1_attn_Wq"]),
        "l1a_wk": kpack,
        "l1a_wv": vpack,
        "l1a_q2l": np.ascontiguousarray(np.asarray(inp["L1_attn_q_to_latent"], np.float32)),
        "l1a_vup": np.ascontiguousarray(np.asarray(inp["L1_attn_v_up"], np.float32)),
        "l1a_wo": pack_wT(inp["L1_attn_Wo"]),
    }


def declare_l1a_inputs(ctx):
    ctx.din("l1a_wq", [CC, 128, D], F32)
    ctx.din("l1a_wk", [1, 128, 1024], F32)
    ctx.din("l1a_wv", [1, 128, 1024], F32)
    ctx.din("l1a_q2l", [16, 128, 64], F32)
    ctx.din("l1a_vup", [16, 64, 128], F32)
    ctx.din("l1a_wo", [CC, 128, D], F32)


def alloc_l1a(ctx):
    nc = ctx.nc
    ctx.rowsF = nc.alloc_sbuf_tensor("rowsF", [128, T], F32)
    ctx.scal_sb = nc.alloc_sbuf_tensor("scal_sb", [1, 16], F32)
    ctx.scal8_sb = nc.alloc_sbuf_tensor("scal8_sb", [128, 4], F32)
    ctx.kcol_sb = nc.alloc_sbuf_tensor("kcol_sb", [128, 16], F32)
    ctx.q2l_sb = nc.alloc_sbuf_tensor("q2l_sb", [128, 128], BF16)
    ctx.vup2_sb = nc.alloc_sbuf_tensor("vup2_sb", [128, 128], BF16)
    ctx.wband_sb = None
    i = np.arange(128)[:, None]
    tt_ = np.arange(TW)[None, :]
    czs = np.stack([(i <= tt_ - 128 * p).astype(np.float32) for p in range(4)], axis=1)
    cz_d = nc.inline_tensor(czs.astype(BF), "cz_const")
    t2 = np.arange(128)[None, :]
    b1 = np.where((i - t2 >= 64) & (i - t2 <= 127), 0.5, 0.0).astype(np.float32)
    b2 = np.where((t2 - i >= 0) & (t2 - i <= 64), 0.5, 0.0).astype(np.float32)
    b1_d = nc.inline_tensor(b1.astype(BF), "band1_const")
    b2_d = nc.inline_tensor(b2.astype(BF), "band2_const")
    ctx.cz_dram = cz_d
    ctx.ident64_sb = nc.alloc_sbuf_tensor("ident64_sb", [64, 64], BF16)
    ctx.ident64_d = nc.inline_tensor(np.eye(64, dtype=BF), "ident64_const")
    ctx.band1_sb = nc.alloc_sbuf_tensor("band1_sb", [128, 128], BF16)
    ctx.band2_sb = nc.alloc_sbuf_tensor("band2_sb", [128, 128], BF16)
    return cz_d, b1_d, b2_d


def build_l1a_program():
    ctx = Ctx()
    nc = ctx.nc
    ctx.din("x_in", [CC, 128, T], BF16)
    declare_l1a_inputs(ctx)
    ctx.dout("x_out", [CC, 128, T], BF16)

    b = alloc_shared(ctx)
    cz_d, b1_d, b2_d = alloc_l1a(ctx)
    ctx.ident_sb = nc.alloc_sbuf_tensor("ident_sb", [128, 128], F32).ap()
    ident_dram = nc.inline_tensor(np.eye(128, dtype=np.float32), "ident_const")
    s_init = ctx.sem("init")

    with nc.Block() as blk:
        @blk.sync
        def _(sy):
            sy.dma_start(out=ctx.band1_sb[:], in_=b1_d.ap()).then_inc(s_init.h, 16)
            sy.dma_start(out=ctx.band2_sb[:], in_=b2_d.ap()).then_inc(s_init.h, 16)
            sy.dma_start(out=ctx.ident_sb, in_=ident_dram.ap()).then_inc(s_init.h, 16)
            sy.dma_start(out=ctx.ident64_sb[:], in_=ctx.ident64_d.ap()).then_inc(s_init.h, 16)
        s_init.n += 64

        @blk.vector
        def _(v):
            ins = v.memset(b.ones_bf[:], 1.0)
            s_init.inc(ins, 1)
            ins = v.memset(b.zb[:], 0.0)
            s_init.inc(ins, 1)

        for eng in [blk.tensor, blk.gpsimd, blk.vector, blk.scalar, blk.sync]:
            def emit(e):
                s_init.wait(e)
            eng(emit)

        emit_l1_attn(ctx, blk, b, "x_in", "x_out")

        @blk.sync
        def _(sy):
            for sem in ctx.dma_sems:
                sy.wait_ge(sem.h, sem.n)
    return ctx



def emit_l1_ffn(ctx, blk, b, x_in_name, x_out_name):
    x_in = ctx.dram[x_in_name]
    x_out = ctx.dram[x_out_name]
    wgu_scr = ctx.dscratch("wgu_scr", [16, 128, 11264], BF16)
    wdn_scr = ctx.dscratch("wdn_scr", [44, 128, 2048], BF16)
    sc2_gu = ctx.dscratch("sc2_gu", [16, 11264], F32)
    sc2_dn = ctx.dscratch("sc2_dn", [44, 2048], F32)
    sp_row4 = ctx.dscratch("sp_row_l1f", [6, T], F32)

    s_mm = ctx.sem("s_mm")
    s_v = ctx.sem("s_v")
    s_s = ctx.sem("s_s")
    s_p = ctx.sem("s_p")
    s_evV = ctx.sem("s_evV")
    s_evS = ctx.sem("s_evS")
    s_srg = [ctx.sem(f"s_srg{j}") for j in range(4)]
    s_srs = [ctx.sem(f"s_srs{j}") for j in range(4)]
    s_wr = [ctx.sem(f"s_wr{j}") for j in range(3)]
    s_t1 = [ctx.sem(f"s_t1{j}") for j in range(3)]
    s_bc = ctx.sem("s_bc")
    s_rsp = ctx.sem("s_rsp")
    s_cv = ctx.sem("s_cv")
    s_cp = ctx.sem("s_cp")
    s_qb = [ctx.sem("s_qb0"), ctx.sem("s_qb1")]
    s_qg = ctx.sem("s_qg")
    if not hasattr(ctx, "dma_sems"):
        ctx.dma_sems = [s_bc, s_rsp] + s_srg + s_srs + s_wr + s_t1 + s_qb
    if s_qg not in ctx.dma_sems:
        ctx.dma_sems.append(s_qg)

    if not hasattr(ctx, "rowsF"):
        ctx.rowsF = ctx.nc.alloc_sbuf_tensor("rowsF", [128, T], F32)
    scb_even = b.aB
    scb_odd = ctx.rowsF

    slotBflat = b.slotB.ap().rearrange("p a c -> p (a c)")

    def act_sl(i):
        return slotBflat[:, i * TW:(i + 1) * TW]

    def n3_sl(cc):
        return b.slotA.ap()[:, cc, 1:1 + TW]

    def y_sl(cc):
        return b.slotA.ap()[:, cc, 520:520 + TW]

    def x3_sl(cc):
        return b.slotA.ap()[:, cc, 1040:1040 + TW]

    tmp1f32 = b.tmp1.ap().rearrange("p a c -> p (a c)").bitcast(F32)
    tmp2f32 = b.tmp2.ap().rearrange("p a c -> p (a c)").bitcast(F32)

    sgu = ctx.dram["l1f_sc_gu"]
    sdn = ctx.dram["l1f_sc_dn"]
    v0 = s_v.n
    s0_ = s_s.n
    mm0_ = s_mm.n
    sp_jobs = [(sgu, sc2_gu, 16, j) for j in range(6)] + [(sdn, sc2_dn, 44, 0)]
    for ji, (src_d, dst_d, nrows, j) in enumerate(sp_jobs):
        w = min(2048, src_d.shape[1] - j * 2048)

        @blk.sync
        def _(sy, src_d=src_d, j=j, w=w, nrows=nrows, ji=ji, vv=s_v.n, t1e=[s.n for s in s_t1]):
            if ji == 0:
                sy.wait_ge(s_v.h, v0)
                sy.wait_ge(s_s.h, s0_)
                sy.wait_ge(s_mm.h, mm0_)
                for sem, val in zip(s_t1, t1e):
                    sy.wait_ge(sem.h, val)
            else:
                sy.wait_ge(s_v.h, vv)
            ins = sy.dma_start(out=b.sring.ap()[:, 0].bitcast(FP16)[0:nrows, 0:w],
                               in_=src_d[:, j * 2048:j * 2048 + w])
            s_srs[0].inc(ins, 16)
        ldv = s_srs[0].n

        @blk.vector
        def _(v, nrows=nrows, w=w, ldv=ldv, t1n=s_t1[0].n):
            v.wait_ge(s_srs[0].h, ldv)
            v.wait_ge(s_t1[0].h, t1n)
            ins = v.tensor_scalar(tmp1f32[0:nrows, 0:w],
                                  b.sring.ap()[:, 0].bitcast(FP16)[0:nrows, 0:w], 16384.0, None, ALU.mult)
            s_v.inc(ins, 1)

        @blk.scalar
        def _(s, dst_d=dst_d, j=j, w=w, nrows=nrows, vv=s_v.n):
            s.wait_ge(s_v.h, vv)
            ins = s.dma_start(out=dst_d[:, j * 2048:j * 2048 + w], in_=tmp1f32[0:nrows, 0:w])
            s_t1[0].inc(ins, 16)
    scales_done = s_t1[0].n

    s_dqp = [ctx.sem("s_dqp0"), ctx.sem("s_dqp1")]
    s_dqs = [ctx.sem("s_dqs0"), ctx.sem("s_dqs1")]
    for x in s_dqp + s_dqs:
        if x not in ctx.dma_sems:
            ctx.dma_sems.append(x)

    deq_tiles = ([("gu", cc, og) for og in range(6) for cc in range(16)] +
                 [("dn", cc, 0) for cc in range(44)])
    N_DEQ = len(deq_tiles)
    deq = {"idx": 0, "v4": [], "cv_v3": []}

    def emit_deq_tile(gates=()):
        idx = deq["idx"]
        deq["idx"] += 1
        kind, cc, og = deq_tiles[idx]
        src_d = ctx.dram["l1f_pk_gu"] if kind == "gu" else ctx.dram["l1f_pk_dn"]
        scr = wgu_scr if kind == "gu" else wdn_scr
        scd = sc2_gu if kind == "gu" else sc2_dn
        ncols = 11264 if kind == "gu" else 2048
        w = min(2048, ncols - og * 2048)
        slot = idx % 2
        sgn_slot = 2 + idx % 2
        gates = tuple(gates)

        @blk.sync
        def _(sy, idx=idx, cc=cc, og=og, w=w, slot=slot, src_d=src_d, gates=gates):
            for sem, val in gates:
                sy.wait_ge(sem.h, val)
            if idx >= 2:
                sy.wait_ge(s_v.h, deq["v4"][idx - 2])
            ins = sy.dma_start(out=b.sring.ap()[:, slot].bitcast(U16)[:, 0:w],
                               in_=src_d[cc * 128:(cc + 1) * 128, og * 2048:og * 2048 + w])
            s_dqs[slot].inc(ins, 16)
        ldv = s_dqs[slot].n

        @blk.gpsimd
        def _(g, idx=idx, cc=cc, og=og, w=w, scd=scd, gates=gates):
            for sem, val in gates:
                g.wait_ge(sem.h, val)
            if idx >= 2:
                g.wait_ge(s_v.h, deq["v4"][idx - 2])
            scb = scb_even if idx % 2 == 0 else scb_odd
            ins = g.dma_start(out=scb.ap()[:, 0:w],
                              in_=scd[cc, og * 2048:og * 2048 + w].unsqueeze(0).broadcast_to([128, w]))
            s_qg.inc(ins, 16)
        qgv = s_qg.n

        @blk.vector
        def _(v, idx=idx, w=w, slot=slot, sgn_slot=sgn_slot, qgv=qgv, ldv=ldv):
            v.wait_ge(s_dqs[slot].h, ldv)
            v.wait_ge(s_qg.h, qgv)
            if idx >= 2:
                v.wait_ge(s_dqp[idx % 2].h, 16 * (idx // 2))
                v.wait_ge(s_cv.h, deq["cv_v3"][idx - 2])
            w16 = b.sring.ap()[:, slot].bitcast(U16)
            sg = b.sring.ap()[:, sgn_slot].bitcast(U16)
            ins = v.tensor_scalar(sg[:, 0:w], w16[:, 0:w], 8, 12,
                                  ALU.bitwise_and, ALU.logical_shift_left)
            s_cv.inc(ins, 1)
            s_cv.wait(v)
            ins = v.tensor_scalar(w16[:, 0:w], w16[:, 0:w], 7, 9,
                                  ALU.bitwise_and, ALU.logical_shift_left)
            s_cv.inc(ins, 1)
            s_cv.wait(v)
            ins = v.tensor_tensor(w16[:, 0:w], w16[:, 0:w], sg[:, 0:w], ALU.bitwise_or)
            s_cv.inc(ins, 1)
            deq["cv_v3"].append(s_cv.n)
            s_cv.wait(v)
            ostage = (b.tmp1.ap()[:, 2, 0:w] if idx % 2 == 0
                      else b.bB.ap().bitcast(BF16)[:, 0:w])
            scb = scb_even if idx % 2 == 0 else scb_odd
            ins = v.tensor_tensor(ostage, w16.bitcast(FP16)[:, 0:w],
                                  scb.ap()[:, 0:w], ALU.mult)
            s_v.inc(ins, 1)
        deq["v4"].append(s_v.n)

        @blk.scalar
        def _(s, idx=idx, cc=cc, og=og, w=w, scr=scr, v4v=deq["v4"][idx]):
            s.wait_ge(s_v.h, v4v)
            ostage = (b.tmp1.ap()[:, 2, 0:w] if idx % 2 == 0
                      else b.bB.ap().bitcast(BF16)[:, 0:w])
            ins = s.dma_start(out=scr[cc][:, og * 2048:og * 2048 + w], in_=ostage)
            s_dqp[idx % 2].inc(ins, 16)

    og0_gates = [(s_v, s_v.n), (s_s, s_s.n), (s_mm, s_mm.n), (s_t1[0], scales_done),
                 (s_rsp, s_rsp.n)]
    for _i in range(16):
        emit_deq_tile(gates=og0_gates if _i < 4 else ())

    mmS = s_mm.n
    sqS = s_s.n
    ld_vals = []
    v_need = s_v.n

    t1_x3 = [s.n for s in s_t1]

    @blk.gpsimd
    def _(g):
        g.wait_ge(s_v.h, v_need)
        for sem, val in zip(s_t1, t1_x3):
            g.wait_ge(sem.h, val)
        for ci in range(CC):
            if ci >= 4:
                g.wait_ge(s_mm.h, mmS + ci - 3)
            ins = g.dma_start(out=b.sring.ap()[:, ci % 4], in_=x_in[ci])
            s_srg[ci % 4].inc(ins, 16)
            ld_vals.append(s_srg[ci % 4].n)

    @blk.scalar
    def _(s):
        for ci in range(CC):
            s.wait_ge(s_srg[ci % 4].h, ld_vals[ci])
            if ci >= 2:
                s.wait_ge(s_mm.h, mmS + ci - 1)
            zact(s, b, b.tmp2.ap()[:, ci % 2], b.sring.ap()[:, ci % 4], AF.Square).then_inc(s_s.h, 1)
            s_s.n += 1

    evV0, evS0 = s_evV.n, s_evS.n

    @blk.tensor
    def _(t):
        t.wait_ge(s_evV.h, evV0)
        t.wait_ge(s_evS.h, evS0)
        for ci in range(CC):
            t.wait_ge(s_s.h, sqS + ci + 1)
            for tt in range(NT):
                t.matmul(b.psA.ap()[0:1, tt], b.ones_bf[:],
                         b.sring.ap()[:, ci % 4, tt * TW:(tt + 1) * TW],
                         start=(ci == 0), stop=(ci == CC - 1))
                ins = t.matmul(b.psB.ap()[0:1, tt], b.ones_bf[:],
                               b.tmp2.ap()[:, ci % 2, tt * TW:(tt + 1) * TW],
                               start=(ci == 0), stop=(ci == CC - 1))
            ins.then_inc(s_mm.h, 1)
    s_mm.n += CC

    r0 = b.aB.ap()[0:1, :]
    r1 = b.bB.ap()[0:1, :]

    @blk.vector
    def _(v, dq1=s_dqp[1].n):
        s_mm.wait(v)
        s_qb[0].wait(v)
        v.wait_ge(s_dqp[1].h, dq1)
        for tt in range(NT):
            ins = v.tensor_scalar(r1[:, tt * TW:(tt + 1) * TW], b.psA.ap()[0:1, tt], 1.0 / D, None, ALU.mult)
        s_v.inc(ins, 1)

    @blk.scalar
    def _(s):
        s_v.wait(s)
        ins = zact(s, b, r0, r1, AF.Square)
        s_s.inc(ins, 1)

    @blk.vector
    def _(v):
        s_s.wait(v)
        for tt in range(NT):
            ins = v.tensor_scalar(r1[:, tt * TW:(tt + 1) * TW], b.psB.ap()[0:1, tt], 1.0 / D, None, ALU.mult)
        s_cv.inc(ins, 1)
        s_cv.wait(v)
        ins = v.tensor_tensor(r1, r1, r0, ALU.subtract)
        s_cv.inc(ins, 1)
        s_cv.wait(v)
        ins = v.tensor_scalar(r1, r1, 1.0, LN_EPS, ALU.mult, ALU.add)
        s_cv.inc(ins, 1)
        s_cv.wait(v)
        ins = v.reciprocal(r0, r1)
        s_v.inc(ins, 1)

    @blk.scalar
    def _(s):
        s_v.wait(s)
        ins = zact(s, b, r0, r0, AF.Sqrt)
        s_s.inc(ins, 1)

    @blk.vector
    def _(v):
        s_s.wait(v)
        for tt in range(NT):
            ins = v.tensor_scalar(r1[:, tt * TW:(tt + 1) * TW], b.psA.ap()[0:1, tt], -1.0 / D, None, ALU.mult)
        s_cv.inc(ins, 1)
        s_cv.wait(v)
        ins = v.tensor_tensor(r1, r1, r0, ALU.mult)
        s_evV.inc(ins, 1)
    rows_ok = s_evV.n

    @blk.scalar
    def _(s):
        s.wait_ge(s_evV.h, rows_ok)
        s.dma_start(out=sp_row4[0:1, :], in_=r0).then_inc(s_rsp.h, 16)
        s.dma_start(out=sp_row4[1:2, :], in_=r1).then_inc(s_rsp.h, 16)
    s_rsp.n += 32
    rows_sp = s_rsp.n

    NOG = 22
    for tq in range(NT):
        tq0 = tq * TW
        ldn_vals = []
        vq0 = s_v.n
        mmq0 = s_mm.n

        @blk.sync
        def _(sy, tq=tq, vq0=vq0, mmq0=mmq0):
            sy.wait_ge(s_rsp.h, rows_sp)
            sy.wait_ge(s_v.h, vq0)
            sy.wait_ge(s_mm.h, mmq0)
            ins = sy.dma_start(out=tmp2f32[:, 1024:1536],
                               in_=sp_row4[0, tq0:tq0 + TW].unsqueeze(0).broadcast_to([128, TW]))
            s_qb[0].inc(ins, 16)
            ins = sy.dma_start(out=tmp2f32[:, 1536:2048],
                               in_=sp_row4[1, tq0:tq0 + TW].unsqueeze(0).broadcast_to([128, TW]))
            s_qb[0].inc(ins, 16)
            for ci in range(CC):
                if ci >= 4:
                    sy.wait_ge(s_v.h, vq0 + ci - 3)
                ins = sy.dma_start(out=x3_sl(ci), in_=x_in[ci][:, tq0:tq0 + TW])
                s_srs[ci % 4].inc(ins, 16)
                ldn_vals.append(s_srs[ci % 4].n)
        qbv = s_qb[0].n

        t1_now = [s.n for s in s_t1]

        @blk.vector
        def _(v, tq=tq, ldn_vals=ldn_vals, qbv=qbv, mmq0=mmq0, t1_now=t1_now):
            v.wait_ge(s_qb[0].h, qbv)
            v.wait_ge(s_mm.h, mmq0)
            for sem, val in zip(s_t1, t1_now):
                v.wait_ge(sem.h, val)
            aBq = tmp2f32[:, 1024:1536]
            bBq = tmp2f32[:, 1536:2048]
            for ci in range(CC):
                v.wait_ge(s_srs[ci % 4].h, ldn_vals[ci])
                v.wait_ge(s_v.h, s_v.n)
                ins = v.tensor_tensor(b.tmp1.ap()[:, 0, 0:TW], x3_sl(ci), aBq, ALU.mult)
                s_cv.inc(ins, 1)
                s_cv.wait(v)
                ins = v.tensor_tensor(n3_sl(ci), b.tmp1.ap()[:, 0, 0:TW], bBq, ALU.add)
                s_v.inc(ins, 1)
        n3_done = s_v.n

        for og in range(NOG):
            lw_vals = []
            mmg0 = s_mm.n
            evp = (s_evS.n, s_evV.n)

            dq_need = 16 * 8 * (og // 4 + 1) if tq == 0 else 16 * (N_DEQ // 2)

            lw_vals.extend([None] * CC)
            for qi, qeng in enumerate((blk.gpsimd, blk.sync, blk.scalar)):
                def emit(q, qi=qi, og=og, mmg0=mmg0, dq_need=dq_need, lw_vals=lw_vals):
                    q.wait_ge(s_dqp[0].h, dq_need)
                    q.wait_ge(s_dqp[1].h, dq_need)
                    q.wait_ge(s_mm.h, mmg0)
                    for cc in range(qi, CC, 3):
                        if cc >= 3:
                            q.wait_ge(s_mm.h, mmg0 + cc - 2)
                        ins = q.dma_start(out=b.wring.ap()[:, cc % 3, 0:TW],
                                          in_=wgu_scr[cc][:, og * TW:(og + 1) * TW])
                        s_wr[cc % 3].inc(ins, 16)
                        lw_vals[cc] = s_wr[cc % 3].n
                qeng(emit)

            @blk.tensor
            def _(t, og=og, lw_vals=lw_vals, n3_done=n3_done, evp=evp):
                t.wait_ge(s_v.h, n3_done)
                t.wait_ge(s_evS.h, evp[0])
                t.wait_ge(s_evV.h, evp[1])
                for cc in range(CC):
                    t.wait_ge(s_wr[cc % 3].h, lw_vals[cc])
                    wv = b.wring.ap()[:, cc % 3, 0:TW].rearrange("p (j o) -> p j o", o=128)
                    for j in range(4):
                        ins = t.matmul((b.psA if og % 2 == 0 else b.psB).ap()[:, j],
                                       wv[:, j], n3_sl(cc), start=(cc == 0), stop=(cc == CC - 1))
                    ins.then_inc(s_mm.h, 1)
            s_mm.n += CC
            mm_done = s_mm.n

            ps = b.psA if og % 2 == 0 else b.psB
            if og < 11:
                @blk.scalar
                def _(s, og=og, mm_done=mm_done):
                    s.wait_ge(s_mm.h, mm_done)
                    for j in range(4):
                        ins = zact(s, b, act_sl(og * 4 + j), ps.ap()[:, j], AF.Silu)
                        s_evS.inc(ins, 1)
            else:
                @blk.vector
                def _(v, og=og, mm_done=mm_done):
                    v.wait_ge(s_mm.h, mm_done)
                    for j in range(4):
                        i = (og - 11) * 4 + j
                        v.wait_ge(s_v.h, s_v.n)
                        ins = v.tensor_tensor(act_sl(i), act_sl(i), ps.ap()[:, j], ALU.mult)
                        s_evV.inc(ins, 1)

            if tq == 0:
                target = min(N_DEQ, 16 + 6 * (og + 1))
                post_stats_gates = [(s_s, s_s.n), (s_mm, s_mm.n), (s_rsp, rows_sp)]
                while deq["idx"] < target:
                    emit_deq_tile(gates=post_stats_gates if deq["idx"] <= 17 else ())
        act_done = (s_evS.n, s_evV.n)

        for og in range(4):
            lw_vals = []
            mmg0 = s_mm.n
            evp = (s_evS.n, s_evV.n)

            lw_vals.extend([None] * 44)
            for qi, qeng in enumerate((blk.gpsimd, blk.sync, blk.scalar)):
                def emit(q, qi=qi, og=og, mmg0=mmg0, lw_vals=lw_vals):
                    q.wait_ge(s_dqp[0].h, 16 * (N_DEQ // 2))
                    q.wait_ge(s_dqp[1].h, 16 * (N_DEQ // 2))
                    q.wait_ge(s_mm.h, mmg0)
                    for cc in range(qi, 44, 3):
                        if cc >= 3:
                            q.wait_ge(s_mm.h, mmg0 + cc - 2)
                        ins = q.dma_start(out=b.wring.ap()[:, cc % 3, 0:TW],
                                          in_=wdn_scr[cc][:, og * TW:(og + 1) * TW])
                        s_wr[cc % 3].inc(ins, 16)
                        lw_vals[cc] = s_wr[cc % 3].n
                qeng(emit)

            @blk.tensor
            def _(t, og=og, lw_vals=lw_vals, evp=evp, act_done=act_done):
                t.wait_ge(s_evS.h, act_done[0])
                t.wait_ge(s_evV.h, act_done[1])
                t.wait_ge(s_evS.h, evp[0])
                t.wait_ge(s_evV.h, evp[1])
                for cc in range(44):
                    t.wait_ge(s_wr[cc % 3].h, lw_vals[cc])
                    wv = b.wring.ap()[:, cc % 3, 0:TW].rearrange("p (j o) -> p j o", o=128)
                    for j in range(4):
                        ins = t.matmul((b.psA if og % 2 == 0 else b.psB).ap()[:, j],
                                       wv[:, j], act_sl(cc), start=(cc == 0), stop=(cc == 43))
                    ins.then_inc(s_mm.h, 1)
            s_mm.n += 44
            mm_done = s_mm.n

            ps = b.psA if og % 2 == 0 else b.psB

            @blk.vector
            def _(v, og=og, mm_done=mm_done):
                v.wait_ge(s_mm.h, mm_done)
                for j in range(4):
                    v.wait_ge(s_v.h, s_v.n)
                    ins = v.tensor_copy(y_sl(og * 4 + j), ps.ap()[:, j])
                    s_evV.inc(ins, 1)
        y_done = s_evV.n

        sqq0 = s_s.n
        evq = s_evV.n
        mmr_base = s_mm.n

        @blk.scalar
        def _(s, y_done=y_done, mmr_base=mmr_base, sqq0=sqq0):
            s.wait_ge(s_evV.h, y_done)
            for cc in range(CC):
                if cc >= 4:
                    s.wait_ge(s_mm.h, mmr_base + cc - 3)
                zact(s, b, b.tmp2.ap()[:, 0, (cc % 4) * TW:(cc % 4 + 1) * TW], y_sl(cc), AF.Square).then_inc(s_s.h, 1)
                s_s.n += 1

        @blk.tensor
        def _(t, sqq0=sqq0, evq=evq):
            t.wait_ge(s_evV.h, evq)
            for cc in range(CC):
                t.wait_ge(s_s.h, sqq0 + cc + 1)
                ins = t.matmul(b.psA.ap()[0:1, 0], b.ones_bf[:],
                               b.tmp2.ap()[:, 0, (cc % 4) * TW:(cc % 4 + 1) * TW],
                               start=(cc == 0), stop=(cc == CC - 1))
                ins.then_inc(s_mm.h, 1)
        s_mm.n += CC
        mm_rms = s_mm.n

        rrow = tmp2f32[0:1, 1024:1536]

        @blk.vector
        def _(v, mm_rms=mm_rms):
            v.wait_ge(s_mm.h, mm_rms)
            ins = v.tensor_scalar(rrow, b.psA.ap()[0:1, 0], 1.0 / D, 1e-6, ALU.mult, ALU.add)
            s_cv.inc(ins, 1)
            s_cv.wait(v)
            ins = v.reciprocal(rrow, rrow)
            s_evV.inc(ins, 1)

        @blk.scalar
        def _(s, tq=tq):
            s.wait_ge(s_evV.h, s_evV.n)
            ins = zact(s, b, rrow, rrow, AF.Sqrt)
            s_s.inc(ins, 1)
            s.wait_ge(s_s.h, s_s.n)
            s.dma_start(out=sp_row4[2:3, tq0:tq0 + TW], in_=rrow).then_inc(s_rsp.h, 16)
        s_rsp.n += 16
        rr_sp = s_rsp.n

        @blk.sync
        def _(sy, rr_sp=rr_sp):
            sy.wait_ge(s_rsp.h, rr_sp)
            ins = sy.dma_start(out=tmp2f32[:, 1536:2048],
                               in_=sp_row4[2, tq0:tq0 + TW].unsqueeze(0).broadcast_to([128, TW]))
            s_qb[1].inc(ins, 16)
        rb_val = s_qb[1].n

        for cc in range(CC):
            t1v = s_t1[cc % 3].n

            @blk.vector
            def _(v, cc=cc, tq=tq, t1v=t1v, rb_val=rb_val):
                v.wait_ge(s_qb[1].h, rb_val)
                v.wait_ge(s_t1[cc % 3].h, t1v)
                v.wait_ge(s_v.h, s_v.n)
                rstd_b = tmp2f32[:, 1536:2048]
                ins = v.tensor_tensor(tmp1f32[:, (cc % 3) * TW:(cc % 3 + 1) * TW], y_sl(cc), rstd_b, ALU.mult)
                s_cv.inc(ins, 1)
                s_cv.wait(v)
                ins = v.tensor_tensor(tmp1f32[:, (cc % 3) * TW:(cc % 3 + 1) * TW],
                                      tmp1f32[:, (cc % 3) * TW:(cc % 3 + 1) * TW],
                                      x3_sl(cc), ALU.add)
                s_v.inc(ins, 1)

            @blk.scalar
            def _(s, cc=cc, tq=tq, vv=None):
                s.wait_ge(s_v.h, s_v.n)
                ins = s.dma_start(out=x_out[cc][:, tq0:tq0 + TW],
                                  in_=tmp1f32[:, (cc % 3) * TW:(cc % 3 + 1) * TW])
                s_t1[cc % 3].inc(ins, 16)


def _unpack_nib_T(packed):
    a = np.asarray(packed, np.int32).view(np.uint32)
    out_f, in_p = a.shape
    nib = (a[:, :, None] >> (np.arange(8, dtype=np.uint32) * 4)) & np.uint32(15)
    return np.ascontiguousarray(nib.astype(np.uint16).reshape(out_f, in_p * 8).T)


def host_pack_l1f(inp):
    pk_gu = _unpack_nib_T(inp["L1_ffn_gate_up_packed"])
    pk_dn = _unpack_nib_T(inp["L1_ffn_down_packed"])
    return {
        "l1f_pk_gu": pk_gu,
        "l1f_pk_dn": pk_dn,
        "l1f_sc_gu": np.ascontiguousarray(np.asarray(inp["L1_ffn_gate_up_scales"])),
        "l1f_sc_dn": np.ascontiguousarray(np.asarray(inp["L1_ffn_down_scales"])),
    }


def declare_l1f_inputs(ctx):
    ctx.din("l1f_pk_gu", [2048, 11264], U16)
    ctx.din("l1f_pk_dn", [5632, 2048], U16)
    ctx.din("l1f_sc_gu", [16, 11264], FP16)
    ctx.din("l1f_sc_dn", [44, 2048], FP16)


def build_l1f_program():
    ctx = Ctx()
    nc = ctx.nc
    ctx.din("x_in", [CC, 128, T], BF16)
    declare_l1f_inputs(ctx)
    ctx.dout("x_out", [CC, 128, T], F32)

    b = alloc_shared(ctx)
    ctx.shp_sb = nc.alloc_sbuf_tensor("shp_sb", [128, 1], F32)
    shp_d = nc.inline_tensor((np.arange(128) % 4 * 4.0).astype(np.float32).reshape(128, 1), "shp_const")
    s_init = ctx.sem("init")

    with nc.Block() as blk:
        @blk.sync
        def _(sy):
            sy.dma_start(out=ctx.shp_sb[:], in_=shp_d.ap()).then_inc(s_init.h, 16)
        s_init.n += 16

        @blk.vector
        def _(v):
            ins = v.memset(b.ones_bf[:], 1.0)
            s_init.inc(ins, 1)
            ins = v.memset(b.zb[:], 0.0)
            s_init.inc(ins, 1)

        for eng in [blk.tensor, blk.gpsimd, blk.vector, blk.scalar, blk.sync]:
            def emit(e):
                s_init.wait(e)
            eng(emit)

        emit_l1_ffn(ctx, blk, b, "x_in", "x_out")

        @blk.sync
        def _(sy):
            for sem in ctx.dma_sems:
                sy.wait_ge(sem.h, sem.n)
    return ctx



def build_l0a_program():
    ctx = Ctx()
    nc = ctx.nc
    ctx.din("x_in", [CC, 128, T], F32)
    for nm in ["wr", "wk", "wv", "wo", "cwr", "cwk", "cwv"]:
        ctx.din(f"l0a_{nm}", [CC, 128, D], F32)
    ctx.din("l0a_vecs", [128, 10, CC], F32)
    ctx.dout("x_out", [CC, 128, T], BF16)

    b = alloc_shared(ctx)
    vecs_sb = nc.alloc_sbuf_tensor("vecs_l0a", [128, 10, CC], F32)
    s_init = ctx.sem("init")

    with nc.Block() as blk:
        @blk.sync
        def _(sy):
            sy.dma_start(out=vecs_sb[:], in_=ctx.dram["l0a_vecs"]).then_inc(s_init.h, 16)
        s_init.n += 16

        @blk.vector
        def _(v):
            ins = v.memset(b.ones_bf[:], 1.0)
            s_init.inc(ins, 1)
            ins = v.memset(b.zb[:], 0.0)
            s_init.inc(ins, 1)

        @blk.vector
        def _(v):
            s_init.wait(v)
            ins = v.tensor_scalar(vecs_sb.ap()[:, 4, :], vecs_sb.ap()[:, 4, :],
                                  18.420680743952367, None, ALU.add)
            s_init.inc(ins, 1)

        for eng in [blk.tensor, blk.gpsimd, blk.vector]:
            def emit(e):
                s_init.wait(e)
            eng(emit)

        emit_l0_attn(ctx, blk, b, vecs_sb, "x_in", "x_out")

        @blk.sync
        def _(sy):
            for sem in ctx.dma_sems:
                sy.wait_ge(sem.h, sem.n)
            for sem in (ctx.sem("s_sr0"), ):
                pass
    return ctx


def host_pack_l0a(inp):
    return {
        "l0a_wr": pack_wT(inp["L0_attn_tm_Wr"]),
        "l0a_wk": pack_wT(inp["L0_attn_tm_Wk"]),
        "l0a_wv": pack_wT(inp["L0_attn_tm_Wv"]),
        "l0a_wo": pack_wT(inp["L0_attn_tm_Wout"]),
        "l0a_cwr": pack_wT(inp["L0_attn_cm_Wr"]),
        "l0a_cwk": pack_wT(inp["L0_attn_cm_Wk"]),
        "l0a_cwv": pack_wT(inp["L0_attn_cm_Wv"]),
        "l0a_vecs": pack_vecs([
            inp["L0_attn_ln1_g"], inp["L0_attn_ln1_b"], inp["L0_attn_ln2_g"], inp["L0_attn_ln2_b"],
            inp["L0_attn_tm_u"],
            inp["L0_attn_tm_mu_r"], inp["L0_attn_tm_mu_k"], inp["L0_attn_tm_mu_v"],
            inp["L0_attn_cm_mu_r"], inp["L0_attn_cm_mu_k"]]),
    }


_PROGS = {}


def _get_programs():
    if "p" not in _PROGS:
        _PROGS["p"] = (build_l0a_program(), build_l0f_program(), build_l1a_program(), build_l1f_program())
    return _PROGS["p"]


def build_fused_program():
    ctx = Ctx()
    nc = ctx.nc
    ctx.din("x_in", [CC, 128, T], F32)
    for nm in ["wr", "wk", "wv", "wo", "cwr", "cwk", "cwv"]:
        ctx.din(f"l0a_{nm}", [CC, 128, D], F32)
    ctx.din("l0a_vecs", [128, 10, CC], F32)
    declare_l0f_inputs(ctx)
    declare_l1a_inputs(ctx)
    declare_l1f_inputs(ctx)
    ctx.dscratch("xres1", [CC, 128, T], BF16)
    ctx.dscratch("xres2", [CC, 128, T], BF16)
    ctx.dscratch("xres3", [CC, 128, T], BF16)
    ctx.dout("x_out", [CC, 128, T], F32)

    b = alloc_shared(ctx)
    vecs_sb = nc.alloc_sbuf_tensor("vecs_l0a", [128, 10, CC], F32)
    ctx.ident_sb = nc.alloc_sbuf_tensor("ident_sb", [128, 128], F32).ap()
    ident_dram = nc.inline_tensor(np.eye(128, dtype=np.float32), "ident_const")
    cz_d, b1_d, b2_d = alloc_l1a(ctx)
    ctx.shp_sb = nc.alloc_sbuf_tensor("shp_sb", [128, 1], F32)
    shp_d = nc.inline_tensor((np.arange(128) % 4 * 4.0).astype(np.float32).reshape(128, 1), "shp_const")
    s_init = ctx.sem("init")

    with nc.Block() as blk:
        @blk.sync
        def _(sy):
            sy.dma_start(out=vecs_sb[:], in_=ctx.dram["l0a_vecs"]).then_inc(s_init.h, 16)
            sy.dma_start(out=ctx.ident_sb, in_=ident_dram.ap()).then_inc(s_init.h, 16)
            sy.dma_start(out=ctx.ident64_sb[:], in_=ctx.ident64_d.ap()).then_inc(s_init.h, 16)
            sy.dma_start(out=ctx.band1_sb[:], in_=b1_d.ap()).then_inc(s_init.h, 16)
            sy.dma_start(out=ctx.band2_sb[:], in_=b2_d.ap()).then_inc(s_init.h, 16)
            sy.dma_start(out=ctx.shp_sb[:], in_=shp_d.ap()).then_inc(s_init.h, 16)
        s_init.n += 96

        @blk.vector
        def _(v):
            ins = v.memset(b.ones_bf[:], 1.0)
            s_init.inc(ins, 1)
            ins = v.memset(b.zb[:], 0.0)
            s_init.inc(ins, 1)

        @blk.vector
        def _(v):
            s_init.wait(v)
            ins = v.tensor_scalar(vecs_sb.ap()[:, 4, :], vecs_sb.ap()[:, 4, :],
                                  18.420680743952367, None, ALU.add)
            s_init.inc(ins, 1)

        for eng in [blk.tensor, blk.gpsimd, blk.vector, blk.scalar, blk.sync]:
            def emit(e):
                s_init.wait(e)
            eng(emit)

        emit_l0_attn(ctx, blk, b, vecs_sb, "x_in", "xres1")
        emit_l0_ffn(ctx, blk, b, "xres1", "xres2")
        emit_l1_attn(ctx, blk, b, "xres2", "xres3")
        emit_l1_ffn(ctx, blk, b, "xres3", "x_out")

        @blk.sync
        def _(sy):
            for sem in ctx.dma_sems:
                sy.wait_ge(sem.h, sem.n)
    return ctx


def _get_fused():
    if "f" not in _PROGS:
        _PROGS["f"] = build_fused_program()
    return _PROGS["f"]


def kernel(**inputs):
    from concourse.bass_utils import run_bass_kernel_spmd
    ncores = 8
    ctx = _get_fused()
    w = {}
    w.update(host_pack_l0a(inputs))
    w.update(host_pack_l0f(inputs))
    w.update(host_pack_l1a(inputs))
    w.update(host_pack_l1f(inputs))
    x = np.asarray(inputs["L0_attn_x"], np.float32)
    in_maps = [dict(w, x_in=pack_x_chunks(x[bb])) for bb in range(ncores)]
    res = run_bass_kernel_spmd(ctx.nc, in_maps, core_ids=list(range(ncores)))
    out = np.stack([unpack_x_chunks(np.asarray(res.results[bb]["x_out"]).astype(np.float32))
                    for bb in range(ncores)], axis=0)
    return out


# version 19 (final)
# speedup vs baseline: 1.0112x; 1.0112x over previous
import numpy as np
import ml_dtypes
import concourse.bass as bass
import concourse.mybir as mybir
from concourse.bass_utils import run_bass_kernel_spmd

F32 = mybir.dt.float32
BF16 = mybir.dt.bfloat16
FP16 = mybir.dt.float16
U16 = mybir.dt.uint16
U8 = mybir.dt.uint8
AF = mybir.ActivationFunctionType
ALU = mybir.AluOpType

D = 2048
T = 2048
CC = 16
TW = 512
NT = T // TW
LN_EPS = 1e-5
BF = ml_dtypes.bfloat16


def pack_wT(W):
    dout, din = W.shape
    n_oc, n_cc = dout // 128, din // 128
    a = np.asarray(W, np.float32).reshape(n_oc, 128, n_cc, 128)
    return np.ascontiguousarray(a.transpose(0, 3, 2, 1)).reshape(n_oc, 128, n_cc * 128)


def pack_vecs(vecs):
    cols = [np.asarray(v, np.float32).reshape(CC, 128).T for v in vecs]
    return np.ascontiguousarray(np.stack(cols, axis=1))


def pack_x_chunks(xb):
    return np.ascontiguousarray(np.asarray(xb, np.float32).T.reshape(CC, 128, T))


def unpack_x_chunks(ch):
    return np.ascontiguousarray(ch.reshape(D, T).T)



class Sem:
    def __init__(self, nc, name):
        self.h = nc.alloc_semaphore(name)
        self.n = 0

    def inc(self, ins, k=1):
        ins.then_inc(self.h, k)
        self.n += k
        return ins

    def wait(self, eng, n=None):
        eng.wait_ge(self.h, self.n if n is None else n)


class Ctx:
    def __init__(self):
        self.nc = bass.Bass("TRN2", target_bir_lowering=False, debug=False)
        self.dram = {}
        self.sems = {}

    def sem(self, name):
        if name not in self.sems:
            self.sems[name] = Sem(self.nc, name)
        return self.sems[name]

    def din(self, name, shape, dtype):
        self.dram[name] = self.nc.dram_tensor(name, list(shape), dtype, kind="ExternalInput").ap()
        return self.dram[name]

    def dout(self, name, shape, dtype):
        self.dram[name] = self.nc.dram_tensor(name, list(shape), dtype, kind="ExternalOutput").ap()
        return self.dram[name]

    def dscratch(self, name, shape, dtype):
        self.dram[name] = self.nc.dram_tensor(name, list(shape), dtype, kind="Internal").ap()
        return self.dram[name]


class Bufs:
    pass


def alloc_shared(ctx):
    nc = ctx.nc
    b = Bufs()
    b.slotA = nc.alloc_sbuf_tensor("slotA", [128, CC, T + 4], BF16)
    b.slotB = nc.alloc_sbuf_tensor("slotB", [128, CC, T + 4], BF16)
    b.wring = nc.alloc_sbuf_tensor("wring", [128, 3, CC * 128], BF16)
    b.sring = nc.alloc_sbuf_tensor("sring", [128, 4, T], BF16)
    b.tmp1 = nc.alloc_sbuf_tensor("tmp1", [128, 3, T], BF16)
    b.tmp2 = nc.alloc_sbuf_tensor("tmp2", [128, 2, T], BF16)
    b.aB = nc.alloc_sbuf_tensor("aB", [128, T], F32)
    b.bB = nc.alloc_sbuf_tensor("bB", [128, T], F32)
    b.ones_bf = nc.alloc_sbuf_tensor("ones_bf", [128, 1], BF16)
    b.zb = nc.alloc_sbuf_tensor("zb128", [128, 1], F32)
    b.psA = nc.alloc_psum_tensor("psA", [128, NT, TW], F32)
    b.psB = nc.alloc_psum_tensor("psB", [128, NT, TW], F32)
    return b


def psflat(ps):
    return ps.ap().rearrange("p a b -> p (a b)")


def zact(s, b, out, in_, af, **kw):
    kw.setdefault("bias", b.zb.ap()[0:out.shape[0], 0:1])
    return s.activation(out, in_, af, **kw)


def zcopy(s, b, out, in_):
    return s.activation(out, in_, AF.Identity, bias=b.zb.ap()[0:out.shape[0], 0:1])



def emit_l0_attn(ctx, blk, b, vecs_sb, x_in_name, x_out_name):
    V = {"ln1_g": 0, "ln1_b": 1, "ln2_g": 2, "ln2_b": 3, "u18": 4,
         "mu_r": 5, "mu_k": 6, "mu_v": 7, "cmu_r": 8, "cmu_k": 9}

    x_in = ctx.dram[x_in_name]
    x_out = ctx.dram[x_out_name]
    sp_n0 = ctx.dscratch("sp_n0", [CC, 128, T], BF16)
    sp_z = ctx.dscratch("sp_z", [CC, 128, T], BF16)
    sp_wkv = ctx.dscratch("sp_wkv", [CC, 128, T], BF16)
    sp_rwkv = ctx.dscratch("sp_rwkv", [CC, 128, T], BF16)
    sp_xmid = ctx.dscratch("sp_xmid", [CC, 128, T], BF16)
    sp_k2 = ctx.dscratch("sp_k2", [CC, 128, T], BF16)
    sp_row = ctx.dscratch("sp_row_l0a", [8, T], F32)

    s_mm = ctx.sem("s_mm")
    s_v = ctx.sem("s_v")
    s_s = ctx.sem("s_s")
    s_p = ctx.sem("s_p")
    s_evV = ctx.sem("s_evV")
    s_evS = ctx.sem("s_evS")
    s_srg = [ctx.sem(f"s_srg{j}") for j in range(4)]
    s_srs = [ctx.sem(f"s_srs{j}") for j in range(4)]
    s_wr = [ctx.sem(f"s_wr{j}") for j in range(3)]
    s_t1 = [ctx.sem(f"s_t1{j}") for j in range(3)]
    s_bc = ctx.sem("s_bc")
    s_slot = ctx.sem("s_slot")
    s_rsp = ctx.sem("s_rsp")
    s_cv = ctx.sem("s_cv")
    ctx.dma_sems = [s_bc, s_slot, s_rsp] + s_srg + s_srs + s_wr + s_t1

    def stat_pass(src_ap, queue, gate=None):
        mm0 = s_mm.n
        sq0 = s_s.n
        ld_vals = []

        s_q = s_srg if queue == "g" else s_srs

        v_need, s_need = s_v.n, s_s.n

        def loads(q):
            if gate is not None:
                for sem, val in gate:
                    q.wait_ge(sem.h, val)
            q.wait_ge(s_v.h, v_need)
            q.wait_ge(s_s.h, s_need)
            q.wait_ge(s_mm.h, mm0)
            for ci in range(CC):
                if ci >= 4:
                    q.wait_ge(s_mm.h, mm0 + ci - 3)
                ins = q.dma_start(out=b.sring.ap()[:, ci % 4], in_=src_ap[ci])
                s_q[ci % 4].inc(ins, 16)
                ld_vals.append((s_q[ci % 4], s_q[ci % 4].n))
        (blk.gpsimd if queue == "g" else blk.sync)(loads)

        @blk.scalar
        def _(s):
            for ci in range(CC):
                s.wait_ge(ld_vals[ci][0].h, ld_vals[ci][1])
                if ci >= 2:
                    s.wait_ge(s_mm.h, mm0 + ci - 1)
                zact(s, b, b.tmp2.ap()[:, ci % 2], b.sring.ap()[:, ci % 4], AF.Square).then_inc(s_s.h, 1)
                s_s.n += 1

        evV0, evS0 = s_evV.n, s_evS.n

        @blk.tensor
        def _(t):
            t.wait_ge(s_evV.h, evV0)
            t.wait_ge(s_evS.h, evS0)
            for ci in range(CC):
                t.wait_ge(s_s.h, sq0 + ci + 1)
                for tt in range(NT):
                    t.matmul(b.psA.ap()[0:1, tt], b.ones_bf[:],
                             b.sring.ap()[:, ci % 4, tt * TW:(tt + 1) * TW],
                             start=(ci == 0), stop=(ci == CC - 1))
                    ins = t.matmul(b.psB.ap()[0:1, tt], b.ones_bf[:],
                                   b.tmp2.ap()[:, ci % 2, tt * TW:(tt + 1) * TW],
                                   start=(ci == 0), stop=(ci == CC - 1))
                ins.then_inc(s_mm.h, 1)
        s_mm.n += CC

    def rows_from_stats():
        r0 = b.aB.ap()[0:1, :]
        r1 = b.bB.ap()[0:1, :]

        @blk.vector
        def _(v):
            s_mm.wait(v)
            s_bc.wait(v)
            for tt in range(NT):
                ins = v.tensor_scalar(r1[:, tt * TW:(tt + 1) * TW], b.psA.ap()[0:1, tt], 1.0 / D, None, ALU.mult)
            s_v.inc(ins, 1)

        @blk.scalar
        def _(s):
            s_v.wait(s)
            ins = zact(s, b, r0, r1, AF.Square)
            s_s.inc(ins, 1)

        @blk.vector
        def _(v):
            s_s.wait(v)
            for tt in range(NT):
                ins = v.tensor_scalar(r1[:, tt * TW:(tt + 1) * TW], b.psB.ap()[0:1, tt], 1.0 / D, None, ALU.mult)
            s_cv.inc(ins, 1)
            s_cv.wait(v)
            ins = v.tensor_tensor(r1, r1, r0, ALU.subtract)
            s_cv.inc(ins, 1)
            s_cv.wait(v)
            ins = v.tensor_scalar(r1, r1, 1.0, LN_EPS, ALU.mult, ALU.add)
            s_cv.inc(ins, 1)
            s_cv.wait(v)
            ins = v.reciprocal(r0, r1)
            s_v.inc(ins, 1)

        @blk.scalar
        def _(s):
            s_v.wait(s)
            ins = zact(s, b, r0, r0, AF.Sqrt)
            s_s.inc(ins, 1)

        @blk.vector
        def _(v):
            s_s.wait(v)
            for tt in range(NT):
                ins = v.tensor_scalar(r1[:, tt * TW:(tt + 1) * TW], b.psA.ap()[0:1, tt], -1.0 / D, None, ALU.mult)
            s_cv.inc(ins, 1)
            s_cv.wait(v)
            ins = v.tensor_tensor(r1, r1, r0, ALU.mult)
            s_evV.inc(ins, 1)

    def bcast_rows(slot_a, slot_b):
        row_done = s_evV.n

        @blk.scalar
        def _(s):
            s.wait_ge(s_evV.h, row_done)
            s.dma_start(out=sp_row[slot_a:slot_a + 1], in_=b.aB.ap()[0:1, :]).then_inc(s_rsp.h, 16)
            s.dma_start(out=sp_row[slot_b:slot_b + 1], in_=b.bB.ap()[0:1, :]).then_inc(s_rsp.h, 16)
        s_rsp.n += 32

        mm_need_bc = s_mm.n

        @blk.sync
        def _(sy):
            s_rsp.wait(sy)
            sy.wait_ge(s_mm.h, mm_need_bc)
            sy.dma_start(out=b.aB[:], in_=sp_row[slot_a].unsqueeze(0).broadcast_to([128, T])).then_inc(s_bc.h, 16)
            sy.dma_start(out=b.bB[:], in_=sp_row[slot_b].unsqueeze(0).broadcast_to([128, T])).then_inc(s_bc.h, 16)
        s_bc.n += 32

    def ln_apply(src_dram, dst_slot, g_idx, b_idx, t1_gate):
        bc_need = s_bc.n
        mm_need = s_mm.n
        bld0 = s_v.n
        p0 = s_p.n
        ld_vals = []

        ss_need = s_s.n

        @blk.sync
        def _(sy):
            for sem, val in t1_gate:
                sy.wait_ge(sem.h, val)
            sy.wait_ge(s_v.h, bld0)
            sy.wait_ge(s_s.h, ss_need)
            sy.wait_ge(s_mm.h, mm_need)
            for ci in range(CC):
                if ci >= 4:
                    sy.wait_ge(s_v.h, bld0 + 2 * (ci - 4) + 1)
                ins = sy.dma_start(out=b.sring.ap()[:, ci % 4], in_=src_dram[ci])
                s_srs[ci % 4].inc(ins, 16)
                ld_vals.append(s_srs[ci % 4].n)

        ss0 = s_s.n

        @blk.vector
        def _(v):
            v.wait_ge(s_bc.h, bc_need)
            v.wait_ge(s_mm.h, mm_need)
            v.wait_ge(s_s.h, ss0)
            for ci in range(CC):
                v.wait_ge(s_srs[ci % 4].h, ld_vals[ci])
                v.wait_ge(s_v.h, s_v.n)
                ins = v.tensor_tensor(b.tmp2.ap()[:, 0], b.sring.ap()[:, ci % 4], b.aB[:], ALU.mult)
                s_cv.inc(ins, 1)
                s_cv.wait(v)
                ins1 = v.tensor_tensor(b.tmp2.ap()[:, 0], b.tmp2.ap()[:, 0], b.bB[:], ALU.add)
                s_v.inc(ins1, 1)
                v.wait_ge(s_v.h, s_v.n)
                ins = v.tensor_scalar(dst_slot.ap()[:, ci, 1:1 + T], b.tmp2.ap()[:, 0],
                                      vecs_sb.ap()[:, g_idx, ci:ci + 1], vecs_sb.ap()[:, b_idx, ci:ci + 1],
                                      ALU.mult, ALU.add)
                s_v.inc(ins, 1)

        @blk.gpsimd
        def _(g):
            g.wait_ge(s_mm.h, mm_need)
            for ci in range(CC):
                ins = g.memset(dst_slot.ap()[:, ci, 0:1], 0.0)
            s_p.inc(ins, 1)
        return s_v.n, s_p.n

    def build_mix(mu_idx, src_slot, dst_slot, extra_waits=()):
        m0 = s_mm.n

        s0 = s_s.n
        p0 = s_p.n

        @blk.vector
        def _(v):
            v.wait_ge(s_mm.h, m0)
            v.wait_ge(s_s.h, s0)
            v.wait_ge(s_p.h, p0)
            for sem, val in extra_waits:
                v.wait_ge(sem.h, val)
            for ci in range(CC):
                h = src_slot.ap()[:, ci, 1:1 + T]
                hp = src_slot.ap()[:, ci, 0:T]
                d = b.tmp2.ap()[:, 0]
                v.wait_ge(s_v.h, s_v.n)
                ins = v.tensor_tensor(d, h, hp, ALU.subtract)
                s_cv.inc(ins, 1)
                s_cv.wait(v)
                ins = v.scalar_tensor_tensor(dst_slot.ap()[:, ci, 1:1 + T], d,
                                             vecs_sb.ap()[:, mu_idx, ci:ci + 1], hp, ALU.mult, ALU.add)
                s_v.inc(ins, 1)
        return s_v.n

    def proj_pass(w_ap, rhs_slot, ev_sem, evac, gate=None, n_oc=CC):
        ev0 = ev_sem.n
        mm0 = s_mm.n
        lw_vals = []

        @blk.gpsimd
        def _(g):
            g.wait_ge(s_mm.h, mm0)
            for oc in range(n_oc):
                if oc >= 3:
                    g.wait_ge(ev_sem.h, ev0 + oc - 2)
                ins = g.dma_start(out=b.wring.ap()[:, oc % 3], in_=w_ap[oc])
                s_wr[oc % 3].inc(ins, 16)
                lw_vals.append(s_wr[oc % 3].n)

        evV_need, evS_need = s_evV.n, s_evS.n

        @blk.tensor
        def _(t):
            if gate is not None:
                for sem, val in gate:
                    t.wait_ge(sem.h, val)
            t.wait_ge(s_evV.h, evV_need)
            t.wait_ge(s_evS.h, evS_need)
            for oc in range(n_oc):
                ps = b.psA if oc % 2 == 0 else b.psB
                t.wait_ge(s_wr[oc % 3].h, lw_vals[oc])
                if oc >= 2:
                    t.wait_ge(ev_sem.h, ev0 + oc - 1)
                wv = b.wring.ap()[:, oc % 3].rearrange("p (cc o) -> p cc o", o=128)
                for cc in range(CC):
                    for tt in range(NT):
                        ins = t.matmul(ps.ap()[:, tt], wv[:, cc],
                                       rhs_slot.ap()[:, cc, 1 + tt * TW:1 + (tt + 1) * TW],
                                       start=(cc == 0), stop=(cc == CC - 1))
                ins.then_inc(s_mm.h, 1)
        s_mm.n += n_oc

        for oc in range(n_oc):
            evac(oc, b.psA if oc % 2 == 0 else b.psB, mm0 + oc + 1)

    def spill(oc, dst_dram, ready_sem, ready_val):
        @blk.scalar
        def _(s):
            s.wait_ge(ready_sem.h, ready_val)
            ins = s.dma_start(out=dst_dram[oc], in_=b.tmp1.ap()[:, oc % 3])
            s_t1[oc % 3].inc(ins, 16)

    def t1_free_wait(eng, oc):
        if oc >= 3:
            eng.wait_ge(s_t1[oc % 3].h, s_t1[oc % 3].n + 16)

    def t1_vals():
        return [s.n for s in s_t1]

    stat_pass(x_in, "g")
    rows_from_stats()
    bcast_rows(0, 1)

    bcB = s_bc.n
    mmB0 = s_mm.n
    sqB0 = s_s.n
    bldB0 = s_v.n
    ldB_vals = []
    t1B = t1_vals()

    vB_need, sB_need = s_v.n, s_s.n

    @blk.gpsimd
    def _(g):
        g.wait_ge(s_bc.h, bcB)
        g.wait_ge(s_v.h, vB_need)
        g.wait_ge(s_s.h, sB_need)
        g.wait_ge(s_mm.h, mmB0)
        for ci in range(CC):
            if ci >= 4:
                g.wait_ge(s_v.h, bldB0 + ci - 3)
            ins = g.dma_start(out=b.sring.ap()[:, ci % 4], in_=x_in[ci])
            s_srg[ci % 4].inc(ins, 16)
            ldB_vals.append(s_srg[ci % 4].n)

    @blk.vector
    def _(v):
        for ci in range(CC):
            v.wait_ge(s_srg[ci % 4].h, ldB_vals[ci])
            if ci >= 3:
                v.wait_ge(s_mm.h, mmB0 + ci - 2)
                v.wait_ge(s_t1[ci % 3].h, t1B[ci % 3] + 16)
                t1B[ci % 3] += 16
                v.wait_ge(s_s.h, sqB0 + ci - 2)
            ins = v.tensor_tensor(b.tmp1.ap()[:, ci % 3], b.sring.ap()[:, ci % 4], b.aB[:], ALU.mult)
            s_cv.inc(ins, 1)
            s_cv.wait(v)
            ins = v.tensor_tensor(b.tmp1.ap()[:, ci % 3], b.tmp1.ap()[:, ci % 3], b.bB[:], ALU.add)
            s_v.inc(ins, 1)

    @blk.scalar
    def _(s):
        for ci in range(CC):
            s.wait_ge(s_v.h, bldB0 + ci + 1)
            if ci >= 2:
                s.wait_ge(s_mm.h, mmB0 + ci - 1)
            zact(s, b, b.tmp2.ap()[:, ci % 2], b.tmp1.ap()[:, ci % 3], AF.Square).then_inc(s_s.h, 1)
            s_s.n += 1
            ins = s.dma_start(out=sp_n0[ci], in_=b.tmp1.ap()[:, ci % 3])
            s_t1[ci % 3].inc(ins, 16)
    spB_done = t1_vals()

    evV0, evS0 = s_evV.n, s_evS.n

    @blk.tensor
    def _(t):
        t.wait_ge(s_evV.h, evV0)
        t.wait_ge(s_evS.h, evS0)
        for ci in range(CC):
            t.wait_ge(s_s.h, sqB0 + ci + 1)
            for tt in range(NT):
                t.matmul(b.psA.ap()[0:1, tt], b.ones_bf[:],
                         b.tmp1.ap()[:, ci % 3, tt * TW:(tt + 1) * TW],
                         start=(ci == 0), stop=(ci == CC - 1))
                ins = t.matmul(b.psB.ap()[0:1, tt], b.ones_bf[:],
                               b.tmp2.ap()[:, ci % 2, tt * TW:(tt + 1) * TW],
                               start=(ci == 0), stop=(ci == CC - 1))
            ins.then_inc(s_mm.h, 1)
    s_mm.n += CC

    rows_from_stats()
    bcast_rows(2, 3)

    ln_apply(sp_n0, b.slotA, V["ln1_g"], V["ln1_b"],
             [(s, v) for s, v in zip(s_t1, spB_done)])

    bldk = build_mix(V["mu_k"], b.slotA, b.slotB)

    def evac_k(oc, ps, mmv):
        t1v = s_t1[oc % 3].n

        @blk.scalar
        def _(s):
            s.wait_ge(s_mm.h, mmv)
            s.wait_ge(s_t1[oc % 3].h, t1v)
            ins = zact(s, b, b.tmp1.ap()[:, oc % 3], psflat(ps), AF.Sigmoid,
                               bias=vecs_sb.ap()[:, V["u18"], oc:oc + 1])
            s_evS.inc(ins, 1)
        spill(oc, sp_z, s_evS, s_evS.n)

    proj_pass(ctx.dram["l0a_wk"], b.slotB, s_evS, evac_k, gate=[(s_v, bldk)])
    spz_done = t1_vals()

    bldv = build_mix(V["mu_v"], b.slotA, b.slotB)
    rcv_vals = []
    evVv0 = s_evV.n

    vE, sE, mE = s_v.n, s_s.n, s_mm.n

    @blk.sync
    def _(sy):
        for sem, val in zip(s_t1, spz_done):
            sy.wait_ge(sem.h, val)
        sy.wait_ge(s_v.h, vE)
        sy.wait_ge(s_s.h, sE)
        sy.wait_ge(s_mm.h, mE)
        for oc in range(CC):
            if oc >= 4:
                sy.wait_ge(s_evV.h, evVv0 + oc - 3)
            ins = sy.dma_start(out=b.sring.ap()[:, oc % 4], in_=sp_z[oc])
            s_srs[oc % 4].inc(ins, 16)
            rcv_vals.append(s_srs[oc % 4].n)

    def evac_v(oc, ps, mmv):
        t1v = s_t1[oc % 3].n

        @blk.vector
        def _(v):
            v.wait_ge(s_mm.h, mmv)
            v.wait_ge(s_srs[oc % 4].h, rcv_vals[oc])
            v.wait_ge(s_t1[oc % 3].h, t1v)
            ins = v.tensor_tensor(b.tmp1.ap()[:, oc % 3], b.sring.ap()[:, oc % 4], psflat(ps), ALU.mult)
            s_evV.inc(ins, 1)
        spill(oc, sp_wkv, s_evV, s_evV.n)

    proj_pass(ctx.dram["l0a_wv"], b.slotB, s_evV, evac_v, gate=[(s_v, bldv)])
    spwkv_done = t1_vals()

    bldr = build_mix(V["mu_r"], b.slotA, b.slotB)
    rcr_vals = []
    vr0 = s_v.n
    evSr0 = s_evS.n

    vE, sE, mE = s_v.n, s_s.n, s_mm.n

    @blk.sync
    def _(sy):
        for sem, val in zip(s_t1, spwkv_done):
            sy.wait_ge(sem.h, val)
        sy.wait_ge(s_v.h, vE)
        sy.wait_ge(s_s.h, sE)
        sy.wait_ge(s_mm.h, mE)
        for oc in range(CC):
            if oc >= 4:
                sy.wait_ge(s_v.h, vr0 + oc - 3)
            ins = sy.dma_start(out=b.sring.ap()[:, oc % 4], in_=sp_wkv[oc])
            s_srs[oc % 4].inc(ins, 16)
            rcr_vals.append(s_srs[oc % 4].n)

    def evac_r(oc, ps, mmv):
        t1v = s_t1[oc % 3].n

        @blk.scalar
        def _(s):
            s.wait_ge(s_mm.h, mmv)
            if oc >= 2:
                s.wait_ge(s_v.h, vr0 + oc - 1)
            ins = zact(s, b, b.tmp2.ap()[:, oc % 2], psflat(ps), AF.Sigmoid)
            s_evS.inc(ins, 1)

        @blk.vector
        def _(v):
            v.wait_ge(s_evS.h, evSr0 + oc + 1)
            v.wait_ge(s_srs[oc % 4].h, rcr_vals[oc])
            v.wait_ge(s_t1[oc % 3].h, t1v)
            ins = v.tensor_tensor(b.tmp1.ap()[:, oc % 3], b.tmp2.ap()[:, oc % 2],
                                  b.sring.ap()[:, oc % 4], ALU.mult)
            s_v.inc(ins, 1)
        spill(oc, sp_rwkv, s_v, s_v.n)

    proj_pass(ctx.dram["l0a_wr"], b.slotB, s_evS, evac_r, gate=[(s_v, bldr)])
    sprw_done = t1_vals()
    mm_wr_done = s_mm.n
    bld_wr = s_v.n

    @blk.sync
    def _(sy):
        sy.wait_ge(s_v.h, bld_wr)
        for sem, val in zip(s_t1, sprw_done):
            sy.wait_ge(sem.h, val)
        for cc in range(CC):
            sy.dma_start(out=b.slotB.ap()[:, cc, 1:1 + T], in_=sp_rwkv[cc]).then_inc(s_slot.h, 16)
    s_slot.n += 16 * CC
    rwkv_ready = s_slot.n

    rco_vals = []
    evVo0 = s_evV.n

    vE, sE, mE = s_v.n, s_s.n, s_mm.n

    @blk.sync
    def _(sy):
        sy.wait_ge(s_v.h, vE)
        sy.wait_ge(s_s.h, sE)
        sy.wait_ge(s_mm.h, mE)
        for oc in range(CC):
            if oc >= 4:
                sy.wait_ge(s_evV.h, evVo0 + oc - 3)
            ins = sy.dma_start(out=b.sring.ap()[:, oc % 4], in_=sp_n0[oc])
            s_srs[oc % 4].inc(ins, 16)
            rco_vals.append(s_srs[oc % 4].n)

    def evac_o(oc, ps, mmv):
        t1v = s_t1[oc % 3].n

        @blk.vector
        def _(v):
            v.wait_ge(s_mm.h, mmv)
            v.wait_ge(s_srs[oc % 4].h, rco_vals[oc])
            v.wait_ge(s_t1[oc % 3].h, t1v)
            ins = v.tensor_tensor(b.tmp1.ap()[:, oc % 3], b.sring.ap()[:, oc % 4], psflat(ps), ALU.add)
            s_evV.inc(ins, 1)
        spill(oc, sp_xmid, s_evV, s_evV.n)

    proj_pass(ctx.dram["l0a_wo"], b.slotB, s_evV, evac_o, gate=[(s_slot, rwkv_ready)])
    spxm_done = t1_vals()

    stat_pass(sp_xmid, "g", gate=[(s, v) for s, v in zip(s_t1, spxm_done)])
    rows_from_stats()
    bcast_rows(4, 5)

    ln_apply(sp_xmid, b.slotA, V["ln2_g"], V["ln2_b"],
             [(s, v) for s, v in zip(s_t1, spxm_done)])

    bldk2 = build_mix(V["cmu_k"], b.slotA, b.slotB)
    sg0 = s_s.n
    evVg0 = s_evV.n

    def evac_k2(oc, ps, mmv):
        t1v = s_t1[oc % 3].n

        @blk.vector
        def _(v):
            v.wait_ge(s_mm.h, mmv)
            if oc >= 2:
                v.wait_ge(s_s.h, sg0 + oc - 1)
            ins = v.tensor_scalar(b.tmp2.ap()[:, oc % 2], psflat(ps), 0.0, None, ALU.max)
            s_evV.inc(ins, 1)

        @blk.scalar
        def _(s):
            s.wait_ge(s_evV.h, evVg0 + oc + 1)
            s.wait_ge(s_t1[oc % 3].h, t1v)
            ins = zact(s, b, b.tmp1.ap()[:, oc % 3], b.tmp2.ap()[:, oc % 2], AF.Square)
            s_s.inc(ins, 1)
        spill(oc, sp_k2, s_s, s_s.n)

    proj_pass(ctx.dram["l0a_cwk"], b.slotB, s_evV, evac_k2, gate=[(s_v, bldk2)])
    spk2_done = t1_vals()

    xr2_ready = build_mix(V["cmu_r"], b.slotA, b.slotB)

    @blk.sync
    def _(sy):
        sy.wait_ge(s_v.h, xr2_ready)
        for sem, val in zip(s_t1, spk2_done):
            sy.wait_ge(sem.h, val)
        for cc in range(CC):
            sy.dma_start(out=b.slotA.ap()[:, cc, 1:1 + T], in_=sp_k2[cc]).then_inc(s_slot.h, 16)
    s_slot.n += 16 * CC
    k2_ready = s_slot.n

    mmI0 = s_mm.n
    evSI0 = s_evS.n
    evVI0 = s_evV.n
    vI0 = s_v.n
    sI0 = s_s.n
    lwI_vals = []
    rcI_vals = []
    ldI_vals = []

    @blk.gpsimd
    def _(g):
        g.wait_ge(s_mm.h, mmI0)
        g.wait_ge(s_v.h, vI0)
        g.wait_ge(s_s.h, sI0)
        for oc in range(CC):
            if oc >= 1:
                g.wait_ge(s_mm.h, mmI0 + oc)
            ins = g.dma_start(out=b.wring.ap()[:, 0], in_=ctx.dram["l0a_cwv"][oc])
            s_wr[0].inc(ins, 16)
            ins = g.dma_start(out=b.wring.ap()[:, 1], in_=ctx.dram["l0a_cwr"][oc])
            s_wr[1].inc(ins, 16)
            lwI_vals.append((s_wr[0].n, s_wr[1].n))
            if oc >= 2:
                g.wait_ge(s_v.h, vI0 + oc - 1)
            ins = g.dma_start(out=b.sring.ap()[:, 2 + oc % 2], in_=x_in[oc])
            s_srg[2 + oc % 2].inc(ins, 16)
            ldI_vals.append(s_srg[2 + oc % 2].n)

    @blk.sync
    def _(sy):
        sy.wait_ge(s_mm.h, mmI0)
        sy.wait_ge(s_v.h, vI0)
        sy.wait_ge(s_s.h, sI0)
        for oc in range(CC):
            if oc >= 2:
                sy.wait_ge(s_v.h, vI0 + oc - 1)
            ins = sy.dma_start(out=b.sring.ap()[:, oc % 2], in_=sp_xmid[oc])
            s_srs[oc % 2].inc(ins, 16)
            rcI_vals.append(s_srs[oc % 2].n)

    @blk.tensor
    def _(t):
        t.wait_ge(s_slot.h, k2_ready)
        t.wait_ge(s_v.h, xr2_ready)
        t.wait_ge(s_evV.h, evVI0)
        t.wait_ge(s_evS.h, evSI0)
        for oc in range(CC):
            t.wait_ge(s_wr[0].h, lwI_vals[oc][0])
            t.wait_ge(s_wr[1].h, lwI_vals[oc][1])
            if oc >= 1:
                t.wait_ge(s_evV.h, evVI0 + oc)
                t.wait_ge(s_evS.h, evSI0 + oc)
            for cc in range(CC):
                for tt in range(NT):
                    t.matmul(b.psA.ap()[:, tt],
                             b.wring.ap()[:, 0].rearrange("p (cc o) -> p cc o", o=128)[:, cc],
                             b.slotA.ap()[:, cc, 1 + tt * TW:1 + (tt + 1) * TW],
                             start=(cc == 0), stop=(cc == CC - 1))
            for cc in range(CC):
                for tt in range(NT):
                    ins = t.matmul(b.psB.ap()[:, tt],
                                   b.wring.ap()[:, 1].rearrange("p (cc o) -> p cc o", o=128)[:, cc],
                                   b.slotB.ap()[:, cc, 1 + tt * TW:1 + (tt + 1) * TW],
                                   start=(cc == 0), stop=(cc == CC - 1))
            ins.then_inc(s_mm.h, 1)
    s_mm.n += CC

    for oc in range(CC):
        t1v = s_t1[oc % 3].n

        @blk.scalar
        def _(s, oc=oc):
            s.wait_ge(s_mm.h, mmI0 + oc + 1)
            if oc >= 2:
                s.wait_ge(s_evV.h, evVI0 + oc - 1)
            ins = zact(s, b, b.tmp2.ap()[:, oc % 2], psflat(b.psB), AF.Sigmoid)
            s_evS.inc(ins, 1)

        @blk.vector
        def _(v, oc=oc):
            v.wait_ge(s_evS.h, evSI0 + oc + 1)
            v.wait_ge(s_srs[oc % 2].h, rcI_vals[oc])
            v.wait_ge(s_srg[2 + oc % 2].h, ldI_vals[oc])
            v.wait_ge(s_t1[oc % 3].h, t1v)
            ins = v.tensor_tensor(b.tmp1.ap()[:, oc % 3], b.tmp2.ap()[:, oc % 2], psflat(b.psA), ALU.mult)
            s_evV.inc(ins, 1)
            s_evV.wait(v)
            ins = v.tensor_tensor(b.tmp1.ap()[:, oc % 3], b.tmp1.ap()[:, oc % 3],
                                  b.sring.ap()[:, oc % 2], ALU.add)
            s_cv.inc(ins, 1)
            s_cv.wait(v)
            ins = v.tensor_tensor(b.tmp1.ap()[:, oc % 3], b.tmp1.ap()[:, oc % 3],
                                  b.sring.ap()[:, 2 + oc % 2], ALU.add)
            s_v.inc(ins, 1)
        spill(oc, x_out, s_v, s_v.n)



def emit_l0_ffn(ctx, blk, b, x_in_name, x_out_name):
    x_in = ctx.dram[x_in_name]
    x_out = ctx.dram[x_out_name]
    sp_ymoe = ctx.dscratch("sp_ymoe", [CC, 128, T], BF16)
    sp_row2 = ctx.dscratch("sp_row_l0f", [4, T], F32)
    we_dram = ctx.dscratch("we_dram", [32, T], F32)

    s_mm = ctx.sem("s_mm")
    s_v = ctx.sem("s_v")
    s_s = ctx.sem("s_s")
    s_evV = ctx.sem("s_evV")
    s_evS = ctx.sem("s_evS")
    s_srg = [ctx.sem(f"s_srg{j}") for j in range(4)]
    s_srs = [ctx.sem(f"s_srs{j}") for j in range(4)]
    s_wr = [ctx.sem(f"s_wr{j}") for j in range(3)]
    s_t1 = [ctx.sem(f"s_t1{j}") for j in range(3)]
    s_bc = ctx.sem("s_bc")
    s_rsp = ctx.sem("s_rsp")
    s_cv = ctx.sem("s_cv")
    s_er = [ctx.sem(f"s_er{j}") for j in range(2)]
    if not hasattr(ctx, "dma_sems"):
        ctx.dma_sems = [s_bc, s_rsp] + s_srg + s_srs + s_wr + s_t1
    for x in s_er:
        if x not in ctx.dma_sems:
            ctx.dma_sems.append(x)

    fl32 = b.slotB.ap().rearrange("p a c -> p (a c)").bitcast(F32)

    def ch_bf(c):
        return b.slotB.ap()[:, c, 0:T]

    rg_sb, ru_sb, srg_sb, sru_sb = ch_bf(0), ch_bf(1), ch_bf(2), ch_bf(3)
    srg2_sb, sru2_sb, sh1_sb, sh2_sb = ch_bf(4), ch_bf(5), ch_bf(6), ch_bf(7)
    lgT_sb = fl32[:, 8 * 1026: 8 * 1026 + 2048]
    rtok_sb = fl32[:, 10 * 1026: 10 * 1026 + 512]
    weE_sb = fl32[:, 11 * 1026: 11 * 1026 + 2048]
    m8_sb = fl32[:, 13 * 1026: 13 * 1026 + 8]
    nm_sb = fl32[:, 13 * 1026 + 8: 13 * 1026 + 9]
    es_sb = fl32[:, 13 * 1026 + 16: 13 * 1026 + 48]
    esum_sb = fl32[:, 13 * 1026 + 48: 13 * 1026 + 49]
    et4_sb = fl32[:, 13 * 1026 + 56: 13 * 1026 + 60]
    den_sb = fl32[:, 13 * 1026 + 64: 13 * 1026 + 65]
    msk_sb = fl32[:, 13 * 1026 + 96: 13 * 1026 + 128]

    ag_sb = b.sring.ap()[:, 0, 0:1024]
    au_sb = b.sring.ap()[:, 0, 1024:2048]
    rd2_sb = b.sring.ap()[:, 2, 0:1024]
    web_sb = b.sring.ap()[:, 3].bitcast(F32)[:, 0:1024]

    def wsb(i):
        s4 = i % 4
        return b.slotB.ap()[:, 2 + s4 // 2, (s4 % 2) * 1024:(s4 % 2) * 1024 + 1024]

    ering = [b.aB.ap().bitcast(BF16)[:, 0:3200], b.bB.ap().bitcast(BF16)[:, 0:3200]]
    EOFF = {"ug": 0, "cg": 128, "uu": 1536, "cu": 1664, "ud": 3072}
    ident = ctx.ident_sb

    def stat_pass(src_ap):
        mm0 = s_mm.n
        sq0 = s_s.n
        ld_vals = []
        v_need, s_need = s_v.n, s_s.n

        @blk.gpsimd
        def _(g):
            g.wait_ge(s_v.h, v_need)
            g.wait_ge(s_s.h, s_need)
            g.wait_ge(s_mm.h, mm0)
            for ci in range(CC):
                if ci >= 4:
                    g.wait_ge(s_mm.h, mm0 + ci - 3)
                ins = g.dma_start(out=b.sring.ap()[:, ci % 4], in_=src_ap[ci])
                s_srg[ci % 4].inc(ins, 16)
                ld_vals.append(s_srg[ci % 4].n)

        @blk.scalar
        def _(s):
            for ci in range(CC):
                s.wait_ge(s_srg[ci % 4].h, ld_vals[ci])
                if ci >= 2:
                    s.wait_ge(s_mm.h, mm0 + ci - 1)
                zact(s, b, b.tmp2.ap()[:, ci % 2], b.sring.ap()[:, ci % 4], AF.Square).then_inc(s_s.h, 1)
                s_s.n += 1

        evV0, evS0 = s_evV.n, s_evS.n

        @blk.tensor
        def _(t):
            t.wait_ge(s_evV.h, evV0)
            t.wait_ge(s_evS.h, evS0)
            for ci in range(CC):
                t.wait_ge(s_s.h, sq0 + ci + 1)
                for tt in range(NT):
                    t.matmul(b.psA.ap()[0:1, tt], b.ones_bf[:],
                             b.sring.ap()[:, ci % 4, tt * TW:(tt + 1) * TW],
                             start=(ci == 0), stop=(ci == CC - 1))
                    ins = t.matmul(b.psB.ap()[0:1, tt], b.ones_bf[:],
                                   b.tmp2.ap()[:, ci % 2, tt * TW:(tt + 1) * TW],
                                   start=(ci == 0), stop=(ci == CC - 1))
                ins.then_inc(s_mm.h, 1)
        s_mm.n += CC

    def rows_from_stats():
        r0 = b.aB.ap()[0:1, :]
        r1 = b.bB.ap()[0:1, :]

        @blk.vector
        def _(v):
            s_mm.wait(v)
            s_bc.wait(v)
            for tt in range(NT):
                ins = v.tensor_scalar(r1[:, tt * TW:(tt + 1) * TW], b.psA.ap()[0:1, tt], 1.0 / D, None, ALU.mult)
            s_v.inc(ins, 1)

        @blk.scalar
        def _(s):
            s_v.wait(s)
            ins = zact(s, b, r0, r1, AF.Square)
            s_s.inc(ins, 1)

        @blk.vector
        def _(v):
            s_s.wait(v)
            for tt in range(NT):
                ins = v.tensor_scalar(r1[:, tt * TW:(tt + 1) * TW], b.psB.ap()[0:1, tt], 1.0 / D, None, ALU.mult)
            s_cv.inc(ins, 1)
            s_cv.wait(v)
            ins = v.tensor_tensor(r1, r1, r0, ALU.subtract)
            s_cv.inc(ins, 1)
            s_cv.wait(v)
            ins = v.tensor_scalar(r1, r1, 1.0, LN_EPS, ALU.mult, ALU.add)
            s_cv.inc(ins, 1)
            s_cv.wait(v)
            ins = v.reciprocal(r0, r1)
            s_v.inc(ins, 1)

        @blk.scalar
        def _(s):
            s_v.wait(s)
            ins = zact(s, b, r0, r0, AF.Sqrt)
            s_s.inc(ins, 1)

        @blk.vector
        def _(v):
            s_s.wait(v)
            for tt in range(NT):
                ins = v.tensor_scalar(r1[:, tt * TW:(tt + 1) * TW], b.psA.ap()[0:1, tt], -1.0 / D, None, ALU.mult)
            s_cv.inc(ins, 1)
            s_cv.wait(v)
            ins = v.tensor_tensor(r1, r1, r0, ALU.mult)
            s_evV.inc(ins, 1)

    def bcast_rows(slot_a, slot_b):
        row_done = s_evV.n

        @blk.scalar
        def _(s):
            s.wait_ge(s_evV.h, row_done)
            s.dma_start(out=sp_row2[slot_a:slot_a + 1], in_=b.aB.ap()[0:1, :]).then_inc(s_rsp.h, 16)
            s.dma_start(out=sp_row2[slot_b:slot_b + 1], in_=b.bB.ap()[0:1, :]).then_inc(s_rsp.h, 16)
        s_rsp.n += 32

        mm_need_bc = s_mm.n

        @blk.sync
        def _(sy):
            s_rsp.wait(sy)
            sy.wait_ge(s_mm.h, mm_need_bc)
            sy.dma_start(out=b.aB[:], in_=sp_row2[slot_a].unsqueeze(0).broadcast_to([128, T])).then_inc(s_bc.h, 16)
            sy.dma_start(out=b.bB[:], in_=sp_row2[slot_b].unsqueeze(0).broadcast_to([128, T])).then_inc(s_bc.h, 16)
        s_bc.n += 32

    def ln_apply_plain(src_dram, dst_slot):
        bc_need = s_bc.n
        mm_need = s_mm.n
        bld0 = s_v.n
        ss_need = s_s.n
        ld_vals = []

        @blk.sync
        def _(sy):
            sy.wait_ge(s_v.h, bld0)
            sy.wait_ge(s_s.h, ss_need)
            sy.wait_ge(s_mm.h, mm_need)
            for ci in range(CC):
                if ci >= 4:
                    sy.wait_ge(s_v.h, bld0 + ci - 3)
                ins = sy.dma_start(out=b.sring.ap()[:, ci % 4], in_=src_dram[ci])
                s_srs[ci % 4].inc(ins, 16)
                ld_vals.append(s_srs[ci % 4].n)

        @blk.vector
        def _(v):
            v.wait_ge(s_bc.h, bc_need)
            v.wait_ge(s_mm.h, mm_need)
            v.wait_ge(s_s.h, ss_need)
            for ci in range(CC):
                v.wait_ge(s_srs[ci % 4].h, ld_vals[ci])
                v.wait_ge(s_v.h, s_v.n)
                ins = v.tensor_tensor(b.tmp2.ap()[:, 0], b.sring.ap()[:, ci % 4], b.aB[:], ALU.mult)
                s_cv.inc(ins, 1)
                s_cv.wait(v)
                ins = v.tensor_tensor(dst_slot.ap()[:, ci, 1:1 + T], b.tmp2.ap()[:, 0], b.bB[:], ALU.add)
                s_v.inc(ins, 1)

    def proj2(w_ap, n_oc, n_cc, rhs_fn, evac, gate=None, o_width=128):
        ev_sem = s_evS
        ev0 = ev_sem.n
        mm0 = s_mm.n
        evV_need, evS_need = s_evV.n, s_evS.n
        lw_vals = []

        @blk.gpsimd
        def _(g):
            g.wait_ge(s_mm.h, mm0)
            for oc in range(n_oc):
                if oc >= 3:
                    g.wait_ge(ev_sem.h, ev0 + oc - 2)
                ins = g.dma_start(out=b.wring.ap()[:, oc % 3, 0:n_cc * o_width], in_=w_ap[oc])
                s_wr[oc % 3].inc(ins, 16)
                lw_vals.append(s_wr[oc % 3].n)

        @blk.tensor
        def _(t):
            if gate is not None:
                for sem, val in gate:
                    t.wait_ge(sem.h, val)
            t.wait_ge(s_evV.h, evV_need)
            t.wait_ge(s_evS.h, evS_need)
            for oc in range(n_oc):
                ps = b.psA if oc % 2 == 0 else b.psB
                t.wait_ge(s_wr[oc % 3].h, lw_vals[oc])
                if oc >= 2:
                    t.wait_ge(ev_sem.h, ev0 + oc - 1)
                wv = b.wring.ap()[:, oc % 3, 0:n_cc * o_width].rearrange("p (cc o) -> p cc o", o=o_width)
                for cc in range(n_cc):
                    for tt in range(NT):
                        ins = t.matmul(ps.ap()[0:o_width, tt], wv[:, cc], rhs_fn(cc, tt),
                                       start=(cc == 0), stop=(cc == n_cc - 1))
                ins.then_inc(s_mm.h, 1)
        s_mm.n += n_oc

        for oc in range(n_oc):
            evac(oc, b.psA if oc % 2 == 0 else b.psB, mm0 + oc + 1)

    def n1_rhs(cc, tt):
        return b.slotA.ap()[:, cc, 1 + tt * TW:1 + (tt + 1) * TW]

    def mk_copy_evac(dst_ap, nrows=128):
        def evac(oc, ps, mmv):
            @blk.scalar
            def _(s):
                s.wait_ge(s_mm.h, mmv)
                ins = zcopy(s, b, dst_ap, psflat(ps)[0:nrows, :])
                s_evS.inc(ins, 1)
        return evac

    t1_entry = [s.n for s in s_t1]

    @blk.gpsimd
    def _(g):
        for sem, val in zip(s_t1, t1_entry):
            g.wait_ge(sem.h, val)

    @blk.sync
    def _(sy):
        for sem, val in zip(s_t1, t1_entry):
            sy.wait_ge(sem.h, val)

    stat_pass(x_in)
    rows_from_stats()
    bcast_rows(0, 1)
    ln_apply_plain(x_in, b.slotA)
    n1_done = s_v.n

    proj2(ctx.dram["l0f_rg"], 1, CC, n1_rhs, mk_copy_evac(rg_sb), gate=[(s_v, n1_done)])
    proj2(ctx.dram["l0f_ru"], 1, CC, n1_rhs, mk_copy_evac(ru_sb))
    proj2(ctx.dram["l0f_srg"], 1, CC, n1_rhs, mk_copy_evac(srg_sb))
    proj2(ctx.dram["l0f_sru"], 1, CC, n1_rhs, mk_copy_evac(sru_sb))
    proj2(ctx.dram["l0f_gate"], 1, CC, n1_rhs, mk_copy_evac(lgT_sb[0:32, :], nrows=32), o_width=32)
    proj2(ctx.dram["l0f_sug"], 1, 1, lambda cc, tt: srg_sb[:, tt * TW:(tt + 1) * TW],
          mk_copy_evac(srg2_sb), gate=[(s_evS, s_evS.n)])
    proj2(ctx.dram["l0f_suu"], 1, 1, lambda cc, tt: sru_sb[:, tt * TW:(tt + 1) * TW],
          mk_copy_evac(sru2_sb), gate=[(s_evS, s_evS.n)])

    lg_done = s_evS.n
    mmR0 = s_mm.n
    evV_R = s_evV.n
    vbaseR = s_v.n

    @blk.tensor
    def _(t):
        t.wait_ge(s_evS.h, lg_done)
        t.wait_ge(s_evV.h, evV_R)
        for i in range(CC):
            if i >= 1:
                t.wait_ge(s_v.h, vbaseR + i)
            ins = t.transpose(b.psB.ap()[:, 0, 0:32], lgT_sb[0:32, i * 128:(i + 1) * 128],
                              ident[0:32, 0:32])
            ins.then_inc(s_mm.h, 1)
    s_mm.n += CC

    @blk.vector
    def _(v):
        for i in range(CC):
            v.wait_ge(s_mm.h, mmR0 + i + 1)
            ins = v.tensor_copy(rtok_sb[:, i * 32:(i + 1) * 32], b.psB.ap()[:, 0, 0:32])
            s_v.inc(ins, 1)
    rtok_done = s_v.n

    sR0 = s_s.n
    for i in range(CC):
        @blk.vector
        def _(v, i=i):
            v.wait_ge(s_v.h, rtok_done)
            if i >= 1:
                v.wait_ge(s_s.h, sR0 + 2 * i)
            rt = rtok_sb[:, i * 32:(i + 1) * 32]
            ins = v.max(m8_sb[:, 0:8], rt)
            s_cv.inc(ins, 1)
            s_cv.wait(v)
            ins = v.tensor_scalar(nm_sb, m8_sb[:, 0:1], -1.0, None, ALU.mult)
            s_v.inc(ins, 1)

        @blk.scalar
        def _(s, i=i):
            s_v.wait(s)
            rt = rtok_sb[:, i * 32:(i + 1) * 32]
            ins = zact(s, b, es_sb[:, 0:32], rt, AF.Exp, bias=nm_sb, accum_out=esum_sb)
            s_s.inc(ins, 1)
            ins = zact(s, b, et4_sb[:, 0:4], m8_sb[:, 0:4], AF.Exp, bias=nm_sb)
            s_s.inc(ins, 1)

        @blk.vector
        def _(v, i=i):
            s_s.wait(v)
            rt = rtok_sb[:, i * 32:(i + 1) * 32]
            ins = v.reduce_sum(den_sb, et4_sb[:, 0:4], axis=mybir.AxisListType.X)
            s_cv.inc(ins, 1)
            s_cv.wait(v)
            ins = v.scalar_tensor_tensor(den_sb, esum_sb, 1e-20, den_sb, ALU.mult, ALU.add)
            s_cv.inc(ins, 1)
            s_cv.wait(v)
            ins = v.reciprocal(den_sb, den_sb)
            s_cv.inc(ins, 1)
            s_cv.wait(v)
            ins = v.tensor_scalar(msk_sb[:, 0:32], rt, m8_sb[:, 3:4], None, ALU.is_ge)
            s_cv.inc(ins, 1)
            s_cv.wait(v)
            ins = v.tensor_scalar(es_sb[:, 0:32], es_sb[:, 0:32], den_sb, None, ALU.mult)
            s_cv.inc(ins, 1)
            s_cv.wait(v)
            ins = v.tensor_tensor(rtok_sb[:, i * 32:(i + 1) * 32], es_sb[:, 0:32], msk_sb[:, 0:32], ALU.mult)
            s_v.inc(ins, 1)

    we_done = s_v.n
    mmW0 = s_mm.n
    vbaseW = s_v.n

    @blk.tensor
    def _(t):
        t.wait_ge(s_v.h, we_done)
        for i in range(CC):
            if i >= 1:
                t.wait_ge(s_v.h, vbaseW + i)
            ins = t.transpose(b.psB.ap()[0:32, 0, 0:128], rtok_sb[:, i * 32:(i + 1) * 32],
                              ident[:, 0:128])
            ins.then_inc(s_mm.h, 1)
    s_mm.n += CC

    @blk.vector
    def _(v):
        for i in range(CC):
            v.wait_ge(s_mm.h, mmW0 + i + 1)
            ins = v.tensor_copy(weE_sb[0:32, i * 128:(i + 1) * 128], b.psB.ap()[0:32, 0, 0:128])
            s_v.inc(ins, 1)
    weE_done = s_v.n

    @blk.scalar
    def _(s):
        s.wait_ge(s_v.h, weE_done)
        s.dma_start(out=we_dram, in_=weE_sb[0:32, :]).then_inc(s_rsp.h, 16)
    s_rsp.n += 16
    we_spilled = s_rsp.n

    NSI = 22
    sCg, sCu, sRdT = ctx.dram["l0f_scg"], ctx.dram["l0f_scu"], ctx.dram["l0f_srd"]
    mmS0 = s_mm.n
    evV_S0 = s_evV.n
    evS_S0 = s_evS.n
    vS0 = s_v.n
    lwS_vals = []

    @blk.gpsimd
    def _(g):
        g.wait_ge(s_mm.h, mmS0)
        for si in range(NSI):
            if si >= 1:
                g.wait_ge(s_mm.h, mmS0 + 2 * NT * si)
            ins = g.dma_start(out=b.wring.ap()[:, 0, 0:128], in_=sCg[si])
            s_wr[0].inc(ins, 16)
            ins = g.dma_start(out=b.wring.ap()[:, 1, 0:128], in_=sCu[si])
            s_wr[1].inc(ins, 16)
            ins = g.dma_start(out=b.wring.ap()[:, 2, 0:128], in_=sRdT[0, :, si * 128:(si + 1) * 128])
            s_wr[2].inc(ins, 16)
            lwS_vals.append((s_wr[0].n, s_wr[1].n, s_wr[2].n))

    @blk.tensor
    def _(t):
        t.wait_ge(s_evV.h, evV_S0)
        t.wait_ge(s_evS.h, evS_S0)
        for si in range(NSI):
            t.wait_ge(s_wr[0].h, lwS_vals[si][0])
            t.wait_ge(s_wr[1].h, lwS_vals[si][1])
            t.wait_ge(s_wr[2].h, lwS_vals[si][2])
            for tq in range(NT):
                k = 4 * si + tq
                if k >= 2:
                    t.wait_ge(s_evS.h, evS_S0 + k - 1)
                    t.wait_ge(s_evV.h, evV_S0 + k - 1)
                t.matmul(b.psA.ap()[:, tq % 2], b.wring.ap()[:, 0, 0:128],
                         srg2_sb[:, tq * TW:(tq + 1) * TW], start=True, stop=True)
                ins = t.matmul(b.psA.ap()[:, 2 + tq % 2], b.wring.ap()[:, 1, 0:128],
                               sru2_sb[:, tq * TW:(tq + 1) * TW], start=True, stop=True)
                ins.then_inc(s_mm.h, 1)
                t.wait_ge(s_evV.h, evV_S0 + k + 1)
                ins = t.matmul(b.psB.ap()[:, tq], b.wring.ap()[:, 2, 0:128],
                               b.tmp1.ap()[:, k % 3, 0:TW],
                               start=(si == 0), stop=(si == NSI - 1))
                ins.then_inc(s_mm.h, 1)
    s_mm.n += NSI * NT * 2

    for si in range(NSI):
        for tq in range(NT):
            k = 4 * si + tq

            @blk.scalar
            def _(s, si=si, tq=tq, k=k):
                s.wait_ge(s_mm.h, mmS0 + 2 * k + 1)
                if k >= 2:
                    s.wait_ge(s_evV.h, evV_S0 + k - 1)
                ins = zact(s, b, b.tmp2.ap()[:, k % 2, 0:TW], b.psA.ap()[:, tq % 2], AF.Silu)
                s_evS.inc(ins, 1)

            @blk.vector
            def _(v, si=si, tq=tq, k=k):
                v.wait_ge(s_evS.h, evS_S0 + k + 1)
                if k >= 3:
                    v.wait_ge(s_mm.h, mmS0 + 2 * (k - 3) + 2)
                ins = v.tensor_tensor(b.tmp1.ap()[:, k % 3, 0:TW], b.tmp2.ap()[:, k % 2, 0:TW],
                                      b.psA.ap()[:, 2 + tq % 2], ALU.mult)
                s_evV.inc(ins, 1)

    shmm = s_mm.n

    @blk.scalar
    def _(s):
        s.wait_ge(s_mm.h, shmm)
        for tq in range(NT):
            ins = zcopy(s, b, sh1_sb[:, tq * TW:(tq + 1) * TW], b.psB.ap()[:, tq])
            s_evS.inc(ins, 1)
    sh1_done = s_evS.n

    proj2(ctx.dram["l0f_sud"], 1, 1, lambda cc, tt: sh1_sb[:, tt * TW:(tt + 1) * TW],
          mk_copy_evac(sh2_sb), gate=[(s_evS, sh1_done)])
    sh2_done = s_evS.n

    rdt = ctx.dram["l0f_rdt"]
    epack = ctx.dram["l0f_epack"]
    cd2 = ctx.dram["l0f_cdpack"]
    mm_pre = s_mm.n
    v_pre, s_pre = s_v.n, s_s.n

    rdt_sb = ctx.nc.alloc_sbuf_tensor("rdt_sb", [128, 1408], BF16)

    @blk.gpsimd
    def _(g):
        g.wait_ge(s_mm.h, mm_pre)
        ins = g.dma_start(out=rdt_sb[:], in_=rdt[0])
        s_wr[2].inc(ins, 16)
    rdt_ready = s_wr[2].n

    ad_slot = lambda e, q: b.slotA.ap()[:, e // 2, (e % 2) * 1024 + q * TW:(e % 2) * 1024 + (q + 1) * TW]

    _torder = [("agau", -1)]
    for _i in range(11):
        _torder.append(("gu", _i))
        if _i >= 2:
            _torder.append(("rd", _i - 2))
    _torder += [("rd", 9), ("rd", 10), ("ad", -1)]
    TM = {op: ix + 1 for ix, op in enumerate(_torder)}
    _vorder = [("agc", -1), ("auc", -1), ("rd2", -1), ("adc", -1)]
    VM = {op: ix + 1 for ix, op in enumerate(_vorder)}
    NV = len(_vorder)
    NT_E = len(_torder)
    tmp2fl = b.tmp2.ap().rearrange("p a c -> p (a c)")
    s_p = ctx.sem("s_p")
    s_wm = ctx.sem("s_wm")
    psAflat = psflat(b.psA)
    psBflat = psflat(b.psB)

    def gsl(i):
        return tmp2fl[:, (i % 4) * 1024:(i % 4) * 1024 + 1024]

    mm_after_expert = []
    for half in range(2):
        t0 = half * 1024
        for e in range(32):
            eid = half * 32 + e
            evV_entry = s_evV.n
            bc_pre = s_bc.n
            rsp_pre = s_rsp.n

            @blk.gpsimd
            def _(g, e=e, eid=eid, t0=t0, evV_entry=evV_entry, bc_pre=bc_pre, rsp_pre=rsp_pre):
                if eid >= 2:
                    g.wait_ge(s_mm.h, mm_after_expert[-2])
                elif eid <= 1:
                    g.wait_ge(s_v.h, v_pre)
                    g.wait_ge(s_s.h, s_pre)
                    g.wait_ge(s_mm.h, mm_pre)
                    g.wait_ge(s_bc.h, bc_pre)
                    g.wait_ge(s_rsp.h, rsp_pre)
                ins = g.dma_start(out=ering[eid % 2], in_=epack[e])
                s_er[eid % 2].inc(ins, 16)
                g.wait_ge(s_evV.h, evV_entry)
                g.wait_ge(s_rsp.h, we_spilled)
                ins = g.dma_start(out=web_sb,
                                  in_=we_dram[e, t0:t0 + 1024].unsqueeze(0).broadcast_to([128, 1024]))
                s_srg[3].inc(ins, 16)
            er_val = s_er[eid % 2].n
            web_val = s_srg[3].n

            mme0 = s_mm.n
            evSe0 = s_evS.n
            evVe0 = s_evV.n
            pe0 = s_p.n
            wme0 = s_wm.n

            @blk.tensor
            def _(t, e=e, eid=eid, t0=t0, mme0=mme0, evSe0=evSe0, evVe0=evVe0, pe0=pe0,
                  wme0=wme0, er_val=er_val):
                t.wait_ge(s_er[eid % 2].h, er_val)
                t.wait_ge(s_wr[2].h, rdt_ready)
                if eid == 0:
                    t.wait_ge(s_mm.h, mm_pre)
                    t.wait_ge(s_evV.h, evVe0)
                    t.wait_ge(s_p.h, pe0)
                er = ering[eid % 2]
                t.wait_ge(s_evS.h, evSe0)
                for q in range(2):
                    t.matmul(b.psA.ap()[:, q], er[:, EOFF["ug"]:EOFF["ug"] + 128],
                             rg_sb[:, t0 + q * TW:t0 + (q + 1) * TW], start=True, stop=True)
                for q in range(2):
                    ins = t.matmul(b.psA.ap()[:, 2 + q], er[:, EOFF["uu"]:EOFF["uu"] + 128],
                                   ru_sb[:, t0 + q * TW:t0 + (q + 1) * TW], start=True, stop=True)
                ins.then_inc(s_mm.h, 1)
                for i in range(11):
                    if i == 0:
                        t.wait_ge(s_evV.h, evVe0 + VM[("agc", -1)])
                    elif i == 1:
                        t.wait_ge(s_evV.h, evVe0 + VM[("auc", -1)])
                    else:
                        t.wait_ge(s_evS.h, evSe0 + i - 1)
                    for q in range(2):
                        t.matmul(b.psA.ap()[:, 2 * (i % 2) + q],
                                 er[:, EOFF["cg"] + i * 128:EOFF["cg"] + (i + 1) * 128],
                                 ag_sb[:, q * TW:(q + 1) * TW], start=True, stop=True)
                    if i == 0:
                        t.wait_ge(s_evV.h, evVe0 + VM[("auc", -1)])
                    else:
                        t.wait_ge(s_p.h, pe0 + i)
                    for q in range(2):
                        ins = t.matmul(b.psB.ap()[:, 2 + q],
                                       er[:, EOFF["cu"] + i * 128:EOFF["cu"] + (i + 1) * 128],
                                       au_sb[:, q * TW:(q + 1) * TW], start=True, stop=True)
                    ins.then_inc(s_mm.h, 1)
                    if i >= 2:
                        j = i - 2
                        t.wait_ge(s_wm.h, wme0 + j + 1)
                        for q in range(2):
                            ins = t.matmul(b.psB.ap()[:, q], rdt_sb.ap()[:, j * 128:(j + 1) * 128],
                                           wsb(j)[:, q * TW:(q + 1) * TW],
                                           start=(j == 0), stop=False)
                        ins.then_inc(s_mm.h, 1)
                for j in (9, 10):
                    t.wait_ge(s_wm.h, wme0 + j + 1)
                    for q in range(2):
                        ins = t.matmul(b.psB.ap()[:, q], rdt_sb.ap()[:, j * 128:(j + 1) * 128],
                                       wsb(j)[:, q * TW:(q + 1) * TW],
                                       start=False, stop=(j == 10))
                    ins.then_inc(s_mm.h, 1)
                t.wait_ge(s_evV.h, evVe0 + VM[("rd2", -1)])
                t.wait_ge(s_p.h, pe0 + 11)
                for q in range(2):
                    ins = t.matmul(b.psB.ap()[:, 2 + q], er[:, EOFF["ud"]:EOFF["ud"] + 128],
                                   rd2_sb[:, q * TW:(q + 1) * TW], start=True, stop=True)
                ins.then_inc(s_mm.h, 1)
            s_mm.n += NT_E
            mm_after_expert.append(s_mm.n)

            for i in range(11):
                @blk.scalar
                def _(s, i=i, eid=eid, mme0=mme0, wme0=wme0):
                    s.wait_ge(s_mm.h, mme0 + TM[("gu", i)])
                    if i >= 4:
                        s.wait_ge(s_wm.h, wme0 + i - 3)
                    elif eid > 0:
                        s.wait_ge(s_wm.h, wme0 - 11 + 8 + i)
                    ins = zact(s, b, gsl(i), psAflat[:, (i % 2) * 1024:(i % 2) * 1024 + 1024], AF.Silu)
                    s_evS.inc(ins, 1)

            @blk.vector
            def _(v, mme0=mme0):
                v.wait_ge(s_mm.h, mme0 + TM[("agau", -1)])
                ins = v.tensor_copy(ag_sb, psAflat[:, 0:1024])
                s_evV.inc(ins, 1)
                ins = v.tensor_copy(au_sb, psAflat[:, 1024:2048])
                s_evV.inc(ins, 1)

            def emit_uc(i, mme0=mme0):
                @blk.vector
                def _(v, i=i, mme0=mme0):
                    v.wait_ge(s_mm.h, mme0 + TM[("gu", i)])
                    ins = v.tensor_copy(wsb(i), psBflat[:, 1024:2048])
                    s_p.inc(ins, 1)

            def emit_wmul(i, evSe0=evSe0, pe0=pe0):
                @blk.vector
                def _(v, i=i, evSe0=evSe0, pe0=pe0):
                    v.wait_ge(s_evS.h, evSe0 + i + 1)
                    v.wait_ge(s_p.h, pe0 + i + 1)
                    ins = v.tensor_tensor(wsb(i), wsb(i), gsl(i), ALU.mult)
                    s_wm.inc(ins, 1)

            emit_uc(0)
            for i in range(1, 11):
                emit_uc(i)
                emit_wmul(i - 1)
            emit_wmul(10)

            @blk.vector
            def _(v, mme0=mme0, web_val=web_val):
                v.wait_ge(s_mm.h, mme0 + TM[("rd", 10)])
                v.wait_ge(s_srg[3].h, web_val)
                ins = v.tensor_tensor(rd2_sb, psBflat[:, 0:1024], web_sb, ALU.mult)
                s_evV.inc(ins, 1)

            @blk.vector
            def _(v, e=e, mme0=mme0):
                v.wait_ge(s_mm.h, mme0 + TM[("ad", -1)])
                ins = v.tensor_copy(b.slotA.ap()[:, e // 2, (e % 2) * 1024:(e % 2) * 1024 + 1024],
                                    psBflat[:, 1024:2048])
                s_evV.inc(ins, 1)

        ad_ready = s_evS.n
        mmy_entry = s_mm.n
        grp_mark = []
        grp_load = []
        for oc in range(CC):
            for gq in range(2):
                gidx = oc * 2 + gq
                slot = gidx % 3
                gate_val = grp_mark[gidx - 3] if gidx >= 3 else mmy_entry

                @blk.gpsimd
                def _(g, oc=oc, gq=gq, slot=slot, gate_val=gate_val):
                    g.wait_ge(s_mm.h, gate_val)
                    ins = g.dma_start(out=b.wring.ap()[:, slot], in_=cd2[oc, gq])
                    s_wr[slot].inc(ins, 16)
                grp_load.append((slot, s_wr[slot].n))
                grp_mark.append(mmy_entry + gidx + 1)

            evy_entry = s_evV.n

            @blk.tensor
            def _(t, oc=oc, evy_entry=evy_entry, gl=list(grp_load[-2:])):
                t.wait_ge(s_evS.h, ad_ready)
                t.wait_ge(s_evV.h, evy_entry)
                ps = b.psA if oc % 2 == 0 else b.psB
                for gq in range(2):
                    slot, lv = gl[gq]
                    t.wait_ge(s_wr[slot].h, lv)
                    for el in range(16):
                        e = gq * 16 + el
                        for q in range(2):
                            ins = t.matmul(ps.ap()[:, q], b.wring.ap()[:, slot, el * 128:(el + 1) * 128],
                                           b.slotA.ap()[:, e // 2, (e % 2) * 1024 + q * TW:(e % 2) * 1024 + (q + 1) * TW],
                                           start=(e == 0), stop=(e == 31))
                    ins.then_inc(s_mm.h, 1)
            s_mm.n += 2

            t1v = s_t1[oc % 3].n
            mmy_done = s_mm.n

            @blk.vector
            def _(v, oc=oc, t1v=t1v, mmy_done=mmy_done):
                v.wait_ge(s_mm.h, mmy_done)
                v.wait_ge(s_t1[oc % 3].h, t1v)
                ps = b.psA if oc % 2 == 0 else b.psB
                ins = v.tensor_copy(b.tmp1.ap()[:, oc % 3, 0:1024],
                                    ps.ap().rearrange("p a c -> p (a c)")[:, 0:1024])
                s_evV.inc(ins, 1)

            evy_done = s_evV.n

            @blk.scalar
            def _(s, oc=oc, half=half, evy_done=evy_done):
                s.wait_ge(s_evV.h, evy_done)
                ins = s.dma_start(out=sp_ymoe[oc][:, half * 1024:(half + 1) * 1024],
                                  in_=b.tmp1.ap()[:, oc % 3, 0:1024])
                s_t1[oc % 3].inc(ins, 16)

    ymoe_done = [s.n for s in s_t1]

    sCd = ctx.dram["l0f_scd"]
    mmF0 = s_mm.n
    evVF0, evSF0 = s_evV.n, s_evS.n
    vF0 = s_v.n
    lwF_vals = []
    rcF_vals = []
    ldF_vals = []

    vpreF, spreF = s_v.n, s_s.n

    @blk.gpsimd
    def _(g):
        g.wait_ge(s_mm.h, mmF0)
        g.wait_ge(s_v.h, vpreF)
        g.wait_ge(s_s.h, spreF)
        for oc in range(CC):
            if oc >= 3:
                g.wait_ge(s_evV.h, evVF0 + oc - 2)
            ins = g.dma_start(out=b.wring.ap()[:, oc % 3, 0:128], in_=sCd[oc])
            s_wr[oc % 3].inc(ins, 16)
            lwF_vals.append(s_wr[oc % 3].n)
            if oc >= 2:
                g.wait_ge(s_v.h, vF0 + oc - 1)
            ins = g.dma_start(out=b.sring.ap()[:, 2 + oc % 2], in_=x_in[oc])
            s_srg[2 + oc % 2].inc(ins, 16)
            ldF_vals.append(s_srg[2 + oc % 2].n)

    @blk.sync
    def _(sy):
        for sem, val in zip(s_t1, ymoe_done):
            sy.wait_ge(sem.h, val)
        sy.wait_ge(s_v.h, vpreF)
        sy.wait_ge(s_s.h, spreF)
        sy.wait_ge(s_mm.h, mmF0)
        for oc in range(CC):
            if oc >= 2:
                sy.wait_ge(s_evV.h, evVF0 + oc - 1)
            ins = sy.dma_start(out=b.sring.ap()[:, oc % 2], in_=sp_ymoe[oc])
            s_srs[oc % 2].inc(ins, 16)
            rcF_vals.append(s_srs[oc % 2].n)

    @blk.tensor
    def _(t):
        t.wait_ge(s_evV.h, evVF0)
        t.wait_ge(s_evS.h, evSF0)
        for oc in range(CC):
            ps = b.psA if oc % 2 == 0 else b.psB
            t.wait_ge(s_wr[oc % 3].h, lwF_vals[oc])
            if oc >= 2:
                t.wait_ge(s_evV.h, evVF0 + oc - 1)
            for tt in range(NT):
                ins = t.matmul(ps.ap()[:, tt], b.wring.ap()[:, oc % 3, 0:128],
                               sh2_sb[:, tt * TW:(tt + 1) * TW], start=True, stop=True)
            ins.then_inc(s_mm.h, 1)
    s_mm.n += CC

    for oc in range(CC):
        t1v = s_t1[oc % 3].n

        @blk.vector
        def _(v, oc=oc, t1v=t1v):
            v.wait_ge(s_mm.h, mmF0 + oc + 1)
            v.wait_ge(s_srs[oc % 2].h, rcF_vals[oc])
            v.wait_ge(s_srg[2 + oc % 2].h, ldF_vals[oc])
            v.wait_ge(s_t1[oc % 3].h, t1v)
            ps = b.psA if oc % 2 == 0 else b.psB
            ins = v.tensor_tensor(b.tmp1.ap()[:, oc % 3], b.sring.ap()[:, oc % 2], psflat(ps), ALU.add)
            s_evV.inc(ins, 1)
            s_evV.wait(v)
            ins = v.tensor_tensor(b.tmp1.ap()[:, oc % 3], b.tmp1.ap()[:, oc % 3],
                                  b.sring.ap()[:, 2 + oc % 2], ALU.add)
            s_v.inc(ins, 1)

        @blk.scalar
        def _(s, oc=oc):
            s.wait_ge(s_v.h, s_v.n)
            ins = s.dma_start(out=x_out[oc], in_=b.tmp1.ap()[:, oc % 3])
            s_t1[oc % 3].inc(ins, 16)


def host_pack_l0f(inp):
    gw = np.asarray(inp["L0_ffn_gate_w"], np.float32)
    gpack = np.ascontiguousarray(gw.reshape(32, 16, 128).transpose(2, 1, 0)).reshape(1, 128, 512)
    Ug = np.asarray(inp["L0_ffn_Ug"], np.float32)
    Cg = np.asarray(inp["L0_ffn_Cg"], np.float32)
    Uu = np.asarray(inp["L0_ffn_Uu"], np.float32)
    Cu = np.asarray(inp["L0_ffn_Cu"], np.float32)
    Ud = np.asarray(inp["L0_ffn_Ud"], np.float32)
    Cd = np.asarray(inp["L0_ffn_Cd"], np.float32)
    epack = np.concatenate([Ug.transpose(0, 2, 1), Cg.transpose(0, 2, 1),
                            Uu.transpose(0, 2, 1), Cu.transpose(0, 2, 1),
                            Ud.transpose(0, 2, 1)], axis=2)
    epack = np.ascontiguousarray(epack)
    cdpack = Cd.transpose(0, 2, 1).reshape(32, 128, 16, 128).transpose(0, 2, 1, 3)
    cdpack = np.ascontiguousarray(
        cdpack.transpose(1, 0, 2, 3).reshape(16, 2, 16, 128, 128).transpose(0, 1, 3, 2, 4)
        .reshape(16, 2, 128, 2048))
    return {
        "l0f_rg": pack_wT(inp["L0_ffn_Rg"]),
        "l0f_ru": pack_wT(inp["L0_ffn_Ru"]),
        "l0f_srg": pack_wT(inp["L0_ffn_s_Rg"]),
        "l0f_sru": pack_wT(inp["L0_ffn_s_Ru"]),
        "l0f_gate": gpack,
        "l0f_sug": pack_wT(inp["L0_ffn_s_Ug"]),
        "l0f_suu": pack_wT(inp["L0_ffn_s_Uu"]),
        "l0f_sud": pack_wT(inp["L0_ffn_s_Ud"]),
        "l0f_scg": pack_wT(inp["L0_ffn_s_Cg"]),
        "l0f_scu": pack_wT(inp["L0_ffn_s_Cu"]),
        "l0f_srd": pack_wT(inp["L0_ffn_s_Rd"]),
        "l0f_scd": pack_wT(inp["L0_ffn_s_Cd"]),
        "l0f_rdt": pack_wT(inp["L0_ffn_Rd"]),
        "l0f_epack": epack,
        "l0f_cdpack": cdpack,
    }


def declare_l0f_inputs(ctx):
    ctx.din("l0f_rg", [1, 128, D], F32)
    ctx.din("l0f_ru", [1, 128, D], F32)
    ctx.din("l0f_srg", [1, 128, D], F32)
    ctx.din("l0f_sru", [1, 128, D], F32)
    ctx.din("l0f_gate", [1, 128, 512], F32)
    ctx.din("l0f_sug", [1, 128, 128], F32)
    ctx.din("l0f_suu", [1, 128, 128], F32)
    ctx.din("l0f_sud", [1, 128, 128], F32)
    ctx.din("l0f_scg", [22, 128, 128], F32)
    ctx.din("l0f_scu", [22, 128, 128], F32)
    ctx.din("l0f_srd", [1, 128, 2816], F32)
    ctx.din("l0f_scd", [16, 128, 128], F32)
    ctx.din("l0f_rdt", [1, 128, 1408], F32)
    ctx.din("l0f_epack", [32, 128, 3200], F32)
    ctx.din("l0f_cdpack", [16, 2, 128, 2048], F32)


def build_l0f_program():
    ctx = Ctx()
    nc = ctx.nc
    ctx.din("x_in", [CC, 128, T], BF16)
    declare_l0f_inputs(ctx)
    ctx.dout("x_out", [CC, 128, T], BF16)

    b = alloc_shared(ctx)
    ctx.ident_sb = nc.alloc_sbuf_tensor("ident_sb", [128, 128], F32).ap()
    ident_dram = nc.inline_tensor(np.eye(128, dtype=np.float32), "ident_const")
    s_init = ctx.sem("init")

    with nc.Block() as blk:
        @blk.sync
        def _(sy):
            sy.dma_start(out=ctx.ident_sb, in_=ident_dram.ap()).then_inc(s_init.h, 16)
        s_init.n += 16

        @blk.vector
        def _(v):
            ins = v.memset(b.ones_bf[:], 1.0)
            s_init.inc(ins, 1)
            ins = v.memset(b.zb[:], 0.0)
            s_init.inc(ins, 1)

        for eng in [blk.tensor, blk.gpsimd, blk.vector, blk.scalar, blk.sync]:
            def emit(e):
                s_init.wait(e)
            eng(emit)

        emit_l0_ffn(ctx, blk, b, "x_in", "x_out")

        @blk.sync
        def _(sy):
            for sem in ctx.dma_sems:
                sy.wait_ge(sem.h, sem.n)
    return ctx



def emit_l1_attn(ctx, blk, b, x_in_name, x_out_name):
    x_in = ctx.dram[x_in_name]
    x_out = ctx.dram[x_out_name]
    sp_cv = ctx.dscratch("sp_cv", [16, 64, T], BF16)
    sp_row3 = ctx.dscratch("sp_row_l1a", [8, T], F32)
    sp_rows8 = ctx.dscratch("sp_rows8", [24, T], F32)
    qs_dram = ctx.dscratch("qs_dram", [16, T], F32)
    l_dram = ctx.dscratch("l_dram", [16, T], F32)

    s_mm = ctx.sem("s_mm")
    s_v = ctx.sem("s_v")
    s_s = ctx.sem("s_s")
    s_evV = ctx.sem("s_evV")
    s_evS = ctx.sem("s_evS")
    s_srg = [ctx.sem(f"s_srg{j}") for j in range(4)]
    s_srs = [ctx.sem(f"s_srs{j}") for j in range(4)]
    s_wr = [ctx.sem(f"s_wr{j}") for j in range(3)]
    s_t1 = [ctx.sem(f"s_t1{j}") for j in range(3)]
    s_bc = ctx.sem("s_bc")
    s_rsp = ctx.sem("s_rsp")
    s_cv = ctx.sem("s_cv")
    s_qb = [ctx.sem("s_qb0"), ctx.sem("s_qb1")]
    if not hasattr(ctx, "dma_sems"):
        ctx.dma_sems = [s_bc, s_rsp] + s_srg + s_srs + s_wr + s_t1
    for x in s_qb:
        if x not in ctx.dma_sems:
            ctx.dma_sems.append(x)

    def ch_bf(c):
        return b.slotB.ap()[:, c, 0:T]

    kaug_sb = ch_bf(0)
    v_sb = ch_bf(1)
    def qlslot(hh):
        return b.slotB.ap()[:, 2 + hh, 0:T]
    qh_sb = ch_bf(10)
    def pt_tile(sc):
        return b.slotB.ap()[:, 11 + sc // 4, (sc % 4) * TW:(sc % 4 + 1) * TW]

    rowsF = ctx.rowsF
    r_qq = rowsF.ap()[0:1, :]
    r_qs = rowsF.ap()[32:33, :]
    r_sc = rowsF.ap()[64:65, :]
    scal = ctx.scal_sb
    kcol_sb = ctx.kcol_sb
    qsb_f32 = b.tmp2.ap()[:, 1].bitcast(F32)

    def czv(p):
        return b.sring.ap()[:, 0, p * TW:(p + 1) * TW]

    tmp1f32 = b.tmp1.ap().rearrange("p a c -> p (a c)").bitcast(F32)
    w_max = tmp1f32[32:33, 0:T]
    w_min = b.aB.ap()[32:33, :]
    scr2 = tmp1f32[64:65, 0:T]
    k32 = b.bB.ap()[32:33, :]
    m64 = b.aB.ap()[64:65, :]
    s0 = b.aB.ap()[0:1, :]
    band1 = ctx.band1_sb
    band2 = ctx.band2_sb

    def stat_pass(src_ap):
        mm0 = s_mm.n
        sq0 = s_s.n
        ld_vals = []
        v_need, s_need = s_v.n, s_s.n

        @blk.gpsimd
        def _(g):
            g.wait_ge(s_v.h, v_need)
            g.wait_ge(s_s.h, s_need)
            g.wait_ge(s_mm.h, mm0)
            for ci in range(CC):
                if ci >= 4:
                    g.wait_ge(s_mm.h, mm0 + ci - 3)
                ins = g.dma_start(out=b.sring.ap()[:, ci % 4], in_=src_ap[ci])
                s_srg[ci % 4].inc(ins, 16)
                ld_vals.append(s_srg[ci % 4].n)

        @blk.scalar
        def _(s):
            for ci in range(CC):
                s.wait_ge(s_srg[ci % 4].h, ld_vals[ci])
                if ci >= 2:
                    s.wait_ge(s_mm.h, mm0 + ci - 1)
                zact(s, b, b.tmp2.ap()[:, ci % 2], b.sring.ap()[:, ci % 4], AF.Square).then_inc(s_s.h, 1)
                s_s.n += 1

        evV0, evS0 = s_evV.n, s_evS.n

        @blk.tensor
        def _(t):
            t.wait_ge(s_evV.h, evV0)
            t.wait_ge(s_evS.h, evS0)
            for ci in range(CC):
                t.wait_ge(s_s.h, sq0 + ci + 1)
                for tt in range(NT):
                    t.matmul(b.psA.ap()[0:1, tt], b.ones_bf[:],
                             b.sring.ap()[:, ci % 4, tt * TW:(tt + 1) * TW],
                             start=(ci == 0), stop=(ci == CC - 1))
                    ins = t.matmul(b.psB.ap()[0:1, tt], b.ones_bf[:],
                                   b.tmp2.ap()[:, ci % 2, tt * TW:(tt + 1) * TW],
                                   start=(ci == 0), stop=(ci == CC - 1))
                ins.then_inc(s_mm.h, 1)
        s_mm.n += CC

    def rows_from_stats():
        r0 = b.aB.ap()[0:1, :]
        r1 = b.bB.ap()[0:1, :]

        @blk.vector
        def _(v):
            s_mm.wait(v)
            s_bc.wait(v)
            for tt in range(NT):
                ins = v.tensor_scalar(r1[:, tt * TW:(tt + 1) * TW], b.psA.ap()[0:1, tt], 1.0 / D, None, ALU.mult)
            s_v.inc(ins, 1)

        @blk.scalar
        def _(s):
            s_v.wait(s)
            ins = zact(s, b, r0, r1, AF.Square)
            s_s.inc(ins, 1)

        @blk.vector
        def _(v):
            s_s.wait(v)
            for tt in range(NT):
                ins = v.tensor_scalar(r1[:, tt * TW:(tt + 1) * TW], b.psB.ap()[0:1, tt], 1.0 / D, None, ALU.mult)
            s_cv.inc(ins, 1)
            s_cv.wait(v)
            ins = v.tensor_tensor(r1, r1, r0, ALU.subtract)
            s_cv.inc(ins, 1)
            s_cv.wait(v)
            ins = v.tensor_scalar(r1, r1, 1.0, LN_EPS, ALU.mult, ALU.add)
            s_cv.inc(ins, 1)
            s_cv.wait(v)
            ins = v.reciprocal(r0, r1)
            s_v.inc(ins, 1)

        @blk.scalar
        def _(s):
            s_v.wait(s)
            ins = zact(s, b, r0, r0, AF.Sqrt)
            s_s.inc(ins, 1)

        @blk.vector
        def _(v):
            s_s.wait(v)
            for tt in range(NT):
                ins = v.tensor_scalar(r1[:, tt * TW:(tt + 1) * TW], b.psA.ap()[0:1, tt], -1.0 / D, None, ALU.mult)
            s_cv.inc(ins, 1)
            s_cv.wait(v)
            ins = v.tensor_tensor(r1, r1, r0, ALU.mult)
            s_evV.inc(ins, 1)

    def bcast_rows(slot_a, slot_b):
        row_done = s_evV.n

        @blk.scalar
        def _(s):
            s.wait_ge(s_evV.h, row_done)
            s.dma_start(out=sp_row3[slot_a:slot_a + 1], in_=b.aB.ap()[0:1, :]).then_inc(s_rsp.h, 16)
            s.dma_start(out=sp_row3[slot_b:slot_b + 1], in_=b.bB.ap()[0:1, :]).then_inc(s_rsp.h, 16)
        s_rsp.n += 32

        mm_need_bc = s_mm.n

        @blk.sync
        def _(sy):
            s_rsp.wait(sy)
            sy.wait_ge(s_mm.h, mm_need_bc)
            sy.dma_start(out=b.aB[:], in_=sp_row3[slot_a].unsqueeze(0).broadcast_to([128, T])).then_inc(s_bc.h, 16)
            sy.dma_start(out=b.bB[:], in_=sp_row3[slot_b].unsqueeze(0).broadcast_to([128, T])).then_inc(s_bc.h, 16)
        s_bc.n += 32

    def ln_apply_plain(src_dram, dst_slot):
        bc_need = s_bc.n
        mm_need = s_mm.n
        bld0 = s_v.n
        ss_need = s_s.n
        ld_vals = []

        @blk.sync
        def _(sy):
            sy.wait_ge(s_v.h, bld0)
            sy.wait_ge(s_s.h, ss_need)
            sy.wait_ge(s_mm.h, mm_need)
            for ci in range(CC):
                if ci >= 4:
                    sy.wait_ge(s_v.h, bld0 + ci - 3)
                ins = sy.dma_start(out=b.sring.ap()[:, ci % 4], in_=src_dram[ci])
                s_srs[ci % 4].inc(ins, 16)
                ld_vals.append(s_srs[ci % 4].n)

        @blk.vector
        def _(v):
            v.wait_ge(s_bc.h, bc_need)
            v.wait_ge(s_mm.h, mm_need)
            v.wait_ge(s_s.h, ss_need)
            for ci in range(CC):
                v.wait_ge(s_srs[ci % 4].h, ld_vals[ci])
                v.wait_ge(s_v.h, s_v.n)
                ins = v.tensor_tensor(b.tmp2.ap()[:, 0], b.sring.ap()[:, ci % 4], b.aB[:], ALU.mult)
                s_cv.inc(ins, 1)
                s_cv.wait(v)
                ins = v.tensor_tensor(dst_slot.ap()[:, ci, 1:1 + T], b.tmp2.ap()[:, 0], b.bB[:], ALU.add)
                s_v.inc(ins, 1)

    def n2_rhs(cc, tt):
        return b.slotA.ap()[:, cc, 1 + tt * TW:1 + (tt + 1) * TW]

    t1_entry = [s.n for s in s_t1]

    @blk.gpsimd
    def _(g):
        for sem, val in zip(s_t1, t1_entry):
            g.wait_ge(sem.h, val)

    @blk.sync
    def _(sy):
        for sem, val in zip(s_t1, t1_entry):
            sy.wait_ge(sem.h, val)

    stat_pass(x_in)
    rows_from_stats()
    bcast_rows(0, 1)
    ln_apply_plain(x_in, b.slotA)
    n2_done = s_v.n

    s_cz = ctx.sem("s_cz")
    if s_cz not in ctx.dma_sems:
        ctx.dma_sems.append(s_cz)
    ss_cz, mm_cz = s_s.n, s_mm.n

    @blk.sync
    def _(sy):
        sy.wait_ge(s_v.h, n2_done)
        sy.wait_ge(s_s.h, ss_cz)
        sy.wait_ge(s_mm.h, mm_cz)
        ins = sy.dma_start(out=b.sring.ap()[:, 0].rearrange("p (a c) -> p a c", c=TW),
                           in_=ctx.cz_dram.ap())
        s_cz.inc(ins, 16)
    cz_ready = s_cz.n

    def kv_pass(w_name, dst, gate=None):
        lw_vals = []
        mm0 = s_mm.n
        evV_need, evS_need = s_evV.n, s_evS.n
        w_ap = ctx.dram[w_name]

        @blk.gpsimd
        def _(g):
            g.wait_ge(s_mm.h, mm0)
            ins = g.dma_start(out=b.wring.ap()[:, 0, 0:1024], in_=w_ap[0])
            s_wr[0].inc(ins, 16)
        lwv = s_wr[0].n

        @blk.tensor
        def _(t):
            if gate is not None:
                for sem, val in gate:
                    t.wait_ge(sem.h, val)
            t.wait_ge(s_wr[0].h, lwv)
            t.wait_ge(s_evV.h, evV_need)
            t.wait_ge(s_evS.h, evS_need)
            wv = b.wring.ap()[:, 0, 0:1024].rearrange("p (cc o) -> p cc o", o=64)
            for cc in range(CC):
                for tt in range(NT):
                    ins = t.matmul(b.psA.ap()[0:64, tt], wv[:, cc], n2_rhs(cc, tt),
                                   start=(cc == 0), stop=(cc == CC - 1))
            ins.then_inc(s_mm.h, 1)
        s_mm.n += 1

        @blk.scalar
        def _(s):
            s.wait_ge(s_mm.h, s_mm.n)
            ins = zcopy(s, b, dst[0:64, :], psflat(b.psA)[0:64, :])
            s_evS.inc(ins, 1)

    kv_pass("l1a_wk", kaug_sb, gate=[(s_v, n2_done)])
    kv_pass("l1a_wv", v_sb)
    kv_done = s_evS.n

    @blk.gpsimd
    def _(g):
        g.wait_ge(s_evS.h, kv_done)
        ins = g.memset(kaug_sb[64:65, :], 1.0)
        ctx.sem("s_p").inc(ins, 1)
    s_p = ctx.sem("s_p")
    ones_row_done = s_p.n

    @blk.scalar
    def _(s):
        s.wait_ge(s_evS.h, kv_done)
        ins = zact(s, b, b.tmp2.ap()[0:64, 0], kaug_sb[0:64, :], AF.Square)
        s_s.inc(ins, 1)
    k2_done = s_s.n
    mmK0 = s_mm.n
    evS_K = s_evS.n

    @blk.tensor
    def _(t):
        t.wait_ge(s_s.h, k2_done)
        t.wait_ge(s_evS.h, evS_K)
        for tt in range(NT):
            t.matmul(b.psA.ap()[0:1, tt], b.ones_bf[0:64, :], kaug_sb[0:64, tt * TW:(tt + 1) * TW],
                     start=True, stop=True)
            ins = t.matmul(b.psB.ap()[0:1, tt], b.ones_bf[0:64, :],
                           b.tmp2.ap()[0:64, 0, tt * TW:(tt + 1) * TW], start=True, stop=True)
        ins.then_inc(s_mm.h, 1)
        for sc in range(CC):
            ins = t.matmul(b.psB.ap()[:, 2, sc:sc + 1], kaug_sb[0:64, sc * 128:(sc + 1) * 128],
                           b.ones_bf[0:64, :], start=True, stop=True)
        ins.then_inc(s_mm.h, 1)
    s_mm.n += 2

    r_ks = b.bB.ap()[0:1, :]

    @blk.vector
    def _(v):
        v.wait_ge(s_mm.h, s_mm.n)
        v.wait_ge(s_bc.h, s_bc.n)
        for tt in range(NT):
            ins = v.tensor_copy(r_sc[:, tt * TW:(tt + 1) * TW], b.psA.ap()[0:1, tt])
        s_cv.inc(ins, 1)
        s_cv.wait(v)
        ins = v.tensor_copy(r_ks, r_sc)
        s_cv.inc(ins, 1)
        s_cv.wait(v)
        ins = v.tensor_copy(k32, r_sc)
        s_cv.inc(ins, 1)
        s_cv.wait(v)
        ins = v.tensor_copy(kcol_sb.ap()[:, 0:CC], b.psB.ap()[:, 2, 0:CC])
        s_cv.inc(ins, 1)
        s_cv.wait(v)
        ins = v.reduce_max(scal.ap()[0:1, 0:1], r_sc, axis=mybir.AxisListType.X, apply_absolute_value=True)
        s_cv.inc(ins, 1)
        s_cv.wait(v)
        for tt in range(NT):
            ins = v.tensor_copy(r_qq[:, tt * TW:(tt + 1) * TW], b.psB.ap()[0:1, tt])
        s_cv.inc(ins, 1)
        s_cv.wait(v)
        ins = v.reduce_max(scal.ap()[0:1, 1:2], r_qq, axis=mybir.AxisListType.X)
        s_v.inc(ins, 1)

    @blk.scalar
    def _(s):
        s_v.wait(s)
        ins = zact(s, b, scal.ap()[0:1, 1:2], scal.ap()[0:1, 1:2], AF.Sqrt)
        s_evS.inc(ins, 1)
    scal_done = s_evS.n


    @blk.vector
    def _(v):
        v.wait_ge(s_evS.h, scal_done)
        ins = v.tensor_copy(w_max, r_ks)
        s_cv.inc(ins, 1)
        s_cv.wait(v)
        ins = v.tensor_copy(w_min, r_ks)
        s_cv.inc(ins, 1)
        s_cv.wait(v)
        win = 1
        for step in [1, 2, 4, 8, 16, 32, 64]:
            d = min(step, 65 - win)
            if d <= 0:
                break
            for (buf, op) in ((w_max, ALU.max), (w_min, ALU.min)):
                ins = v.tensor_tensor(scr2[:, d:T], buf[:, d:T], buf[:, 0:T - d], op)
                s_cv.inc(ins, 1)
                s_cv.wait(v)
                ins = v.tensor_copy(buf[:, d:T], scr2[:, d:T])
                s_cv.inc(ins, 1)
                s_cv.wait(v)
            win += d
        ins = v.nop()
        s_v.inc(ins, 1)
    wdone = s_v.n


    stk_qq = rowsF.ap()[0:8, :]
    stk_qs = rowsF.ap()[32:40, :]
    stk_dg = b.aB.ap()[0:8, :]
    wmax8 = b.aB.ap()[32:40, :]
    wmin8 = b.bB.ap()[32:40, :]
    ks8 = tmp1f32[32:40, 0:T]
    sc1 = b.bB.ap()[0:8, :]
    sc2 = tmp1f32[0:8, 0:T]
    stq_row = rowsF.ap()[96:97, :]
    sts_row = b.aB.ap()[96:97, :]
    std_row = b.bB.ap()[96:97, :]
    r_l = b.bB.ap()[64:65, :]
    scal8 = ctx.scal8_sb
    vup2 = ctx.vup2_sb
    vtok_sb = ch_bf(15)
    wq = ctx.dram["l1a_wq"]
    q2l = ctx.dram["l1a_q2l"]
    s_p = ctx.sem("s_p")

    @blk.scalar
    def _(s, wdone=wdone):
        s.wait_ge(s_v.h, wdone)
        s.dma_start(out=sp_row3[2:3, :], in_=w_max).then_inc(s_rsp.h, 16)
        s.dma_start(out=sp_row3[3:4, :], in_=w_min).then_inc(s_rsp.h, 16)
        s.dma_start(out=sp_row3[4:5, :], in_=r_ks).then_inc(s_rsp.h, 16)
        s.dma_start(out=sp_row3[5:6, 0:2], in_=scal.ap()[0:1, 0:2]).then_inc(s_rsp.h, 16)
    s_rsp.n += 64
    wrows_sp = s_rsp.n

    @blk.sync
    def _(sy, wrows_sp=wrows_sp, n2_done=n2_done):
        sy.wait_ge(s_rsp.h, wrows_sp)
        sy.wait_ge(s_v.h, n2_done)
        sy.dma_start(out=wmax8, in_=sp_row3[2].unsqueeze(0).broadcast_to([8, T])).then_inc(s_bc.h, 16)
        sy.dma_start(out=wmin8, in_=sp_row3[3].unsqueeze(0).broadcast_to([8, T])).then_inc(s_bc.h, 16)
        sy.dma_start(out=ks8, in_=sp_row3[4].unsqueeze(0).broadcast_to([8, T])).then_inc(s_bc.h, 16)
        sy.dma_start(out=scal8.ap()[:, 0:2],
                     in_=sp_row3[5, 0:2].unsqueeze(0).broadcast_to([128, 2])).then_inc(s_bc.h, 16)
    s_bc.n += 64
    w8_ready = s_bc.n


    mmV0 = s_mm.n
    evV_V0 = s_evV.n

    @blk.tensor
    def _(t, mmV0=mmV0, evV_V0=evV_V0):
        t.wait_ge(s_evS.h, scal_done)
        t.wait_ge(s_evV.h, evV_V0)
        for sc in range(CC):
            if sc >= 2:
                t.wait_ge(s_evV.h, evV_V0 + sc - 1)
            ins = t.matmul(b.psA.ap()[:, sc % 2, 0:64], v_sb[0:64, sc * 128:(sc + 1) * 128],
                           ctx.ident64_sb.ap()[0:64, 0:64], start=True, stop=True)
            ins.then_inc(s_mm.h, 1)
    s_mm.n += CC

    @blk.vector
    def _(v, mmV0=mmV0):
        for sc in range(CC):
            v.wait_ge(s_mm.h, mmV0 + sc + 1)
            ins = v.tensor_copy(vtok_sb[:, sc * 64:(sc + 1) * 64], b.psA.ap()[:, sc % 2, 0:64])
            s_evV.inc(ins, 1)
    vtok_done = s_evV.n

    def qs512(slot):
        return b.sring.ap()[:, 2 + slot].bitcast(F32)[:, 0:TW]

    bsc_scr = b.sring.ap()[:, 1].bitcast(F32)[:, 0:128]
    band_done = {0: s_v.n, 1: s_v.n}
    pt_free = [s_mm.n] * 4
    cv_bank_free = [s_evV.n] * 4
    exp_last = (s_evS.n, s_evS.n)
    l_sp_last = s_rsp.n
    qs_sp_marks = []

    for g in range(2):
        entry_mm = s_mm.n
        entry_evS = s_evS.n
        entry_evV = s_evV.n
        entry_v = s_v.n
        qh_marks = []
        ql_marks = []
        rows_marks = []
        wq_pass_mark = []
        q2l_mark = []
        row_mm_mark = []
        for hh in range(8):
            h = g * 8 + hh

            @blk.gpsimd
            def _(gq, h=h, hh=hh, entry_mm=entry_mm):
                if hh >= 3:
                    gq.wait_ge(s_mm.h, wq_pass_mark[hh - 3])
                else:
                    gq.wait_ge(s_mm.h, entry_mm)
                ins = gq.dma_start(out=b.wring.ap()[:, hh % 3], in_=wq[h])
                s_wr[hh % 3].inc(ins, 16)
                if hh >= 2:
                    gq.wait_ge(s_mm.h, q2l_mark[hh - 2])
                ins = gq.dma_start(out=ctx.q2l_sb.ap()[:, (hh % 2) * 64:(hh % 2) * 64 + 64], in_=q2l[h])
                s_wr[hh % 3].inc(ins, 16)
            lw_val = s_wr[hh % 3].n

            evV_need = qh_marks[hh - 1] if hh >= 1 else (vtok_done if g == 0 else entry_evV)

            @blk.tensor
            def _(t, hh=hh, g=g, lw_val=lw_val, evV_need=evV_need,
                  entry_evS=entry_evS):
                t.wait_ge(s_wr[hh % 3].h, lw_val)
                t.wait_ge(s_evV.h, evV_need)
                if hh == 0:
                    t.wait_ge(s_v.h, n2_done)
                wv = b.wring.ap()[:, hh % 3].rearrange("p (cc o) -> p cc o", o=128)
                for cc in range(CC):
                    for tt in range(NT):
                        ins = t.matmul(b.psA.ap()[:, tt], wv[:, cc], n2_rhs(cc, tt),
                                       start=(cc == 0), stop=(cc == CC - 1))
                ins.then_inc(s_mm.h, 1)
            s_mm.n += 1
            wq_pass_mark.append(s_mm.n)

            @blk.vector
            def _(v, mmv=s_mm.n):
                v.wait_ge(s_mm.h, mmv)
                ins = v.tensor_copy(qh_sb, psflat(b.psA))
                s_evV.inc(ins, 1)
            qh_marks.append(s_evV.n)

            rows_prev = rows_marks[hh - 1] if hh >= 1 else None

            @blk.tensor
            def _(t, hh=hh, g=g, qh_val=qh_marks[-1], lw_val=lw_val, rows_prev=rows_prev,
                  entry_evS=entry_evS, entry_v=entry_v):
                t.wait_ge(s_evV.h, qh_val)
                t.wait_ge(s_wr[hh % 3].h, lw_val)
                if rows_prev is not None:
                    t.wait_ge(s_v.h, rows_prev)
                elif g == 0:
                    t.wait_ge(s_evS.h, scal_done)
                else:
                    t.wait_ge(s_evS.h, entry_evS)
                    t.wait_ge(s_v.h, entry_v)
                for tt in range(NT):
                    ins = t.matmul(b.psB.ap()[0:64, tt],
                                   ctx.q2l_sb.ap()[:, (hh % 2) * 64:(hh % 2) * 64 + 64],
                                   qh_sb[:, tt * TW:(tt + 1) * TW], start=True, stop=True)
                ins.then_inc(s_mm.h, 1)
            s_mm.n += 1
            q2l_mark.append(s_mm.n)

            @blk.vector
            def _(v, hh=hh, mmv=s_mm.n):
                v.wait_ge(s_mm.h, mmv)
                ins = v.tensor_scalar(qlslot(hh)[0:64, :], psflat(b.psB)[0:64, :], 0.125, None, ALU.mult)
                s_evV.inc(ins, 1)
            ql_marks.append(s_evV.n)

            mm_prev_rows = row_mm_mark[hh - 1] if hh >= 1 else None

            @blk.scalar
            def _(s, hh=hh, ql_val=ql_marks[-1], mm_prev_rows=mm_prev_rows):
                s.wait_ge(s_evV.h, ql_val)
                if mm_prev_rows is not None:
                    s.wait_ge(s_mm.h, mm_prev_rows)
                ins = zact(s, b, b.tmp2.ap()[0:64, 0], qlslot(hh)[0:64, :], AF.Square)
                s_s.inc(ins, 1)
            sq_mark = s_s.n

            @blk.tensor
            def _(t, hh=hh, sqv=sq_mark, ql_val=ql_marks[-1]):
                t.wait_ge(s_s.h, sqv)
                t.wait_ge(s_evV.h, ql_val)
                for tt in range(NT):
                    t.matmul(b.psB.ap()[0:1, tt], b.ones_bf[0:64, :],
                             b.tmp2.ap()[0:64, 0, tt * TW:(tt + 1) * TW], start=True, stop=True)
                    ins = t.matmul(b.psB.ap()[32:33, tt], b.ones_bf[0:64, :],
                                   qlslot(hh)[0:64, tt * TW:(tt + 1) * TW], start=True, stop=True)
                ins.then_inc(s_mm.h, 1)
            s_mm.n += 1
            qqqs_mark = s_mm.n

            @blk.vector
            def _(v, hh=hh, qqqs_mark=qqqs_mark):
                v.wait_ge(s_mm.h, qqqs_mark)
                ins = v.tensor_tensor(b.tmp2.ap()[0:64, 0], qlslot(hh)[0:64, :],
                                      kaug_sb[0:64, :], ALU.mult)
                s_v.inc(ins, 1)
            qk_mark = s_v.n

            @blk.tensor
            def _(t, qkv=qk_mark):
                t.wait_ge(s_v.h, qkv)
                for tt in range(NT):
                    ins = t.matmul(b.psB.ap()[64:65, tt], b.ones_bf[0:64, :],
                                   b.tmp2.ap()[0:64, 0, tt * TW:(tt + 1) * TW], start=True, stop=True)
                ins.then_inc(s_mm.h, 1)
            s_mm.n += 1
            row_mm_mark.append(s_mm.n)

            stage_gate = s_rsp.n

            @blk.vector
            def _(v, mmv=s_mm.n, stage_gate=stage_gate):
                v.wait_ge(s_mm.h, mmv)
                v.wait_ge(s_rsp.h, stage_gate)
                ins = v.tensor_copy(stq_row, psflat(b.psB)[0:1, :])
                s_cv.inc(ins, 1)
                s_cv.wait(v)
                ins = v.tensor_copy(sts_row, psflat(b.psB)[32:33, :])
                s_cv.inc(ins, 1)
                s_cv.wait(v)
                ins = v.tensor_copy(std_row, psflat(b.psB)[64:65, :])
                s_v.inc(ins, 1)
            rows_marks.append(s_v.n)

            @blk.sync
            def _(sy, hh=hh, vmark=rows_marks[-1], srg0_gate=s_srg[0].n):
                sy.wait_ge(s_v.h, vmark)
                sy.wait_ge(s_srg[0].h, srg0_gate)
                sy.dma_start(out=sp_rows8[hh:hh + 1, :], in_=stq_row).then_inc(s_rsp.h, 16)
                sy.dma_start(out=sp_rows8[8 + hh:9 + hh, :], in_=sts_row).then_inc(s_rsp.h, 16)
                sy.dma_start(out=sp_rows8[16 + hh:17 + hh, :], in_=std_row).then_inc(s_rsp.h, 16)
            s_rsp.n += 48

        all_spills = s_rsp.n
        vchain_gate = s_v.n

        @blk.sync
        def _(sy, all_spills=all_spills, vchain_gate=vchain_gate, g=g,
              qs_sp_marks=list(qs_sp_marks)):
            sy.wait_ge(s_rsp.h, all_spills)
            sy.wait_ge(s_v.h, vchain_gate)
            if g == 1:
                sy.wait_ge(s_rsp.h, qs_sp_marks[-1])
            sy.dma_start(out=stk_qq, in_=sp_rows8[0:8, :]).then_inc(s_rsp.h, 16)
            sy.dma_start(out=stk_qs, in_=sp_rows8[8:16, :]).then_inc(s_rsp.h, 16)
            sy.dma_start(out=stk_dg, in_=sp_rows8[16:24, :]).then_inc(s_rsp.h, 16)
        s_rsp.n += 48
        stacks_ready = s_rsp.n

        @blk.scalar
        def _(s, stacks_ready=stacks_ready, w8=w8_ready):
            s.wait_ge(s_rsp.h, stacks_ready)
            s.wait_ge(s_bc.h, w8)
            ins = zact(s, b, stk_qq, stk_qq, AF.Sqrt)
            s_s.inc(ins, 1)
        sqrt_mark = s_s.n

        @blk.vector
        def _(v, sqrt_mark=sqrt_mark, lsp=l_sp_last):
            v.wait_ge(s_s.h, sqrt_mark)
            v.wait_ge(s_rsp.h, lsp)
            ins = v.tensor_scalar(stk_qq, stk_qq, scal8.ap()[0:8, 1:2], None, ALU.mult)
            s_cv.inc(ins, 1)
            s_cv.wait(v)
            ins = v.tensor_tensor(sc1, stk_qs, wmax8, ALU.mult)
            s_cv.inc(ins, 1)
            s_cv.wait(v)
            ins = v.tensor_tensor(sc2, stk_qs, wmin8, ALU.mult)
            s_cv.inc(ins, 1)
            s_cv.wait(v)
            ins = v.tensor_tensor(sc1, sc1, sc2, ALU.max)
            s_cv.inc(ins, 1)
            s_cv.wait(v)
            ins = v.tensor_scalar(sc1, sc1, 0.5, None, ALU.mult)
            s_cv.inc(ins, 1)
            s_cv.wait(v)
            ins = v.scalar_tensor_tensor(sc2, stk_qs, 0.5, ks8, ALU.mult, ALU.mult)
            s_cv.inc(ins, 1)
            s_cv.wait(v)
            ins = v.tensor_tensor(stk_dg, stk_dg, sc2, ALU.add)
            s_cv.inc(ins, 1)
            s_cv.wait(v)
            ins = v.tensor_tensor(sc2, sc1, stk_qq, ALU.subtract)
            s_cv.inc(ins, 1)
            s_cv.wait(v)
            ins = v.tensor_scalar(sc1, sc1, 1.0, 0.0, ALU.mult, ALU.max)
            s_cv.inc(ins, 1)
            s_cv.wait(v)
            ins = v.tensor_tensor(sc1, sc1, stk_qq, ALU.add)
            s_cv.inc(ins, 1)
            s_cv.wait(v)
            ins = v.tensor_tensor(stk_dg, stk_dg, sc2, ALU.max)
            s_cv.inc(ins, 1)
            s_cv.wait(v)
            ins = v.tensor_scalar(stk_dg, stk_dg, 1.0, 80.0, ALU.mult, ALU.add)
            s_cv.inc(ins, 1)
            s_cv.wait(v)
            ins = v.tensor_tensor(stk_qq, sc1, stk_dg, ALU.min)
            s_v.inc(ins, 1)
        mhat_rows = s_v.n

        mm_now = s_mm.n

        @blk.vector
        def _(v, mhat_rows=mhat_rows):
            v.wait_ge(s_v.h, mhat_rows)
            ins = v.tensor_scalar(sc1, stk_qq, -1.0, None, ALU.mult)
            s_v.inc(ins, 1)
        neg_mark = s_v.n

        @blk.scalar
        def _(s, g=g, neg_mark=neg_mark):
            s.wait_ge(s_v.h, neg_mark)
            s.dma_start(out=sp_rows8[0:8, :], in_=sc1).then_inc(s_rsp.h, 16)
            s.dma_start(out=qs_dram[8 * g:8 * g + 8, :], in_=stk_qs).then_inc(s_rsp.h, 16)
        s_rsp.n += 32
        qs_sp_marks.append(s_rsp.n)
        qs_spilled = s_rsp.n
        negsp = s_rsp.n

        @blk.gpsimd
        def _(gq, negsp=negsp, mm_now=mm_now):
            gq.wait_ge(s_rsp.h, negsp)
            gq.wait_ge(s_mm.h, mm_now)
            for hh in range(8):
                ins = gq.dma_start(out=qlslot(hh)[64:65, :], in_=sp_rows8[hh:hh + 1, :])
                s_srg[0].inc(ins, 16)
        mhat_v = s_srg[0].n

        for hh in range(8):
            h = g * 8 + hh
            for ap_ in range(4):
                nbat = ap_ + 1
                nsc = 4 * nbat
                cv_entry = cv_bank_free[ap_]
                qslot = ap_ % 2
                bgate = band_done[qslot]

                @blk.sync
                def _(sy, h=h, ap_=ap_, qslot=qslot, qs_spilled=qs_spilled, bgate=bgate):
                    sy.wait_ge(s_rsp.h, qs_spilled)
                    sy.wait_ge(s_v.h, bgate)
                    ins = sy.dma_start(out=qs512(qslot),
                                       in_=qs_dram[h, ap_ * TW:(ap_ + 1) * TW]
                                       .unsqueeze(0).broadcast_to([128, TW]))
                    s_qb[qslot].inc(ins, 16)
                qv = s_qb[qslot].n

                for bi in range(nbat):
                    exp_need = exp_last
                    ptf = pt_free[bi]
                    tile_mm = []
                    units = []
                    for j in range(4):
                        sc = 4 * bi + j

                        @blk.tensor
                        def _(t, sc=sc, hh=hh, ap_=ap_, j=j, exp_need=exp_need, mhat_v=mhat_v):
                            if j == 0:
                                t.wait_ge(s_srg[0].h, mhat_v)
                                t.wait_ge(s_p.h, ones_row_done)
                                t.wait_ge(s_evS.h, exp_need[0])
                            elif j == 2:
                                t.wait_ge(s_evS.h, exp_need[1])
                            ins = t.matmul(b.psB.ap()[:, sc % 4],
                                           kaug_sb[0:65, sc * 128:(sc + 1) * 128],
                                           qlslot(hh)[0:65, ap_ * TW:(ap_ + 1) * TW],
                                           start=True, stop=True)
                            ins.then_inc(s_mm.h, 1)
                        s_mm.n += 1
                        tile_mm.append(s_mm.n)

                        if sc >= 4 * ap_ - 1:
                            if 4 * ap_ <= sc <= 4 * ap_ + 3:
                                units.append((j, sc, band2, sc))
                            if 4 * ap_ <= sc + 1 <= 4 * ap_ + 3:
                                units.append((j, sc + 1, band1, sc))

                    v_bands = []
                    for half_u in range(2):
                        hu = [u for u in units if u[0] // 2 == half_u]
                        if hu:
                            @blk.vector
                            def _(v, units=list(hu), tile_mm=list(tile_mm), qv=qv,
                                  ap_=ap_, qslot=qslot):
                                v.wait_ge(s_qb[qslot].h, qv)
                                v.wait_ge(s_v.h, s_v.n)
                                for (j, a_q, mask_t, kidx) in units:
                                    v.wait_ge(s_mm.h, tile_mm[j])
                                    colw = (a_q - 4 * ap_) * 128
                                    ins = v.tensor_tensor(bsc_scr, qs512(qslot)[:, colw:colw + 128],
                                                          mask_t.ap(), ALU.mult)
                                    s_cv.inc(ins, 1)
                                    s_cv.wait(v)
                                    ins = v.scalar_tensor_tensor(
                                        b.psB.ap()[:, j, colw:colw + 128], bsc_scr,
                                        kcol_sb.ap()[:, kidx:kidx + 1],
                                        b.psB.ap()[:, j, colw:colw + 128], ALU.mult, ALU.add)
                                    s_cv.inc(ins, 1)
                                    s_cv.wait(v)
                                ins = v.nop()
                                s_v.inc(ins, 1)
                            band_done[qslot] = s_v.n
                        v_bands.append(s_v.n)

                    @blk.scalar
                    def _(s, bi=bi, v_bands=list(v_bands), m01=tile_mm[1], m23=tile_mm[3], ptf=ptf):
                        s.wait_ge(s_v.h, v_bands[0])
                        s.wait_ge(s_mm.h, ptf)
                        s.wait_ge(s_mm.h, m01)
                        ins = zact(s, b, b.slotB.ap()[:, 11 + bi, 0:1024],
                                   psflat(b.psB)[:, 0:1024], AF.Exp)
                        s_evS.inc(ins, 1)
                        s.wait_ge(s_v.h, v_bands[1])
                        s.wait_ge(s_mm.h, m23)
                        ins = zact(s, b, b.slotB.ap()[:, 11 + bi, 1024:2048],
                                   psflat(b.psB)[:, 1024:2048], AF.Exp)
                        s_evS.inc(ins, 1)
                    exp_last = (s_evS.n - 1, s_evS.n)
                    exp_batch = s_evS.n

                    cz_mark = None
                    if bi == ap_:
                        @blk.vector
                        def _(v, ap_=ap_, exp_batch=exp_batch):
                            v.wait_ge(s_evS.h, exp_batch)
                            v.wait_ge(s_cz.h, cz_ready)
                            for p in range(4):
                                v.wait_ge(s_v.h, s_v.n)
                                ins = v.tensor_tensor(pt_tile(4 * ap_ + p), pt_tile(4 * ap_ + p),
                                                      czv(p), ALU.mult)
                                s_v.inc(ins, 1)
                        cz_mark = s_v.n

                    @blk.tensor
                    def _(t, bi=bi, ap_=ap_, nsc=nsc, exp_batch=exp_batch, cz_mark=cz_mark,
                          cv_entry=cv_entry):
                        t.wait_ge(s_evS.h, exp_batch)
                        if cz_mark is not None:
                            t.wait_ge(s_v.h, cz_mark)
                        if bi == 0:
                            t.wait_ge(s_evV.h, cv_entry)
                        for j in range(4):
                            sc = 4 * bi + j
                            t.matmul(b.psA.ap()[0:64, ap_], vtok_sb[:, sc * 64:(sc + 1) * 64],
                                     pt_tile(sc), start=(sc == 0), stop=(sc == nsc - 1))
                        for j in range(4):
                            sc = 4 * bi + j
                            ins = t.matmul(b.psA.ap()[64:65, ap_], b.ones_bf[:],
                                           pt_tile(sc), start=(sc == 0), stop=(sc == nsc - 1))
                        ins.then_inc(s_mm.h, 1)
                    s_mm.n += 1
                    pt_free[bi] = s_mm.n
                cv_done = s_mm.n

                t12 = s_t1[2].n
                lsp = l_sp_last

                @blk.vector
                def _(v, ap_=ap_, cv_done=cv_done, t12=t12, lsp=lsp):
                    v.wait_ge(s_mm.h, cv_done)
                    v.wait_ge(s_t1[2].h, t12)
                    if ap_ == 0:
                        v.wait_ge(s_rsp.h, lsp)
                    ins = v.tensor_copy(b.tmp1.ap()[0:64, 2, (ap_ % 4) * TW:(ap_ % 4 + 1) * TW],
                                        b.psA.ap()[0:64, ap_])
                    s_cv.inc(ins, 1)
                    s_cv.wait(v)
                    ins = v.tensor_copy(r_l[:, ap_ * TW:(ap_ + 1) * TW], b.psA.ap()[64:65, ap_])
                    s_evV.inc(ins, 1)
                cv_bank_free[ap_] = s_evV.n

                @blk.scalar
                def _(s, h=h, ap_=ap_, evv=s_evV.n):
                    s.wait_ge(s_evV.h, evv)
                    ins = s.dma_start(out=sp_cv[h][:, ap_ * TW:(ap_ + 1) * TW],
                                      in_=b.tmp1.ap()[0:64, 2, (ap_ % 4) * TW:(ap_ % 4 + 1) * TW])
                    s_t1[2].inc(ins, 16)

            @blk.scalar
            def _(s, h=h, evv=s_evV.n):
                s.wait_ge(s_evV.h, evv)
                s.dma_start(out=l_dram[h:h + 1, :], in_=r_l).then_inc(s_rsp.h, 16)
            s_rsp.n += 16
            l_sp_last = s_rsp.n

    attn_mm = s_mm.n
    cv_spilled = s_t1[2].n
    l_all = s_rsp.n
    evS_fin = s_evS.n
    vW0 = s_v.n
    rc_vals = []
    lb_vals = []

    @blk.sync
    def _(sy, attn_mm=attn_mm, cv_spilled=cv_spilled, l_all=l_all, vW0=vW0):
        sy.wait_ge(s_t1[2].h, cv_spilled)
        sy.wait_ge(s_rsp.h, l_all)
        sy.wait_ge(s_mm.h, attn_mm)
        sy.wait_ge(s_v.h, vW0)
        for hp in range(8):
            ins = sy.dma_start(out=b.slotB.ap()[0:64, hp, 0:T], in_=sp_cv[2 * hp])
            s_srs[hp % 4].inc(ins, 16)
            ins = sy.dma_start(out=b.slotB.ap()[64:128, hp, 0:T], in_=sp_cv[2 * hp + 1])
            s_srs[hp % 4].inc(ins, 16)
            rc_vals.append(s_srs[hp % 4].n)
            if hp >= 1:
                sy.wait_ge(s_v.h, vW0 + 2 * (hp - 1) + 1)
            ins = sy.dma_start(out=b.aB.ap()[0:64, :],
                               in_=l_dram[2 * hp].unsqueeze(0).broadcast_to([64, T]))
            s_bc.inc(ins, 16)
            ins = sy.dma_start(out=b.aB.ap()[64:128, :],
                               in_=l_dram[2 * hp + 1].unsqueeze(0).broadcast_to([64, T]))
            s_bc.inc(ins, 16)
            lb_vals.append(s_bc.n)

    @blk.vector
    def _(v, attn_mm=attn_mm):
        v.wait_ge(s_mm.h, attn_mm)
        for hp in range(8):
            v.wait_ge(s_srs[hp % 4].h, rc_vals[hp])
            v.wait_ge(s_bc.h, lb_vals[hp])
            v.wait_ge(s_v.h, s_v.n)
            ins = v.reciprocal(b.bB[:], b.aB[:])
            s_v.inc(ins, 1)
            v.wait_ge(s_v.h, s_v.n)
            ins = v.tensor_tensor(b.slotB.ap()[:, hp, 0:T], b.slotB.ap()[:, hp, 0:T],
                                  b.bB[:], ALU.mult)
            s_v.inc(ins, 1)
    cvn_done = s_v.n

    mmU0 = s_mm.n
    evV_U0 = s_evV.n
    vup_vals = []

    @blk.gpsimd
    def _(g, mmU0=mmU0):
        for hd in range(16):
            if hd >= 2:
                g.wait_ge(s_mm.h, mmU0 + hd - 1)
            else:
                g.wait_ge(s_mm.h, mmU0)
            ins = g.dma_start(out=vup2.ap()[(hd % 2) * 64:(hd % 2) * 64 + 64, :],
                              in_=ctx.dram["l1a_vup"][hd])
            s_wr[2].inc(ins, 16)
            vup_vals.append(s_wr[2].n)

    @blk.tensor
    def _(t, vW0=vW0, evV_U0=evV_U0, evS_fin=evS_fin):
        t.wait_ge(s_evV.h, evV_U0)
        t.wait_ge(s_evS.h, evS_fin)
        for hd in range(16):
            hp, par = hd // 2, hd % 2
            t.wait_ge(s_wr[2].h, vup_vals[hd])
            t.wait_ge(s_v.h, vW0 + 2 * hp + 2)
            if hd >= 2:
                t.wait_ge(s_evV.h, evV_U0 + hd - 1)
            ps = b.psA if hd % 2 == 0 else b.psB
            for tt in range(NT):
                ins = t.matmul(ps.ap()[:, tt], vup2.ap()[par * 64:(par + 1) * 64, :],
                               b.slotB.ap()[par * 64:(par + 1) * 64, hp, tt * TW:(tt + 1) * TW],
                               start=True, stop=True)
            ins.then_inc(s_mm.h, 1)
    s_mm.n += 16

    @blk.vector
    def _(v, mmU0=mmU0):
        for hd in range(16):
            v.wait_ge(s_mm.h, mmU0 + hd + 1)
            ps = b.psA if hd % 2 == 0 else b.psB
            ins = v.tensor_copy(b.slotA.ap()[:, hd, 0:T], psflat(ps))
            s_evV.inc(ins, 1)
    ctx_done0 = s_evV.n - 16

    wo = ctx.dram["l1a_wo"]
    mmWo0 = s_mm.n
    lwWo_vals = []

    @blk.gpsimd
    def _(g, mmWo0=mmWo0):
        g.wait_ge(s_mm.h, mmWo0)
        for oc in range(CC):
            if oc >= 3:
                g.wait_ge(s_mm.h, mmWo0 + oc - 2)
            ins = g.dma_start(out=b.wring.ap()[:, oc % 3], in_=wo[oc])
            s_wr[oc % 3].inc(ins, 16)
            lwWo_vals.append(s_wr[oc % 3].n)

    evV_Wo = s_evV.n

    @blk.tensor
    def _(t, evV_Wo=evV_Wo):
        t.wait_ge(s_evV.h, evV_Wo)
        for oc in range(CC):
            ps = b.psA if oc % 2 == 0 else b.psB
            t.wait_ge(s_wr[oc % 3].h, lwWo_vals[oc])
            if oc >= 2:
                t.wait_ge(s_evV.h, evV_Wo + oc - 1)
            wv = b.wring.ap()[:, oc % 3].rearrange("p (cc o) -> p cc o", o=128)
            for cc in range(CC):
                for tt in range(NT):
                    ins = t.matmul(ps.ap()[:, tt], wv[:, cc],
                                   b.slotA.ap()[:, cc, tt * TW:(tt + 1) * TW],
                                   start=(cc == 0), stop=(cc == CC - 1))
            ins.then_inc(s_mm.h, 1)
    s_mm.n += CC

    evV_fin0 = s_evV.n
    vfin0 = s_v.n
    xin_vals = []

    @blk.sync
    def _(sy, vfin0=vfin0, evV_fin0=evV_fin0):
        sy.wait_ge(s_v.h, vfin0)
        for oc in range(CC):
            if oc >= 2:
                sy.wait_ge(s_evV.h, evV_fin0 + oc - 1)
            ins = sy.dma_start(out=b.tmp2.ap()[:, oc % 2], in_=x_in[oc])
            s_qb[oc % 2].inc(ins, 16)
            xin_vals.append(s_qb[oc % 2].n)

    for oc in range(CC):
        t1v = s_t1[oc % 3].n

        @blk.vector
        def _(v, oc=oc, t1v=t1v, mmWo0=mmWo0):
            v.wait_ge(s_mm.h, mmWo0 + oc + 1)
            v.wait_ge(s_qb[oc % 2].h, xin_vals[oc])
            v.wait_ge(s_t1[oc % 3].h, t1v)
            ps = b.psA if oc % 2 == 0 else b.psB
            ins = v.tensor_tensor(b.tmp1.ap()[:, oc % 3], b.tmp2.ap()[:, oc % 2], psflat(ps), ALU.add)
            s_evV.inc(ins, 1)

        @blk.scalar
        def _(s, oc=oc):
            s.wait_ge(s_evV.h, s_evV.n)
            ins = s.dma_start(out=x_out[oc], in_=b.tmp1.ap()[:, oc % 3])
            s_t1[oc % 3].inc(ins, 16)


def host_pack_l1a(inp):
    wk = np.asarray(inp["L1_attn_Wk"], np.float32)
    wv = np.asarray(inp["L1_attn_Wv"], np.float32)
    kpack = np.ascontiguousarray(wk.reshape(64, 16, 128).transpose(2, 1, 0)).reshape(1, 128, 1024)
    vpack = np.ascontiguousarray(wv.reshape(64, 16, 128).transpose(2, 1, 0)).reshape(1, 128, 1024)
    return {
        "l1a_wq": pack_wT(inp["L1_attn_Wq"]),
        "l1a_wk": kpack,
        "l1a_wv": vpack,
        "l1a_q2l": np.ascontiguousarray(np.asarray(inp["L1_attn_q_to_latent"], np.float32)),
        "l1a_vup": np.ascontiguousarray(np.asarray(inp["L1_attn_v_up"], np.float32)),
        "l1a_wo": pack_wT(inp["L1_attn_Wo"]),
    }


def declare_l1a_inputs(ctx):
    ctx.din("l1a_wq", [CC, 128, D], F32)
    ctx.din("l1a_wk", [1, 128, 1024], F32)
    ctx.din("l1a_wv", [1, 128, 1024], F32)
    ctx.din("l1a_q2l", [16, 128, 64], F32)
    ctx.din("l1a_vup", [16, 64, 128], F32)
    ctx.din("l1a_wo", [CC, 128, D], F32)


def alloc_l1a(ctx):
    nc = ctx.nc
    ctx.rowsF = nc.alloc_sbuf_tensor("rowsF", [128, T], F32)
    ctx.scal_sb = nc.alloc_sbuf_tensor("scal_sb", [1, 16], F32)
    ctx.scal8_sb = nc.alloc_sbuf_tensor("scal8_sb", [128, 4], F32)
    ctx.kcol_sb = nc.alloc_sbuf_tensor("kcol_sb", [128, 16], F32)
    ctx.q2l_sb = nc.alloc_sbuf_tensor("q2l_sb", [128, 128], BF16)
    ctx.vup2_sb = nc.alloc_sbuf_tensor("vup2_sb", [128, 128], BF16)
    ctx.wband_sb = None
    i = np.arange(128)[:, None]
    tt_ = np.arange(TW)[None, :]
    czs = np.stack([(i <= tt_ - 128 * p).astype(np.float32) for p in range(4)], axis=1)
    cz_d = nc.inline_tensor(czs.astype(BF), "cz_const")
    t2 = np.arange(128)[None, :]
    b1 = np.where((i - t2 >= 64) & (i - t2 <= 127), 0.5, 0.0).astype(np.float32)
    b2 = np.where((t2 - i >= 0) & (t2 - i <= 64), 0.5, 0.0).astype(np.float32)
    b1_d = nc.inline_tensor(b1.astype(BF), "band1_const")
    b2_d = nc.inline_tensor(b2.astype(BF), "band2_const")
    ctx.cz_dram = cz_d
    ctx.ident64_sb = nc.alloc_sbuf_tensor("ident64_sb", [64, 64], BF16)
    ctx.ident64_d = nc.inline_tensor(np.eye(64, dtype=BF), "ident64_const")
    ctx.band1_sb = nc.alloc_sbuf_tensor("band1_sb", [128, 128], BF16)
    ctx.band2_sb = nc.alloc_sbuf_tensor("band2_sb", [128, 128], BF16)
    return cz_d, b1_d, b2_d


def build_l1a_program():
    ctx = Ctx()
    nc = ctx.nc
    ctx.din("x_in", [CC, 128, T], BF16)
    declare_l1a_inputs(ctx)
    ctx.dout("x_out", [CC, 128, T], BF16)

    b = alloc_shared(ctx)
    cz_d, b1_d, b2_d = alloc_l1a(ctx)
    ctx.ident_sb = nc.alloc_sbuf_tensor("ident_sb", [128, 128], F32).ap()
    ident_dram = nc.inline_tensor(np.eye(128, dtype=np.float32), "ident_const")
    s_init = ctx.sem("init")

    with nc.Block() as blk:
        @blk.sync
        def _(sy):
            sy.dma_start(out=ctx.band1_sb[:], in_=b1_d.ap()).then_inc(s_init.h, 16)
            sy.dma_start(out=ctx.band2_sb[:], in_=b2_d.ap()).then_inc(s_init.h, 16)
            sy.dma_start(out=ctx.ident_sb, in_=ident_dram.ap()).then_inc(s_init.h, 16)
            sy.dma_start(out=ctx.ident64_sb[:], in_=ctx.ident64_d.ap()).then_inc(s_init.h, 16)
        s_init.n += 64

        @blk.vector
        def _(v):
            ins = v.memset(b.ones_bf[:], 1.0)
            s_init.inc(ins, 1)
            ins = v.memset(b.zb[:], 0.0)
            s_init.inc(ins, 1)

        for eng in [blk.tensor, blk.gpsimd, blk.vector, blk.scalar, blk.sync]:
            def emit(e):
                s_init.wait(e)
            eng(emit)

        emit_l1_attn(ctx, blk, b, "x_in", "x_out")

        @blk.sync
        def _(sy):
            for sem in ctx.dma_sems:
                sy.wait_ge(sem.h, sem.n)
    return ctx



def emit_l1_ffn(ctx, blk, b, x_in_name, x_out_name):
    x_in = ctx.dram[x_in_name]
    x_out = ctx.dram[x_out_name]
    wgu_scr = ctx.dscratch("wgu_scr", [16, 128, 11264], BF16)
    wdn_scr = ctx.dscratch("wdn_scr", [44, 128, 2048], BF16)
    sc2_gu = ctx.dscratch("sc2_gu", [16, 11264], F32)
    sc2_dn = ctx.dscratch("sc2_dn", [44, 2048], F32)
    sp_row4 = ctx.dscratch("sp_row_l1f", [6, T], F32)

    s_mm = ctx.sem("s_mm")
    s_v = ctx.sem("s_v")
    s_s = ctx.sem("s_s")
    s_p = ctx.sem("s_p")
    s_evV = ctx.sem("s_evV")
    s_evS = ctx.sem("s_evS")
    s_srg = [ctx.sem(f"s_srg{j}") for j in range(4)]
    s_srs = [ctx.sem(f"s_srs{j}") for j in range(4)]
    s_wr = [ctx.sem(f"s_wr{j}") for j in range(3)]
    s_t1 = [ctx.sem(f"s_t1{j}") for j in range(3)]
    s_bc = ctx.sem("s_bc")
    s_rsp = ctx.sem("s_rsp")
    s_cv = ctx.sem("s_cv")
    s_cp = ctx.sem("s_cp")
    s_qb = [ctx.sem("s_qb0"), ctx.sem("s_qb1")]
    s_qg = ctx.sem("s_qg")
    if not hasattr(ctx, "dma_sems"):
        ctx.dma_sems = [s_bc, s_rsp] + s_srg + s_srs + s_wr + s_t1 + s_qb
    if s_qg not in ctx.dma_sems:
        ctx.dma_sems.append(s_qg)

    if not hasattr(ctx, "rowsF"):
        ctx.rowsF = ctx.nc.alloc_sbuf_tensor("rowsF", [128, T], F32)
    scb_even = b.aB
    scb_odd = ctx.rowsF

    slotBflat = b.slotB.ap().rearrange("p a c -> p (a c)")

    def act_sl(i):
        return slotBflat[:, i * TW:(i + 1) * TW]

    def n3_sl(cc):
        return b.slotA.ap()[:, cc, 1:1 + TW]

    def y_sl(cc):
        return b.slotA.ap()[:, cc, 520:520 + TW]

    def x3_sl(cc):
        return b.slotA.ap()[:, cc, 1040:1040 + TW]

    tmp1f32 = b.tmp1.ap().rearrange("p a c -> p (a c)").bitcast(F32)
    tmp2f32 = b.tmp2.ap().rearrange("p a c -> p (a c)").bitcast(F32)

    sgu = ctx.dram["l1f_sc_gu"]
    sdn = ctx.dram["l1f_sc_dn"]
    v0 = s_v.n
    s0_ = s_s.n
    mm0_ = s_mm.n
    sp_jobs = [(sgu, sc2_gu, 16, j) for j in range(6)] + [(sdn, sc2_dn, 44, 0)]
    for ji, (src_d, dst_d, nrows, j) in enumerate(sp_jobs):
        w = min(2048, src_d.shape[1] - j * 2048)

        @blk.sync
        def _(sy, src_d=src_d, j=j, w=w, nrows=nrows, ji=ji, vv=s_v.n, t1e=[s.n for s in s_t1]):
            if ji == 0:
                sy.wait_ge(s_v.h, v0)
                sy.wait_ge(s_s.h, s0_)
                sy.wait_ge(s_mm.h, mm0_)
                for sem, val in zip(s_t1, t1e):
                    sy.wait_ge(sem.h, val)
            else:
                sy.wait_ge(s_v.h, vv)
            ins = sy.dma_start(out=b.sring.ap()[:, 0].bitcast(FP16)[0:nrows, 0:w],
                               in_=src_d[:, j * 2048:j * 2048 + w])
            s_srs[0].inc(ins, 16)
        ldv = s_srs[0].n

        @blk.vector
        def _(v, nrows=nrows, w=w, ldv=ldv, t1n=s_t1[0].n):
            v.wait_ge(s_srs[0].h, ldv)
            v.wait_ge(s_t1[0].h, t1n)
            ins = v.tensor_scalar(tmp1f32[0:nrows, 0:w],
                                  b.sring.ap()[:, 0].bitcast(FP16)[0:nrows, 0:w], 16384.0, None, ALU.mult)
            s_v.inc(ins, 1)

        @blk.scalar
        def _(s, dst_d=dst_d, j=j, w=w, nrows=nrows, vv=s_v.n):
            s.wait_ge(s_v.h, vv)
            ins = s.dma_start(out=dst_d[:, j * 2048:j * 2048 + w], in_=tmp1f32[0:nrows, 0:w])
            s_t1[0].inc(ins, 16)
    scales_done = s_t1[0].n

    s_dqp = [ctx.sem("s_dqp0"), ctx.sem("s_dqp1")]
    s_dqs = [ctx.sem("s_dqs0"), ctx.sem("s_dqs1")]
    for x in s_dqp + s_dqs:
        if x not in ctx.dma_sems:
            ctx.dma_sems.append(x)

    deq_tiles = ([("gu", cc, og) for og in range(6) for cc in range(16)] +
                 [("dn", cc, 0) for cc in range(44)])
    N_DEQ = len(deq_tiles)
    deq = {"idx": 0, "v4": [], "cv_v3": []}

    def emit_deq_tile(gates=()):
        idx = deq["idx"]
        deq["idx"] += 1
        kind, cc, og = deq_tiles[idx]
        src_d = ctx.dram["l1f_pk_gu"] if kind == "gu" else ctx.dram["l1f_pk_dn"]
        scr = wgu_scr if kind == "gu" else wdn_scr
        scd = sc2_gu if kind == "gu" else sc2_dn
        ncols = 11264 if kind == "gu" else 2048
        w = min(2048, ncols - og * 2048)
        slot = idx % 2
        sgn_slot = 2 + idx % 2
        gates = tuple(gates)

        @blk.sync
        def _(sy, idx=idx, cc=cc, og=og, w=w, slot=slot, src_d=src_d, gates=gates):
            for sem, val in gates:
                sy.wait_ge(sem.h, val)
            if idx >= 2:
                sy.wait_ge(s_v.h, deq["v4"][idx - 2])
            ins = sy.dma_start(out=b.sring.ap()[:, slot].bitcast(U16)[:, 0:w],
                               in_=src_d[cc * 128:(cc + 1) * 128, og * 2048:og * 2048 + w])
            s_dqs[slot].inc(ins, 16)
        ldv = s_dqs[slot].n

        @blk.gpsimd
        def _(g, idx=idx, cc=cc, og=og, w=w, scd=scd, gates=gates):
            for sem, val in gates:
                g.wait_ge(sem.h, val)
            if idx >= 2:
                g.wait_ge(s_v.h, deq["v4"][idx - 2])
            scb = scb_even if idx % 2 == 0 else scb_odd
            ins = g.dma_start(out=scb.ap()[:, 0:w],
                              in_=scd[cc, og * 2048:og * 2048 + w].unsqueeze(0).broadcast_to([128, w]))
            s_qg.inc(ins, 16)
        qgv = s_qg.n

        @blk.vector
        def _(v, idx=idx, w=w, slot=slot, sgn_slot=sgn_slot, qgv=qgv, ldv=ldv):
            v.wait_ge(s_dqs[slot].h, ldv)
            v.wait_ge(s_qg.h, qgv)
            if idx >= 2:
                v.wait_ge(s_dqp[idx % 2].h, 16 * (idx // 2))
                v.wait_ge(s_cv.h, deq["cv_v3"][idx - 2])
            w16 = b.sring.ap()[:, slot].bitcast(U16)
            sg = b.sring.ap()[:, sgn_slot].bitcast(U16)
            ins = v.tensor_scalar(sg[:, 0:w], w16[:, 0:w], 8, 12,
                                  ALU.bitwise_and, ALU.logical_shift_left)
            s_cv.inc(ins, 1)
            s_cv.wait(v)
            ins = v.tensor_scalar(w16[:, 0:w], w16[:, 0:w], 7, 9,
                                  ALU.bitwise_and, ALU.logical_shift_left)
            s_cv.inc(ins, 1)
            s_cv.wait(v)
            ins = v.tensor_tensor(w16[:, 0:w], w16[:, 0:w], sg[:, 0:w], ALU.bitwise_or)
            s_cv.inc(ins, 1)
            deq["cv_v3"].append(s_cv.n)
            s_cv.wait(v)
            ostage = (b.tmp1.ap()[:, 2, 0:w] if idx % 2 == 0
                      else b.bB.ap().bitcast(BF16)[:, 0:w])
            scb = scb_even if idx % 2 == 0 else scb_odd
            ins = v.tensor_tensor(ostage, w16.bitcast(FP16)[:, 0:w],
                                  scb.ap()[:, 0:w], ALU.mult)
            s_v.inc(ins, 1)
        deq["v4"].append(s_v.n)

        @blk.scalar
        def _(s, idx=idx, cc=cc, og=og, w=w, scr=scr, v4v=deq["v4"][idx]):
            s.wait_ge(s_v.h, v4v)
            ostage = (b.tmp1.ap()[:, 2, 0:w] if idx % 2 == 0
                      else b.bB.ap().bitcast(BF16)[:, 0:w])
            ins = s.dma_start(out=scr[cc][:, og * 2048:og * 2048 + w], in_=ostage)
            s_dqp[idx % 2].inc(ins, 16)

    og0_gates = [(s_v, s_v.n), (s_s, s_s.n), (s_mm, s_mm.n), (s_t1[0], scales_done),
                 (s_rsp, s_rsp.n)]
    for _i in range(16):
        emit_deq_tile(gates=og0_gates if _i < 4 else ())

    mmS = s_mm.n
    sqS = s_s.n
    ld_vals = []
    v_need = s_v.n

    t1_x3 = [s.n for s in s_t1]

    @blk.gpsimd
    def _(g):
        g.wait_ge(s_v.h, v_need)
        for sem, val in zip(s_t1, t1_x3):
            g.wait_ge(sem.h, val)
        for ci in range(CC):
            if ci >= 4:
                g.wait_ge(s_mm.h, mmS + ci - 3)
            ins = g.dma_start(out=b.sring.ap()[:, ci % 4], in_=x_in[ci])
            s_srg[ci % 4].inc(ins, 16)
            ld_vals.append(s_srg[ci % 4].n)

    @blk.scalar
    def _(s):
        for ci in range(CC):
            s.wait_ge(s_srg[ci % 4].h, ld_vals[ci])
            if ci >= 2:
                s.wait_ge(s_mm.h, mmS + ci - 1)
            zact(s, b, b.tmp2.ap()[:, ci % 2], b.sring.ap()[:, ci % 4], AF.Square).then_inc(s_s.h, 1)
            s_s.n += 1

    evV0, evS0 = s_evV.n, s_evS.n

    @blk.tensor
    def _(t):
        t.wait_ge(s_evV.h, evV0)
        t.wait_ge(s_evS.h, evS0)
        for ci in range(CC):
            t.wait_ge(s_s.h, sqS + ci + 1)
            for tt in range(NT):
                t.matmul(b.psA.ap()[0:1, tt], b.ones_bf[:],
                         b.sring.ap()[:, ci % 4, tt * TW:(tt + 1) * TW],
                         start=(ci == 0), stop=(ci == CC - 1))
                ins = t.matmul(b.psB.ap()[0:1, tt], b.ones_bf[:],
                               b.tmp2.ap()[:, ci % 2, tt * TW:(tt + 1) * TW],
                               start=(ci == 0), stop=(ci == CC - 1))
            ins.then_inc(s_mm.h, 1)
    s_mm.n += CC

    r0 = b.aB.ap()[0:1, :]
    r1 = b.bB.ap()[0:1, :]

    @blk.vector
    def _(v, dq1=s_dqp[1].n):
        s_mm.wait(v)
        s_qb[0].wait(v)
        v.wait_ge(s_dqp[1].h, dq1)
        for tt in range(NT):
            ins = v.tensor_scalar(r1[:, tt * TW:(tt + 1) * TW], b.psA.ap()[0:1, tt], 1.0 / D, None, ALU.mult)
        s_v.inc(ins, 1)

    @blk.scalar
    def _(s):
        s_v.wait(s)
        ins = zact(s, b, r0, r1, AF.Square)
        s_s.inc(ins, 1)

    @blk.vector
    def _(v):
        s_s.wait(v)
        for tt in range(NT):
            ins = v.tensor_scalar(r1[:, tt * TW:(tt + 1) * TW], b.psB.ap()[0:1, tt], 1.0 / D, None, ALU.mult)
        s_cv.inc(ins, 1)
        s_cv.wait(v)
        ins = v.tensor_tensor(r1, r1, r0, ALU.subtract)
        s_cv.inc(ins, 1)
        s_cv.wait(v)
        ins = v.tensor_scalar(r1, r1, 1.0, LN_EPS, ALU.mult, ALU.add)
        s_cv.inc(ins, 1)
        s_cv.wait(v)
        ins = v.reciprocal(r0, r1)
        s_v.inc(ins, 1)

    @blk.scalar
    def _(s):
        s_v.wait(s)
        ins = zact(s, b, r0, r0, AF.Sqrt)
        s_s.inc(ins, 1)

    @blk.vector
    def _(v):
        s_s.wait(v)
        for tt in range(NT):
            ins = v.tensor_scalar(r1[:, tt * TW:(tt + 1) * TW], b.psA.ap()[0:1, tt], -1.0 / D, None, ALU.mult)
        s_cv.inc(ins, 1)
        s_cv.wait(v)
        ins = v.tensor_tensor(r1, r1, r0, ALU.mult)
        s_evV.inc(ins, 1)
    rows_ok = s_evV.n

    @blk.scalar
    def _(s):
        s.wait_ge(s_evV.h, rows_ok)
        s.dma_start(out=sp_row4[0:1, :], in_=r0).then_inc(s_rsp.h, 16)
        s.dma_start(out=sp_row4[1:2, :], in_=r1).then_inc(s_rsp.h, 16)
    s_rsp.n += 32
    rows_sp = s_rsp.n

    NOG = 22
    for tq in range(NT):
        tq0 = tq * TW
        ldn_vals = []
        vq0 = s_v.n
        mmq0 = s_mm.n

        @blk.sync
        def _(sy, tq=tq, vq0=vq0, mmq0=mmq0):
            sy.wait_ge(s_rsp.h, rows_sp)
            sy.wait_ge(s_v.h, vq0)
            sy.wait_ge(s_mm.h, mmq0)
            ins = sy.dma_start(out=tmp2f32[:, 1024:1536],
                               in_=sp_row4[0, tq0:tq0 + TW].unsqueeze(0).broadcast_to([128, TW]))
            s_qb[0].inc(ins, 16)
            ins = sy.dma_start(out=tmp2f32[:, 1536:2048],
                               in_=sp_row4[1, tq0:tq0 + TW].unsqueeze(0).broadcast_to([128, TW]))
            s_qb[0].inc(ins, 16)
            for ci in range(CC):
                if ci >= 4:
                    sy.wait_ge(s_v.h, vq0 + ci - 3)
                ins = sy.dma_start(out=x3_sl(ci), in_=x_in[ci][:, tq0:tq0 + TW])
                s_srs[ci % 4].inc(ins, 16)
                ldn_vals.append(s_srs[ci % 4].n)
        qbv = s_qb[0].n

        t1_now = [s.n for s in s_t1]

        @blk.vector
        def _(v, tq=tq, ldn_vals=ldn_vals, qbv=qbv, mmq0=mmq0, t1_now=t1_now):
            v.wait_ge(s_qb[0].h, qbv)
            v.wait_ge(s_mm.h, mmq0)
            for sem, val in zip(s_t1, t1_now):
                v.wait_ge(sem.h, val)
            aBq = tmp2f32[:, 1024:1536]
            bBq = tmp2f32[:, 1536:2048]
            for ci in range(CC):
                v.wait_ge(s_srs[ci % 4].h, ldn_vals[ci])
                v.wait_ge(s_v.h, s_v.n)
                ins = v.tensor_tensor(b.tmp1.ap()[:, 0, 0:TW], x3_sl(ci), aBq, ALU.mult)
                s_cv.inc(ins, 1)
                s_cv.wait(v)
                ins = v.tensor_tensor(n3_sl(ci), b.tmp1.ap()[:, 0, 0:TW], bBq, ALU.add)
                s_v.inc(ins, 1)
        n3_done = s_v.n

        for og in range(NOG):
            lw_vals = []
            mmg0 = s_mm.n
            evp = (s_evS.n, s_evV.n)

            dq_need = 16 * 8 * (og // 4 + 1) if tq == 0 else 16 * (N_DEQ // 2)

            lw_vals.extend([None] * CC)
            for qi, qeng in enumerate((blk.gpsimd, blk.sync, blk.scalar)):
                def emit(q, qi=qi, og=og, mmg0=mmg0, dq_need=dq_need, lw_vals=lw_vals):
                    q.wait_ge(s_dqp[0].h, dq_need)
                    q.wait_ge(s_dqp[1].h, dq_need)
                    q.wait_ge(s_mm.h, mmg0)
                    for cc in range(qi, CC, 3):
                        if cc >= 3:
                            q.wait_ge(s_mm.h, mmg0 + cc - 2)
                        ins = q.dma_start(out=b.wring.ap()[:, cc % 3, 0:TW],
                                          in_=wgu_scr[cc][:, og * TW:(og + 1) * TW])
                        s_wr[cc % 3].inc(ins, 16)
                        lw_vals[cc] = s_wr[cc % 3].n
                qeng(emit)

            @blk.tensor
            def _(t, og=og, lw_vals=lw_vals, n3_done=n3_done, evp=evp):
                t.wait_ge(s_v.h, n3_done)
                t.wait_ge(s_evS.h, evp[0])
                t.wait_ge(s_evV.h, evp[1])
                for cc in range(CC):
                    t.wait_ge(s_wr[cc % 3].h, lw_vals[cc])
                    wv = b.wring.ap()[:, cc % 3, 0:TW].rearrange("p (j o) -> p j o", o=128)
                    for j in range(4):
                        ins = t.matmul((b.psA if og % 2 == 0 else b.psB).ap()[:, j],
                                       wv[:, j], n3_sl(cc), start=(cc == 0), stop=(cc == CC - 1))
                    ins.then_inc(s_mm.h, 1)
            s_mm.n += CC
            mm_done = s_mm.n

            ps = b.psA if og % 2 == 0 else b.psB
            if og < 11:
                @blk.scalar
                def _(s, og=og, mm_done=mm_done):
                    s.wait_ge(s_mm.h, mm_done)
                    for j in range(4):
                        ins = zact(s, b, act_sl(og * 4 + j), ps.ap()[:, j], AF.Silu)
                        s_evS.inc(ins, 1)
            else:
                @blk.vector
                def _(v, og=og, mm_done=mm_done):
                    v.wait_ge(s_mm.h, mm_done)
                    for j in range(4):
                        i = (og - 11) * 4 + j
                        v.wait_ge(s_v.h, s_v.n)
                        ins = v.tensor_tensor(act_sl(i), act_sl(i), ps.ap()[:, j], ALU.mult)
                        s_evV.inc(ins, 1)

            if tq == 0:
                target = min(N_DEQ, 16 + 6 * (og + 1))
                post_stats_gates = [(s_s, s_s.n), (s_mm, s_mm.n), (s_rsp, rows_sp)]
                while deq["idx"] < target:
                    emit_deq_tile(gates=post_stats_gates if deq["idx"] <= 17 else ())
        act_done = (s_evS.n, s_evV.n)

        for og in range(4):
            lw_vals = []
            mmg0 = s_mm.n
            evp = (s_evS.n, s_evV.n)

            lw_vals.extend([None] * 44)
            for qi, qeng in enumerate((blk.gpsimd, blk.sync, blk.scalar)):
                def emit(q, qi=qi, og=og, mmg0=mmg0, lw_vals=lw_vals):
                    q.wait_ge(s_dqp[0].h, 16 * (N_DEQ // 2))
                    q.wait_ge(s_dqp[1].h, 16 * (N_DEQ // 2))
                    q.wait_ge(s_mm.h, mmg0)
                    for cc in range(qi, 44, 3):
                        if cc >= 3:
                            q.wait_ge(s_mm.h, mmg0 + cc - 2)
                        ins = q.dma_start(out=b.wring.ap()[:, cc % 3, 0:TW],
                                          in_=wdn_scr[cc][:, og * TW:(og + 1) * TW])
                        s_wr[cc % 3].inc(ins, 16)
                        lw_vals[cc] = s_wr[cc % 3].n
                qeng(emit)

            @blk.tensor
            def _(t, og=og, lw_vals=lw_vals, evp=evp, act_done=act_done):
                t.wait_ge(s_evS.h, act_done[0])
                t.wait_ge(s_evV.h, act_done[1])
                t.wait_ge(s_evS.h, evp[0])
                t.wait_ge(s_evV.h, evp[1])
                for cc in range(44):
                    t.wait_ge(s_wr[cc % 3].h, lw_vals[cc])
                    wv = b.wring.ap()[:, cc % 3, 0:TW].rearrange("p (j o) -> p j o", o=128)
                    for j in range(4):
                        ins = t.matmul((b.psA if og % 2 == 0 else b.psB).ap()[:, j],
                                       wv[:, j], act_sl(cc), start=(cc == 0), stop=(cc == 43))
                    ins.then_inc(s_mm.h, 1)
            s_mm.n += 44
            mm_done = s_mm.n

            ps = b.psA if og % 2 == 0 else b.psB

            @blk.vector
            def _(v, og=og, mm_done=mm_done):
                v.wait_ge(s_mm.h, mm_done)
                for j in range(4):
                    v.wait_ge(s_v.h, s_v.n)
                    ins = v.tensor_copy(y_sl(og * 4 + j), ps.ap()[:, j])
                    s_evV.inc(ins, 1)
        y_done = s_evV.n

        sqq0 = s_s.n
        evq = s_evV.n
        mmr_base = s_mm.n

        @blk.scalar
        def _(s, y_done=y_done, mmr_base=mmr_base, sqq0=sqq0):
            s.wait_ge(s_evV.h, y_done)
            for cc in range(CC):
                if cc >= 4:
                    s.wait_ge(s_mm.h, mmr_base + cc - 3)
                zact(s, b, b.tmp2.ap()[:, 0, (cc % 4) * TW:(cc % 4 + 1) * TW], y_sl(cc), AF.Square).then_inc(s_s.h, 1)
                s_s.n += 1

        @blk.tensor
        def _(t, sqq0=sqq0, evq=evq):
            t.wait_ge(s_evV.h, evq)
            for cc in range(CC):
                t.wait_ge(s_s.h, sqq0 + cc + 1)
                ins = t.matmul(b.psA.ap()[0:1, 0], b.ones_bf[:],
                               b.tmp2.ap()[:, 0, (cc % 4) * TW:(cc % 4 + 1) * TW],
                               start=(cc == 0), stop=(cc == CC - 1))
                ins.then_inc(s_mm.h, 1)
        s_mm.n += CC
        mm_rms = s_mm.n

        rrow = tmp2f32[0:1, 1024:1536]

        @blk.vector
        def _(v, mm_rms=mm_rms):
            v.wait_ge(s_mm.h, mm_rms)
            ins = v.tensor_scalar(rrow, b.psA.ap()[0:1, 0], 1.0 / D, 1e-6, ALU.mult, ALU.add)
            s_cv.inc(ins, 1)
            s_cv.wait(v)
            ins = v.reciprocal(rrow, rrow)
            s_evV.inc(ins, 1)

        @blk.scalar
        def _(s, tq=tq):
            s.wait_ge(s_evV.h, s_evV.n)
            ins = zact(s, b, rrow, rrow, AF.Sqrt)
            s_s.inc(ins, 1)
            s.wait_ge(s_s.h, s_s.n)
            s.dma_start(out=sp_row4[2:3, tq0:tq0 + TW], in_=rrow).then_inc(s_rsp.h, 16)
        s_rsp.n += 16
        rr_sp = s_rsp.n

        @blk.sync
        def _(sy, rr_sp=rr_sp):
            sy.wait_ge(s_rsp.h, rr_sp)
            ins = sy.dma_start(out=tmp2f32[:, 1536:2048],
                               in_=sp_row4[2, tq0:tq0 + TW].unsqueeze(0).broadcast_to([128, TW]))
            s_qb[1].inc(ins, 16)
        rb_val = s_qb[1].n

        for cc in range(CC):
            t1v = s_t1[cc % 3].n

            @blk.vector
            def _(v, cc=cc, tq=tq, t1v=t1v, rb_val=rb_val):
                v.wait_ge(s_qb[1].h, rb_val)
                v.wait_ge(s_t1[cc % 3].h, t1v)
                v.wait_ge(s_v.h, s_v.n)
                rstd_b = tmp2f32[:, 1536:2048]
                ins = v.tensor_tensor(tmp1f32[:, (cc % 3) * TW:(cc % 3 + 1) * TW], y_sl(cc), rstd_b, ALU.mult)
                s_cv.inc(ins, 1)
                s_cv.wait(v)
                ins = v.tensor_tensor(tmp1f32[:, (cc % 3) * TW:(cc % 3 + 1) * TW],
                                      tmp1f32[:, (cc % 3) * TW:(cc % 3 + 1) * TW],
                                      x3_sl(cc), ALU.add)
                s_v.inc(ins, 1)

            @blk.scalar
            def _(s, cc=cc, tq=tq, vv=None):
                s.wait_ge(s_v.h, s_v.n)
                ins = s.dma_start(out=x_out[cc][:, tq0:tq0 + TW],
                                  in_=tmp1f32[:, (cc % 3) * TW:(cc % 3 + 1) * TW])
                s_t1[cc % 3].inc(ins, 16)


def _unpack_nib_T(packed):
    a = np.asarray(packed, np.int32).view(np.uint32)
    out_f, in_p = a.shape
    nib = (a[:, :, None] >> (np.arange(8, dtype=np.uint32) * 4)) & np.uint32(15)
    return np.ascontiguousarray(nib.astype(np.uint16).reshape(out_f, in_p * 8).T)


def host_pack_l1f(inp):
    pk_gu = _unpack_nib_T(inp["L1_ffn_gate_up_packed"])
    pk_dn = _unpack_nib_T(inp["L1_ffn_down_packed"])
    return {
        "l1f_pk_gu": pk_gu,
        "l1f_pk_dn": pk_dn,
        "l1f_sc_gu": np.ascontiguousarray(np.asarray(inp["L1_ffn_gate_up_scales"])),
        "l1f_sc_dn": np.ascontiguousarray(np.asarray(inp["L1_ffn_down_scales"])),
    }


def declare_l1f_inputs(ctx):
    ctx.din("l1f_pk_gu", [2048, 11264], U16)
    ctx.din("l1f_pk_dn", [5632, 2048], U16)
    ctx.din("l1f_sc_gu", [16, 11264], FP16)
    ctx.din("l1f_sc_dn", [44, 2048], FP16)


def build_l1f_program():
    ctx = Ctx()
    nc = ctx.nc
    ctx.din("x_in", [CC, 128, T], BF16)
    declare_l1f_inputs(ctx)
    ctx.dout("x_out", [CC, 128, T], F32)

    b = alloc_shared(ctx)
    ctx.shp_sb = nc.alloc_sbuf_tensor("shp_sb", [128, 1], F32)
    shp_d = nc.inline_tensor((np.arange(128) % 4 * 4.0).astype(np.float32).reshape(128, 1), "shp_const")
    s_init = ctx.sem("init")

    with nc.Block() as blk:
        @blk.sync
        def _(sy):
            sy.dma_start(out=ctx.shp_sb[:], in_=shp_d.ap()).then_inc(s_init.h, 16)
        s_init.n += 16

        @blk.vector
        def _(v):
            ins = v.memset(b.ones_bf[:], 1.0)
            s_init.inc(ins, 1)
            ins = v.memset(b.zb[:], 0.0)
            s_init.inc(ins, 1)

        for eng in [blk.tensor, blk.gpsimd, blk.vector, blk.scalar, blk.sync]:
            def emit(e):
                s_init.wait(e)
            eng(emit)

        emit_l1_ffn(ctx, blk, b, "x_in", "x_out")

        @blk.sync
        def _(sy):
            for sem in ctx.dma_sems:
                sy.wait_ge(sem.h, sem.n)
    return ctx



def build_l0a_program():
    ctx = Ctx()
    nc = ctx.nc
    ctx.din("x_in", [CC, 128, T], F32)
    for nm in ["wr", "wk", "wv", "wo", "cwr", "cwk", "cwv"]:
        ctx.din(f"l0a_{nm}", [CC, 128, D], F32)
    ctx.din("l0a_vecs", [128, 10, CC], F32)
    ctx.dout("x_out", [CC, 128, T], BF16)

    b = alloc_shared(ctx)
    vecs_sb = nc.alloc_sbuf_tensor("vecs_l0a", [128, 10, CC], F32)
    s_init = ctx.sem("init")

    with nc.Block() as blk:
        @blk.sync
        def _(sy):
            sy.dma_start(out=vecs_sb[:], in_=ctx.dram["l0a_vecs"]).then_inc(s_init.h, 16)
        s_init.n += 16

        @blk.vector
        def _(v):
            ins = v.memset(b.ones_bf[:], 1.0)
            s_init.inc(ins, 1)
            ins = v.memset(b.zb[:], 0.0)
            s_init.inc(ins, 1)

        @blk.vector
        def _(v):
            s_init.wait(v)
            ins = v.tensor_scalar(vecs_sb.ap()[:, 4, :], vecs_sb.ap()[:, 4, :],
                                  18.420680743952367, None, ALU.add)
            s_init.inc(ins, 1)

        for eng in [blk.tensor, blk.gpsimd, blk.vector]:
            def emit(e):
                s_init.wait(e)
            eng(emit)

        emit_l0_attn(ctx, blk, b, vecs_sb, "x_in", "x_out")

        @blk.sync
        def _(sy):
            for sem in ctx.dma_sems:
                sy.wait_ge(sem.h, sem.n)
            for sem in (ctx.sem("s_sr0"), ):
                pass
    return ctx


def host_pack_l0a(inp):
    return {
        "l0a_wr": pack_wT(inp["L0_attn_tm_Wr"]),
        "l0a_wk": pack_wT(inp["L0_attn_tm_Wk"]),
        "l0a_wv": pack_wT(inp["L0_attn_tm_Wv"]),
        "l0a_wo": pack_wT(inp["L0_attn_tm_Wout"]),
        "l0a_cwr": pack_wT(inp["L0_attn_cm_Wr"]),
        "l0a_cwk": pack_wT(inp["L0_attn_cm_Wk"]),
        "l0a_cwv": pack_wT(inp["L0_attn_cm_Wv"]),
        "l0a_vecs": pack_vecs([
            inp["L0_attn_ln1_g"], inp["L0_attn_ln1_b"], inp["L0_attn_ln2_g"], inp["L0_attn_ln2_b"],
            inp["L0_attn_tm_u"],
            inp["L0_attn_tm_mu_r"], inp["L0_attn_tm_mu_k"], inp["L0_attn_tm_mu_v"],
            inp["L0_attn_cm_mu_r"], inp["L0_attn_cm_mu_k"]]),
    }


_PROGS = {}


def _get_programs():
    if "p" not in _PROGS:
        _PROGS["p"] = (build_l0a_program(), build_l0f_program(), build_l1a_program(), build_l1f_program())
    return _PROGS["p"]


def build_fused_program():
    ctx = Ctx()
    nc = ctx.nc
    ctx.din("x_in", [CC, 128, T], F32)
    for nm in ["wr", "wk", "wv", "wo", "cwr", "cwk", "cwv"]:
        ctx.din(f"l0a_{nm}", [CC, 128, D], F32)
    ctx.din("l0a_vecs", [128, 10, CC], F32)
    declare_l0f_inputs(ctx)
    declare_l1a_inputs(ctx)
    declare_l1f_inputs(ctx)
    ctx.dscratch("xres1", [CC, 128, T], BF16)
    ctx.dscratch("xres2", [CC, 128, T], BF16)
    ctx.dscratch("xres3", [CC, 128, T], BF16)
    ctx.dout("x_out", [CC, 128, T], F32)

    b = alloc_shared(ctx)
    vecs_sb = nc.alloc_sbuf_tensor("vecs_l0a", [128, 10, CC], F32)
    ctx.ident_sb = nc.alloc_sbuf_tensor("ident_sb", [128, 128], F32).ap()
    ident_dram = nc.inline_tensor(np.eye(128, dtype=np.float32), "ident_const")
    cz_d, b1_d, b2_d = alloc_l1a(ctx)
    ctx.shp_sb = nc.alloc_sbuf_tensor("shp_sb", [128, 1], F32)
    shp_d = nc.inline_tensor((np.arange(128) % 4 * 4.0).astype(np.float32).reshape(128, 1), "shp_const")
    s_init = ctx.sem("init")

    with nc.Block() as blk:
        @blk.sync
        def _(sy):
            sy.dma_start(out=vecs_sb[:], in_=ctx.dram["l0a_vecs"]).then_inc(s_init.h, 16)
            sy.dma_start(out=ctx.ident_sb, in_=ident_dram.ap()).then_inc(s_init.h, 16)
            sy.dma_start(out=ctx.ident64_sb[:], in_=ctx.ident64_d.ap()).then_inc(s_init.h, 16)
            sy.dma_start(out=ctx.band1_sb[:], in_=b1_d.ap()).then_inc(s_init.h, 16)
            sy.dma_start(out=ctx.band2_sb[:], in_=b2_d.ap()).then_inc(s_init.h, 16)
            sy.dma_start(out=ctx.shp_sb[:], in_=shp_d.ap()).then_inc(s_init.h, 16)
        s_init.n += 96

        @blk.vector
        def _(v):
            ins = v.memset(b.ones_bf[:], 1.0)
            s_init.inc(ins, 1)
            ins = v.memset(b.zb[:], 0.0)
            s_init.inc(ins, 1)

        @blk.vector
        def _(v):
            s_init.wait(v)
            ins = v.tensor_scalar(vecs_sb.ap()[:, 4, :], vecs_sb.ap()[:, 4, :],
                                  18.420680743952367, None, ALU.add)
            s_init.inc(ins, 1)

        for eng in [blk.tensor, blk.gpsimd, blk.vector, blk.scalar, blk.sync]:
            def emit(e):
                s_init.wait(e)
            eng(emit)

        emit_l0_attn(ctx, blk, b, vecs_sb, "x_in", "xres1")
        emit_l0_ffn(ctx, blk, b, "xres1", "xres2")
        emit_l1_attn(ctx, blk, b, "xres2", "xres3")
        emit_l1_ffn(ctx, blk, b, "xres3", "x_out")

        @blk.sync
        def _(sy):
            for sem in ctx.dma_sems:
                sy.wait_ge(sem.h, sem.n)
    return ctx


def _get_fused():
    if "f" not in _PROGS:
        _PROGS["f"] = build_fused_program()
    return _PROGS["f"]


def kernel(**inputs):
    from concourse.bass_utils import run_bass_kernel_spmd
    ncores = 8
    ctx = _get_fused()
    w = {}
    w.update(host_pack_l0a(inputs))
    w.update(host_pack_l0f(inputs))
    w.update(host_pack_l1a(inputs))
    w.update(host_pack_l1f(inputs))
    x = np.asarray(inputs["L0_attn_x"], np.float32)
    in_maps = [dict(w, x_in=pack_x_chunks(x[bb])) for bb in range(ncores)]
    res = run_bass_kernel_spmd(ctx.nc, in_maps, core_ids=list(range(ncores)))
    out = np.stack([unpack_x_chunks(np.asarray(res.results[bb]["x_out"]).astype(np.float32))
                    for bb in range(ncores)], axis=0)
    return out
